# version 07
# baseline (speedup 1.0000x reference)
import numpy as np
import ml_dtypes
import concourse.bass as bass
import concourse.mybir as mybir
from concourse.bass_utils import run_bass_kernel_spmd
from contextlib import ExitStack

F32 = mybir.dt.float32
BF16 = mybir.dt.bfloat16
I32 = mybir.dt.int32
AF = mybir.ActivationFunctionType
ALU = mybir.AluOpType
AX = mybir.AxisListType
bf16 = ml_dtypes.bfloat16

EMB = 2048
N = 4096
NH = 8
DFF = 5632
NT = 2048
QSCALE = float(1.0 / np.sqrt(np.float32(128.0)))
ESHIFT = -12.0
NSCAL = 180224.0
LN_EPS = 1e-5

_NF4F = np.array([-1.0, -0.6962, -0.5251, -0.3949, -0.2844, -0.1848, -0.0911,
                  0.0, 0.0796, 0.1609, 0.2461, 0.3379, 0.4407, 0.5626, 0.723,
                  1.0], dtype=np.float32)
_NF4 = _NF4F.astype(bf16)
_LF = _NF4.astype(np.float32)
_DL = np.diff(_LF)


def _nf4_taus():
    def classify(x):
        v = np.array(x, dtype=bf16)
        d = np.abs((np.full(16, v, dtype=bf16) - _NF4).astype(bf16))
        return int(np.argmin(d.astype(np.float32)))
    taus = []
    for j in range(15):
        lo, hi = float(_LF[j]), float(_LF[j + 1])
        for _ in range(80):
            m = (lo + hi) / 2
            if classify(m) <= j:
                lo = m
            else:
                hi = m
        taus.append(np.float32((lo + hi) / 2))
    return np.array(taus, dtype=np.float32)


_TAUS = _nf4_taus()

DEBUG_TAPS = False
MAX_PHASE = 4
ATTN_HEADS = NH
ATTN_STOP = 'full'
ATTN_GROUPS = 16

WCFG = [(704, 2048, 0), (704, 2048, 1), (256, 5632, 2)]


def _segs(qc):
    kr = 128 * (qc + 1)
    return [(kb, min(512, kr - kb)) for kb in range(0, kr, 512)]


class Sem:
    def __init__(self, es, nc, name):
        self.h = es.enter_context(nc.semaphore(name))
        self.n = 0

    def inc(self, instr, k=1):
        instr.then_inc(self.h, k)
        self.n += k
        return self.n


def attn_schedule(v0, a0, m0, dg0):
    wdc, pj, pje, smm, cmk, expm, norm = {}, {}, {}, {}, {}, {}, {}
    ptm, pte, avm, oev, odma, oad = {}, {}, {}, {}, {}, {}
    v, a, m, dg = v0, a0, m0, dg0
    exp_seq = []
    for h in range(ATTN_HEADS):
        for qc in range(32):
            for sg in range(len(_segs(qc))):
                exp_seq.append((h, qc, sg))
    exp_pos = {k: i for i, k in enumerate(exp_seq)}
    for h in range(ATTN_HEADS):
        for w in range(3):
            v += 1
            wdc[(h, w)] = v
        if ATTN_STOP == 'proj':
            continue
        for g in range(ATTN_GROUPS):
            if ATTN_STOP == 'full' and g >= 1 and 2 * (g - 1) + 2 > 16:
                v += 1
                oad[(h, g - 1, 0)] = v
                v += 1
                oad[(h, g - 1, 1)] = v
                v += 1
                oev[(h, g - 1)] = ('v', v)
            for qc in (2 * g, 2 * g + 1):
                v += 1
                cmk[(h, qc)] = v
                v += 1
                norm[(h, qc)] = v
        g = ATTN_GROUPS - 1
        if ATTN_STOP == 'full' and 2 * g + 2 > 16:
            v += 1
            oad[(h, g, 0)] = v
            v += 1
            oad[(h, g, 1)] = v
            v += 1
            oev[(h, g)] = ('v', v)
    for h in range(ATTN_HEADS):
        for w in (1, 0):
            for tb in range(8):
                m += 1
                pj[(h, w, tb)] = m
        for kb4 in range(8):
            m += 1
            pj[(h, 2, kb4)] = m
        if ATTN_STOP == 'proj':
            continue
        for g in range(ATTN_GROUPS):
            qc0, qc1 = 2 * g, 2 * g + 1
            for qc in (qc0, qc1):
                for sg in range(len(_segs(qc))):
                    m += 1
                    smm[(h, qc, sg)] = m
            if ATTN_STOP == 'smax':
                continue
            nki = qc1 + 1
            for sub0 in range(0, nki, 16):
                sks = list(range(sub0, min(sub0 + 16, nki)))
                for ki in sks:
                    m += 1
                    ptm[(h, g, ki)] = m
                for ki in sks:
                    m += 1
                    avm[(h, g, ki)] = m
    for h in range(ATTN_HEADS):
        for w in (1, 0):
            for tb in range(8):
                a += 1
                pje[(h, w, tb)] = a
        for kb4 in range(8):
            a += 1
            pje[(h, 2, kb4)] = a
        if ATTN_STOP == 'proj':
            continue
        for g in range(ATTN_GROUPS):
            qc0, qc1 = 2 * g, 2 * g + 1
            for qc in (qc0, qc1):
                for sg in range(len(_segs(qc))):
                    a += 1
                    expm[(h, qc, sg)] = a
            if ATTN_STOP == 'smax':
                continue
            nki = qc1 + 1
            for ki in range(nki):
                a += 1
                pte[(h, g, ki)] = a
            if nki <= 16:
                a += 1
                oev[(h, g)] = ('a', a)
    for h in range(ATTN_HEADS):
        dg += 16
        odma[h] = dg
    return dict(wdc=wdc, pj=pj, pje=pje, smm=smm, cmk=cmk, expm=expm,
                norm=norm, ptm=ptm, pte=pte, avm=avm, oev=oev, odma=odma,
                oad=oad, exp_seq=exp_seq, exp_pos=exp_pos,
                v_end=v, a_end=a, m_end=m, dg_end=dg)


def build_program():
    nc = bass.Bass("TRN2", target_bir_lowering=False, debug=False,
                   detect_race_conditions=False)

    xT = nc.dram_tensor("xT", [EMB, N], F32, kind="ExternalInput")
    x_res = nc.dram_tensor("x_res", [NT, EMB], F32, kind="ExternalInput")
    wqT = nc.dram_tensor("wqT", [EMB, 1024], F32, kind="ExternalInput")
    wkT = nc.dram_tensor("wkT", [EMB, 1024], F32, kind="ExternalInput")
    wvT = nc.dram_tensor("wvT", [EMB, 1024], F32, kind="ExternalInput")
    woT = nc.dram_tensor("woT", [EMB, EMB], F32, kind="ExternalInput")
    bqkv = nc.dram_tensor("bqkv", [1024, 3], F32, kind="ExternalInput")
    bvfull = nc.dram_tensor("bvfull", [16, 128], F32, kind="ExternalInput")
    bo_in = nc.dram_tensor("bo_in", [1, EMB], F32, kind="ExternalInput")
    w1s = nc.dram_tensor("w1s", [704, 2048], BF16, kind="ExternalInput")
    w2s = nc.dram_tensor("w2s", [704, 2048], BF16, kind="ExternalInput")
    w3s = nc.dram_tensor("w3s", [256, 5632], BF16, kind="ExternalInput")
    identw = nc.dram_tensor("identw", [128, 128], BF16, kind="ExternalInput")
    cmaskw = nc.dram_tensor("cmaskw", [128, 128], F32, kind="ExternalInput")
    offsw = nc.dram_tensor("offsw", [1, 8], I32, kind="ExternalInput")
    out = nc.dram_tensor("out", [NT, EMB], F32, kind="ExternalOutput")

    scl_flat = nc.dram_tensor("scl_flat", [3, 22528], F32)
    rep_flat = nc.dram_tensor("rep_flat", [3, 22528], F32)
    ar_in = nc.dram_tensor("ar_in", [1, 4], F32)
    ar_out = nc.dram_tensor("ar_out", [1, 4], F32)
    wsrcs = [w1s, w2s, w3s]
    ag_in = [nc.dram_tensor("ag1_in", [2048, 704], BF16),
             nc.dram_tensor("ag2_in", [2048, 704], BF16),
             nc.dram_tensor("ag3_in", [5632, 256], BF16)]
    ag_out = [nc.dram_tensor("ag1_out", [16384, 704], BF16),
              nc.dram_tensor("ag2_out", [16384, 704], BF16),
              nc.dram_tensor("ag3_out", [45056, 256], BF16)]
    ago_in = nc.dram_tensor("ago_in", [1024, N], BF16)
    ago_out = [nc.dram_tensor(f"ago_out{k}", [512, N], BF16) for k in range(4)]
    y1_scr = nc.dram_tensor("y1_scr", [NT, EMB], F32)
    dbg_deqw = nc.dram_tensor("dbg_deqw", [128, DFF], BF16,
                              kind="ExternalOutput") if DEBUG_TAPS else None
    dbg_acc = nc.dram_tensor("dbg_acc", [128, DFF], F32,
                             kind="ExternalOutput") if DEBUG_TAPS else None

    WORLD = [list(range(8))]
    PAIRS = [[0, 1], [2, 3], [4, 5], [6, 7]]

    ges = ExitStack()

    def sb(stk, name, shape, dt=BF16):
        return stk.enter_context(nc.sbuf_tensor(name, shape, dt))

    s_ident = sb(ges, "s_ident", [128, 128], BF16)
    s_cmask = sb(ges, "s_cmask", [128, 128], F32)
    s_bqkv = sb(ges, "s_bqkv", [128, 24], F32)
    s_bqs = sb(ges, "s_bqs", [128, 8], F32)
    s_offs = ges.enter_context(nc.sbuf_tensor("s_offs", [1, 8], I32))
    s_zb = sb(ges, "s_zb", [128, 16], F32)
    s_esh = sb(ges, "s_esh", [128, 1], F32)
    s_lneps = sb(ges, "s_lneps", [128, 1], F32)
    s_sd = sb(ges, "s_sd", [128, 1], F32)
    s_z = sb(ges, "s_z", [128, 1], F32)
    s_zr = sb(ges, "s_zr", [128, 1], F32)
    s_m1 = sb(ges, "s_m1", [128, 1], F32)
    s_v1 = sb(ges, "s_v1", [128, 1], F32)
    s_r1 = sb(ges, "s_r1", [128, 1], F32)

    p_s = [ges.enter_context(nc.psum_tensor(f"p_s{i}", [128, 512], F32))
           for i in range(4)]
    p_pt = ges.enter_context(nc.psum_tensor("p_pt", [128, 512], F32))
    p_o = ges.enter_context(nc.psum_tensor("p_o", [128, 512], F32))
    p_pj = [ges.enter_context(nc.psum_tensor(f"p_pj{i}", [128, 512], F32))
            for i in range(2)]

    mk = lambda name: Sem(ges, nc, name)
    d_x = mk("d_x")
    d_g = mk("d_g")
    d_c = mk("d_c")
    d_wr = [mk(f"d_wr{i}") for i in range(3)]
    d_xl = mk("d_xl")
    d_wh = mk("d_wh")
    d_wo = mk("d_wo")
    d_xr = mk("d_xr")
    d_sl = [mk(f"d_sl{i}") for i in range(2)]
    d_w3p = [mk(f"d_w3p{i}") for i in range(2)]
    d_res = mk("d_res")
    v_c = mk("v_c")
    a_c = mk("a_c")
    m_c = mk("m_c")
    c_ar = mk("c_ar")
    c_ag = [mk(f"c_ag{i}") for i in range(3)]
    c_ao = [mk(f"c_ao{i}") for i in range(4)]

    f_v = mk("f_v")

    def vf(v, instr):
        f_v.inc(instr, 1)
        v.wait_ge(f_v.h, f_v.n)
        return instr

    with nc.Block() as blk:
        @blk.sync
        def _(sync):
            sync.dma_start(out=s_ident[:, :], in_=identw[:, :]).then_inc(d_c.h, 16)
            sync.dma_start(out=s_cmask[:, :], in_=cmaskw[:, :]).then_inc(d_c.h, 16)
            sync.dma_start(
                out=bass.AP(s_bqkv, 0, [[24, 128], [3, 8], [1, 3]]),
                in_=bass.AP(bqkv, 0, [[3, 128], [384, 8], [1, 3]]))\
                .then_inc(d_c.h, 16)
            sync.dma_start(out=s_offs[:, :], in_=offsw[:, :]).then_inc(d_c.h, 16)
            d_c.n += 64

        @blk.vector
        def _(v):
            v.wait_ge(d_c.h, d_c.n)
            v.memset(s_esh[:, :], ESHIFT)
            v.memset(s_lneps[:, :], LN_EPS)
            v.tensor_scalar(s_bqs[:, :],
                            bass.AP(s_bqkv, 0, [[24, 128], [3, 8]]),
                            QSCALE, None, ALU.mult)

    des = ExitStack()
    s_wraw = sb(des, "s_wraw", [128, 12288], BF16)
    s_s1nat = sb(des, "s_s1nat", [128, 192], F32)
    s_repnat = sb(des, "s_repnat", [128, 192], F32)
    s_thr = sb(des, "s_thr", [128, 1320], F32)
    s_wf32 = sb(des, "s_wf32", [128, DFF], F32)
    s_tmp = sb(des, "s_tmp", [128, DFF], F32)
    s_acc = sb(des, "s_acc", [128, DFF], F32)
    s_deqw = sb(des, "s_deqw", [128, DFF], BF16)
    s_slab = sb(des, "s_slab", [128, 11264], BF16)
    s_s1sum = sb(des, "s_s1sum", [128, 1], F32)
    s_ones = sb(des, "s_ones", [128, 1], F32)
    s_means = sb(des, "s_means", [1, 4], F32)
    s_gmean = sb(des, "s_gmean", [1, 4], F32)
    s_k88 = sb(des, "s_k88", [1, 88], BF16)
    s_mean1 = sb(des, "s_mean1", [1, 1], F32)
    s_mb1 = sb(des, "s_mb1", [1, 1], BF16)
    s_meanf = sb(des, "s_meanf", [88, 1], F32)
    s_meanb = sb(des, "s_meanb", [88, 1], BF16)
    s_meanbf = sb(des, "s_meanbf", [88, 1], F32)
    s_s1g = sb(des, "s_s1g", [88, 256], F32)
    s_cb = sb(des, "s_cb", [88, 256], BF16)
    s_cf = sb(des, "s_cf", [88, 256], F32)
    s_gabs = sb(des, "s_gabs", [88, 1], F32)
    s_twob = sb(des, "s_twob", [88, 1], BF16)
    s_twof = sb(des, "s_twof", [88, 1], F32)
    s_rqf = sb(des, "s_rqf", [88, 1], F32)
    s_c256 = sb(des, "s_c256", [88, 1], F32)
    s_qfb = sb(des, "s_qfb", [88, 1], BF16)
    s_qff = sb(des, "s_qff", [88, 1], F32)
    s_prodb = sb(des, "s_prodb", [88, 256], BF16)
    s_qi = des.enter_context(nc.sbuf_tensor("s_qi", [88, 256], I32))
    s_qc = sb(des, "s_qc", [88, 256], F32)
    s_dqb = sb(des, "s_dqb", [88, 256], BF16)
    s_repb = sb(des, "s_repb", [88, 256], BF16)
    s_repf = sb(des, "s_repf", [88, 256], F32)

    with nc.Block() as blk:
        ld0 = d_x.n
        v0, a0, m0, g0 = v_c.n, a_c.n, m_c.n, d_g.n
        vsum = {wi: v0 + wi + 1 for _, _, wi in WCFG}

        @blk.sync
        def _(sync):
            for F, D, wi in WCFG:
                if wi > 0:
                    sync.wait_ge(v_c.h, vsum[wi - 1])
                for ft in range((F + 127) // 128):
                    rows = min(128, F - 128 * ft)
                    sync.dma_start(out=s_wraw[0:rows, D * ft:D * ft + D],
                                   in_=wsrcs[wi][128 * ft:128 * ft + rows, :])\
                        .then_inc(d_wr[wi].h, 16)
                    d_wr[wi].n += 16

        @blk.vector
        def _(v):
            v.memset(s_ones[:, :], 1.0)
            v.memset(s_k88[:, :], 1.0)
            v.memset(s_c256[:, :], 256.0)
            v.memset(s_means[:, :], 0.0)
            dmp = g0
            for F, D, wi in WCFG:
                nblk = D // 64
                natc = 32 if wi < 2 else 88
                nft = (F + 127) // 128
                if wi > 0:
                    v.wait_ge(d_g.h, dmp)
                dmp += 16 * nft
                v.memset(s_s1nat[:, :], 0.0)
                v.wait_ge(d_wr[wi].h, 16 * nft)
                for ft in range(nft):
                    rows = min(128, F - 128 * ft)
                    v.tensor_reduce(
                        s_s1nat[0:rows, natc * ft:natc * ft + nblk],
                        bass.AP(s_wraw, D * ft,
                                [[12288, rows], [64, nblk], [1, 64]]),
                        AX.X, ALU.max, apply_absolute_value=True)
                v.tensor_reduce(s_s1sum[:, :], s_s1nat[:, 0:natc * nft],
                                AX.X, ALU.add)
                v_c.inc(v.tensor_copy(s_tmp[0:128, 0:1], s_s1sum[:, 0:1]))
                assert v_c.n == vsum[wi]

        @blk.tensor
        def _(t):
            for F, D, wi in WCFG:
                t.wait_ge(v_c.h, vsum[wi])
                m_c.inc(t.matmul(p_pj[0][0:1, wi:wi + 1], s_s1sum[:, :],
                                 s_ones[:, :], start=True, stop=True))

        @blk.scalar
        def _(a):
            a.wait_ge(m_c.h, m_c.n)
            a_c.inc(a.copy(s_means[0:1, 0:3], p_pj[0][0:1, 0:3]))

        @blk.gpsimd
        def _(g):
            for F, D, wi in WCFG:
                g.wait_ge(v_c.h, vsum[wi])
                nblk = D // 64
                natc = 32 if wi < 2 else 88
                for ft in range((F + 127) // 128):
                    rows = min(128, F - 128 * ft)
                    g.dma_start(
                        out=bass.AP(scl_flat, 22528 * wi + 128 * nblk * ft,
                                    [[nblk, rows], [1, nblk]]),
                        in_=s_s1nat[0:rows, natc * ft:natc * ft + nblk])\
                        .then_inc(d_g.h, 16)
                    d_g.n += 16
            g.wait_ge(a_c.h, a_c.n)
            g.dma_start(out=ar_in[:, :], in_=s_means[:, :]).then_inc(d_g.h, 16)
            d_g.n += 16
            g.wait_ge(d_g.h, d_g.n)
            g.collective_compute(
                "AllReduce", ALU.add, replica_groups=WORLD,
                ins=[ar_in.ap().opt()], outs=[ar_out.ap().opt()])\
                .then_inc(c_ar.h, 1)
            g.wait_ge(c_ar.h, 1)
            g.dma_start(out=s_gmean[:, :], in_=ar_out[:, :]).then_inc(d_g.h, 16)
            d_g.n += 16

    for F, D, wi in WCFG:
        nblk = D // 64
        nft = (F + 127) // 128
        natc = 32 if wi < 2 else 88
        ndc = D // 128

        with nc.Block() as blk:
            v_mb = v_c.n + 1
            m_mb = m_c.n + 1
            a_mb = a_c.n + 1

            @blk.gpsimd
            def _(g, wi=wi):
                g.wait_ge(d_g.h, d_g.n)
                g.dma_start(
                    out=s_s1g[:, :],
                    in_=bass.AP(scl_flat, 22528 * wi, [[256, 88], [1, 256]]))\
                    .then_inc(d_g.h, 16)
                d_g.n += 16

            @blk.tensor
            def _(t, wi=wi, v_mb=v_mb):
                t.wait_ge(v_c.h, v_mb)
                m_c.inc(t.matmul(p_pj[1][0:88, 0:1], s_k88[:, :],
                                 s_mb1[:, :], start=True, stop=True))

            @blk.scalar
            def _(a, m_mb=m_mb):
                a.wait_ge(m_c.h, m_mb)
                a.copy(s_meanf[:, :], p_pj[1][0:88, 0:1])
                a_c.inc(a.copy(s_meanb[:, :], s_meanf[:, :]))

            @blk.vector
            def _(v, wi=wi, a_mb=a_mb):
                v.wait_ge(d_g.h, d_g.n)
                vf(v, v.tensor_scalar(s_mean1[:, :], s_gmean[0:1, wi:wi + 1],
                                      1.0 / NSCAL, None, ALU.mult))
                v_c.inc(v.tensor_copy(s_mb1[:, :], s_mean1[:, :]))
                v.wait_ge(a_c.h, a_mb)
                vf(v, v.tensor_copy(s_meanbf[:, :], s_meanb[:, :]))
                v.tensor_scalar(s_cb[:, :], s_s1g[:, :], s_meanbf[:, :], None,
                                ALU.subtract)
                v.tensor_copy(s_cf[:, :], s_cb[:, :])
                vf(v, v.tensor_reduce(s_gabs[:, :], s_cf[:, :], AX.X, ALU.max,
                                      apply_absolute_value=True))
                vf(v, v.tensor_scalar(s_twob[:, :], s_gabs[:, :], 2.0, None,
                                      ALU.mult))
                vf(v, v.tensor_copy(s_twof[:, :], s_twob[:, :]))
                vf(v, v.reciprocal(s_rqf[:, :], s_twof[:, :]))
                vf(v, v.tensor_scalar(s_qfb[:, :], s_rqf[:, :], 256.0, None,
                                      ALU.mult))
                vf(v, v.tensor_copy(s_qff[:, :], s_qfb[:, :]))
                vf(v, v.reciprocal(s_rqf[:, :], s_qff[:, :]))
                v.tensor_scalar(s_prodb[:, :], s_cf[:, :], s_qff[:, :], None,
                                ALU.mult)
                v.tensor_copy(s_qi[:, :], s_prodb[:, :])
                v.tensor_copy(s_qc[:, :], s_qi[:, :])
                v.tensor_scalar_min(s_qc[:, :], s_qc[:, :], 127.0)
                v.tensor_scalar_max(s_qc[:, :], s_qc[:, :], -128.0)
                v.tensor_scalar(s_dqb[:, :], s_qc[:, :], s_rqf[:, :], None,
                                ALU.mult)
                v.tensor_scalar(s_repb[:, :], s_dqb[:, :], s_meanbf[:, :],
                                None, ALU.add)
                v_c.inc(v.tensor_copy(s_repf[:, :], s_repb[:, :]))

            @blk.gpsimd
            def _(g, wi=wi, nblk=nblk, nft=nft, F=F, natc=natc):
                g.wait_ge(v_c.h, v_c.n)
                g.dma_start(
                    out=bass.AP(rep_flat, 22528 * wi, [[256, 88], [1, 256]]),
                    in_=s_repf[:, :]).then_inc(d_g.h, 16)
                d_g.n += 16
                g.wait_ge(d_g.h, d_g.n)
                for ft in range(nft):
                    rows = min(128, F - 128 * ft)
                    g.dma_start(
                        out=s_repnat[0:rows, natc * ft:natc * ft + nblk],
                        in_=bass.AP(rep_flat, 22528 * wi + 128 * nblk * ft,
                                    [[nblk, rows], [1, nblk]]))\
                        .then_inc(d_g.h, 16)
                    d_g.n += 16
                    g.dma_start(
                        out=s_s1nat[0:rows, natc * ft:natc * ft + nblk],
                        in_=bass.AP(scl_flat, 22528 * wi + 128 * nblk * ft,
                                    [[nblk, rows], [1, nblk]]))\
                        .then_inc(d_g.h, 16)
                    d_g.n += 16

        with nc.Block() as blk:
            v0, a0, m0 = v_c.n, a_c.n, m_c.n
            wr_tot = d_wr[wi].n + 16 * nft

            @blk.sync
            def _(sync, wi=wi, F=F, D=D, nft=nft):
                for ft in range(nft):
                    rows = min(128, F - 128 * ft)
                    sync.dma_start(out=s_wraw[0:rows, D * ft:D * ft + D],
                                   in_=wsrcs[wi][128 * ft:128 * ft + rows, :])\
                        .then_inc(d_wr[wi].h, 16)
                    d_wr[wi].n += 16

            grps_per_ft = (ndc + 3) // 4
            vm = {ft: v0 + ft + 1 for ft in range(nft)}
            tm = {i: m0 + i + 1 for i in range(nft * grps_per_ft)}
            am = {i: a0 + i + 1 for i in range(nft * grps_per_ft)}

            @blk.vector
            def _(v, wi=wi, nblk=nblk, nft=nft, F=F, D=D, natc=natc,
                  wr_tot=wr_tot):
                v.wait_ge(d_g.h, d_g.n)
                v.wait_ge(d_wr[wi].h, wr_tot)
                for ft in range(nft):
                    rows = min(128, F - 128 * ft)
                    for j in range(15):
                        ti = v.tensor_scalar(
                            s_thr[0:rows, nblk * j:nblk * j + nblk],
                            s_s1nat[0:rows, natc * ft:natc * ft + nblk],
                            float(_TAUS[j]), None, ALU.mult)
                    vf(v, ti)
                    v.tensor_copy(s_wf32[0:rows, 0:D],
                                  s_wraw[0:rows, D * ft:D * ft + D])
                    v.memset(s_acc[0:rows, 0:D], float(_LF[0]))
                    for j in range(15):
                        v.tensor_tensor(
                            s_tmp[0:rows, 0:D], s_wf32[0:rows, 0:D],
                            bass.AP(s_thr, nblk * j,
                                    [[1320, rows], [1, nblk], [0, 64]]),
                            ALU.is_gt)
                        v.scalar_tensor_tensor(
                            s_acc[0:rows, 0:D], s_tmp[0:rows, 0:D],
                            float(_DL[j]), s_acc[0:rows, 0:D],
                            ALU.mult, ALU.add)
                    v_c.inc(v.tensor_tensor(
                        s_deqw[0:rows, 0:D], s_acc[0:rows, 0:D],
                        bass.AP(s_repnat, natc * ft,
                                [[192, rows], [1, nblk], [0, 64]]),
                        ALU.mult))

            @blk.tensor
            def _(t, nft=nft, F=F, D=D, ndc=ndc, grps_per_ft=grps_per_ft):
                grp = 0
                for ft in range(nft):
                    rows = min(128, F - 128 * ft)
                    t.wait_ge(v_c.h, vm[ft])
                    for dc0 in range(0, ndc, 4):
                        if grp >= 2:
                            t.wait_ge(a_c.h, am[grp - 2])
                        bk = [p_pt, p_s[0]][grp % 2]
                        nd = min(4, ndc - dc0)
                        for i in range(nd):
                            mm = t.matmul(
                                bk[:, 128 * i:128 * i + rows],
                                s_deqw[0:rows,
                                       128 * (dc0 + i):128 * (dc0 + i) + 128],
                                s_ident[0:rows, 0:rows],
                                start=True, stop=True)
                            if i == nd - 1:
                                m_c.inc(mm)
                        assert m_c.n == tm[grp]
                        grp += 1

            @blk.scalar
            def _(a, wi=wi, nft=nft, F=F, D=D, ndc=ndc,
                  grps_per_ft=grps_per_ft):
                stride = 704 if wi < 2 else 256
                grp = 0
                for ft in range(nft):
                    rows = min(128, F - 128 * ft)
                    for dc0 in range(0, ndc, 4):
                        a.wait_ge(m_c.h, tm[grp])
                        bk = [p_pt, p_s[0]][grp % 2]
                        nd = min(4, ndc - dc0)
                        a_c.inc(a.copy(
                            bass.AP(s_slab, stride * dc0 + 128 * ft,
                                    [[11264, 128], [stride, nd], [1, rows]]),
                            bass.AP(bk, 0, [[512, 128], [128, nd],
                                            [1, rows]])))
                        assert a_c.n == am[grp]
                        grp += 1

            @blk.gpsimd
            def _(g, wi=wi, D=D, vm=vm):
                if DEBUG_TAPS and wi == 0:
                    g.wait_ge(v_c.h, vm[0])
                    g.dma_start(out=dbg_deqw[:, 0:2048],
                                in_=s_deqw[:, 0:2048]).then_inc(d_g.h, 16)
                    d_g.n += 16
                    g.dma_start(out=dbg_acc[:, 0:2048],
                                in_=s_acc[:, 0:2048]).then_inc(d_g.h, 16)
                    d_g.n += 16
                g.wait_ge(a_c.h, a_c.n)
                stride = 704 if wi < 2 else 256
                for dc in range(D // 128):
                    g.dma_start(
                        out=ag_in[wi][128 * dc:128 * dc + 128, 0:stride],
                        in_=s_slab[:, stride * dc:stride * dc + stride])\
                        .then_inc(d_g.h, 16)
                    d_g.n += 16

    with nc.Block() as blk:
        @blk.gpsimd
        def _(g):
            for i in range(3):
                g.collective_compute(
                    "AllGather", ALU.bypass, replica_groups=WORLD,
                    ins=[ag_in[i].ap().opt()], outs=[ag_out[i].ap().opt()])\
                    .then_inc(c_ag[i].h, 1)

    des.close()

    if MAX_PHASE < 2:
        with nc.Block() as blk:
            @blk.gpsimd
            def _(g_):
                for i in range(3):
                    g_.wait_ge(c_ag[i].h, 1)
                g_.dma_start(out=out[:, :], in_=y1_scr[:, :])\
                    .then_inc(d_g.h, 16)
                d_g.n += 16
                g_.wait_ge(d_g.h, d_g.n)
        ges.close()
        return nc
    aes = ExitStack()
    s_xT = sb(aes, "s_xT", [128, 16 * N], BF16)
    s_P = sb(aes, "s_P", [128, 3 * N], BF16)
    s_kT = sb(aes, "s_kT", [128, N], BF16)
    s_qT = sb(aes, "s_qT", [128, N], BF16)
    s_v = sb(aes, "s_v", [128, N], BF16)
    s_wh = sb(aes, "s_wh", [128, 3 * EMB], BF16)
    s_stg = sb(aes, "s_stg", [128, 2048], F32)
    s_pt = sb(aes, "s_pt", [128, 4096], BF16)
    s_oacc = sb(aes, "s_oacc", [128, 256], F32)
    OT0 = 2 * N

    with nc.Block() as blk:
        v0 = v_c.n
        xm = {i: v0 + i + 1 for i in range(32)}

        @blk.sync
        def _(sync):
            for i in range(32):
                k, hf = i // 2, i % 2
                if i >= 1:
                    sync.wait_ge(v_c.h, xm[i - 1])
                sync.dma_start(out=s_stg[:, :],
                               in_=xT[128 * k:128 * k + 128,
                                      2048 * hf:2048 * hf + 2048])\
                    .then_inc(d_xl.h, 16)
                d_xl.n += 16

        @blk.vector
        def _(v):
            for i in range(32):
                k, hf = i // 2, i % 2
                v.wait_ge(d_xl.h, 16 * (i + 1))
                v_c.inc(v.tensor_copy(
                    s_xT[:, N * k + 2048 * hf:N * k + 2048 * hf + 2048],
                    s_stg[:, :]))

    if MAX_PHASE < 2.5:
        with nc.Block() as blk:
            @blk.gpsimd
            def _(g_):
                for i in range(3):
                    g_.wait_ge(c_ag[i].h, 1)
                g_.dma_start(out=out[:, :], in_=y1_scr[:, :])\
                    .then_inc(d_g.h, 16)
                d_g.n += 16
                g_.wait_ge(d_g.h, d_g.n)
        aes.close()
        ges.close()
        return nc

    with nc.Block() as blk:
        SC = attn_schedule(v_c.n, a_c.n, m_c.n, d_g.n)
        wdc, pj, pje = SC["wdc"], SC["pj"], SC["pje"]
        smm, cmk, expm, norm = SC["smm"], SC["cmk"], SC["expm"], SC["norm"]
        ptm, pte, avm, oev = SC["ptm"], SC["pte"], SC["avm"], SC["oev"]
        odma, oad = SC["odma"], SC["oad"]
        exp_seq, exp_pos = SC["exp_seq"], SC["exp_pos"]

        def wait_oev(eng, key):
            kind, val = oev[key]
            eng.wait_ge(a_c.h if kind == 'a' else v_c.h, val)

        @blk.sync
        def _(sync):
            wts = [wqT, wkT, wvT]
            for h in range(ATTN_HEADS):
                for w in range(3):
                    idx = h * 3 + w
                    if idx >= 1:
                        pidx = idx - 1
                        sync.wait_ge(v_c.h, wdc[(pidx // 3, pidx % 3)])
                    sync.dma_start(
                        out=bass.AP(s_stg, 0,
                                    [[2048, 128], [128, 16], [1, 128]]),
                        in_=bass.AP(wts[w], 128 * h,
                                    [[1024, 128], [131072, 16], [1, 128]]))\
                        .then_inc(d_wh.h, 16)
                    d_wh.n += 16

        @blk.vector
        def _(v):
            for h in range(ATTN_HEADS):
                for w in range(3):
                    idx = h * 3 + w
                    v.wait_ge(d_wh.h, 16 * (idx + 1))
                    if h > 0:
                        v.wait_ge(m_c.h, pj[(h - 1, 2, 7)])
                    v_c.inc(v.tensor_copy(
                        s_wh[:, EMB * w:EMB * w + EMB],
                        s_stg[:, :]))
                    assert v_c.n == wdc[(h, w)]
                if ATTN_STOP == 'proj':
                    continue

                def emit_oad(g):
                    nki = 2 * g + 2
                    v.wait_ge(m_c.h, avm[(h, g, 15)])
                    v_c.inc(v.tensor_copy(s_oacc[:, :], p_o[:, 0:256]))
                    assert v_c.n == oad[(h, g, 0)]
                    v.wait_ge(m_c.h, avm[(h, g, nki - 1)])
                    v_c.inc(v.tensor_tensor(s_oacc[:, :], s_oacc[:, :],
                                            p_o[:, 0:256], ALU.add))
                    assert v_c.n == oad[(h, g, 1)]
                    v_c.inc(v.tensor_copy(
                        s_P[:, OT0 + 256 * g:OT0 + 256 * g + 256],
                        s_oacc[:, :]))
                    assert ('v', v_c.n) == oev[(h, g)]

                for g in range(ATTN_GROUPS):
                    if (ATTN_STOP == 'full' and g >= 1
                            and 2 * (g - 1) + 2 > 16):
                        emit_oad(g - 1)
                    for qc in (2 * g, 2 * g + 1):
                        sgs = _segs(qc)
                        lsg = len(sgs) - 1
                        kb, wd = sgs[lsg]
                        bank = p_s[exp_pos[(h, qc, lsg)] % 4]
                        v.wait_ge(m_c.h, smm[(h, qc, lsg)])
                        v_c.inc(v.tensor_tensor(bank[0:128, wd - 128:wd],
                                                bank[0:128, wd - 128:wd],
                                                s_cmask[:, :], ALU.add))
                        assert v_c.n == cmk[(h, qc)]
                        v.wait_ge(a_c.h, expm[(h, qc, lsg)])
                        zoff = 8 * (qc % 2)
                        vf(v, v.tensor_reduce(s_z[:, :],
                                              s_zb[:, zoff:zoff + len(sgs)],
                                              AX.X, ALU.add))
                        vf(v, v.reciprocal(s_zr[:, :], s_z[:, :]))
                        slot = qc % 2
                        kr = 128 * (qc + 1)
                        v_c.inc(v.tensor_scalar(
                            s_P[:, N * slot:N * slot + kr],
                            s_P[:, N * slot:N * slot + kr],
                            s_zr[:, :], None, ALU.mult))
                        assert v_c.n == norm[(h, qc)]
                if ATTN_STOP == 'smax':
                    continue
                if 2 * (ATTN_GROUPS - 1) + 2 > 16:
                    emit_oad(ATTN_GROUPS - 1)

        @blk.tensor
        def _(t):
            pjseq = []
            for h in range(ATTN_HEADS):
                for w in (1, 0):
                    for tb in range(8):
                        pjseq.append((h, w, tb))
                for kb4 in range(8):
                    pjseq.append((h, 2, kb4))
            pjpos = {k: i for i, k in enumerate(pjseq)}

            for h in range(ATTN_HEADS):
                for w, dst in ((1, s_kT), (0, s_qT)):
                    t.wait_ge(v_c.h, wdc[(h, w)])
                    for tb in range(8):
                        key = (h, w, tb)
                        si = pjpos[key]
                        if si >= 2:
                            t.wait_ge(a_c.h, pje[pjseq[si - 2]])
                        for di in range(16):
                            mm = t.matmul(
                                p_pj[si % 2][:, :],
                                s_wh[:, EMB * w + 128 * di:
                                     EMB * w + 128 * di + 128],
                                s_xT[:, N * di + 512 * tb:
                                     N * di + 512 * tb + 512],
                                start=(di == 0), stop=(di == 15))
                        m_c.inc(mm)
                        assert m_c.n == pj[key]
                t.wait_ge(v_c.h, wdc[(h, 2)])
                for kb4 in range(8):
                    key = (h, 2, kb4)
                    si = pjpos[key]
                    if si >= 2:
                        t.wait_ge(a_c.h, pje[pjseq[si - 2]])
                    for sub in range(4):
                        ki = 4 * kb4 + sub
                        for di in range(16):
                            mm = t.matmul(
                                p_pj[si % 2][:, 128 * sub:128 * sub + 128],
                                s_xT[:, N * di + 128 * ki:
                                     N * di + 128 * ki + 128],
                                s_wh[:, 2 * EMB + 128 * di:
                                     2 * EMB + 128 * di + 128],
                                start=(di == 0), stop=(di == 15))
                    m_c.inc(mm)
                    assert m_c.n == pj[key]
                if ATTN_STOP == 'proj':
                    continue
                t.wait_ge(a_c.h, pje[(h, 2, 7)])
                for g in range(ATTN_GROUPS):
                    qc0, qc1 = 2 * g, 2 * g + 1
                    for qc in (qc0, qc1):
                        for sg, (kb, wd) in enumerate(_segs(qc)):
                            gi = exp_pos[(h, qc, sg)]
                            if gi >= 4:
                                t.wait_ge(a_c.h, expm[exp_seq[gi - 4]])
                            bank = p_s[gi % 4]
                            m_c.inc(t.matmul(
                                bank[0:128, 0:wd],
                                s_qT[:, 128 * qc:128 * qc + 128],
                                s_kT[:, kb:kb + wd],
                                start=True, stop=True))
                            assert m_c.n == smm[(h, qc, sg)]
                    if ATTN_STOP == 'smax':
                        continue
                    nki = qc1 + 1
                    t.wait_ge(v_c.h, norm[(h, qc1)])
                    for sub0 in range(0, nki, 16):
                        sks = list(range(sub0, min(sub0 + 16, nki)))
                        for ki in sks:
                            both = ki <= qc0
                            if ki >= 2:
                                t.wait_ge(a_c.h, pte[(h, g, ki - 2)])
                            bk = p_pj[ki % 2]
                            if both:
                                t.matmul(bk[:, 0:128],
                                         s_P[:, 128 * ki:128 * ki + 128],
                                         s_ident[:, :],
                                         start=True, stop=True)
                            m_c.inc(t.matmul(
                                bk[:, 128:256],
                                s_P[:, N + 128 * ki:N + 128 * ki + 128],
                                s_ident[:, :], start=True, stop=True))
                            assert m_c.n == ptm[(h, g, ki)]
                        t.wait_ge(a_c.h, pte[(h, g, sks[-1])])
                        if sub0 == 0:
                            if (h, g) != (0, 0):
                                prev = (h, g - 1) if g > 0 else (h - 1, ATTN_GROUPS - 1)
                                wait_oev(t, prev)
                        else:
                            t.wait_ge(v_c.h, oad[(h, g, 0)])
                        for ki in sks:
                            both = ki <= qc0
                            slot = 256 * (ki % 16)
                            if both:
                                mm = t.matmul(
                                    p_o[:, 0:256],
                                    s_v[:, 128 * ki:128 * ki + 128],
                                    s_pt[:, slot:slot + 256],
                                    start=(ki == sks[0]),
                                    stop=(ki == sks[-1]))
                            else:
                                mm = t.matmul(
                                    p_o[:, 128:256],
                                    s_v[:, 128 * ki:128 * ki + 128],
                                    s_pt[:, slot + 128:slot + 256],
                                    start=(ki == sks[0]),
                                    stop=(ki == sks[-1]))
                            m_c.inc(mm)
                            assert m_c.n == avm[(h, g, ki)]

        @blk.scalar
        def _(a):
            for h in range(ATTN_HEADS):
                for w, dst in ((1, s_kT), (0, s_qT)):
                    for tb in range(8):
                        key = (h, w, tb)
                        a.wait_ge(m_c.h, pj[key])
                        if w == 0:
                            bias, scale = s_bqs[:, h:h + 1], QSCALE
                        else:
                            bias, scale = s_bqkv[:, 3 * h + 1:3 * h + 2], 1.0
                        si = h * 24 + (0 if w == 1 else 8) + tb
                        a_c.inc(a.activation(
                            dst[:, 512 * tb:512 * tb + 512],
                            p_pj[si % 2][:, :],
                            AF.Identity, bias=bias, scale=scale))
                        assert a_c.n == pje[key]
                for kb4 in range(8):
                    key = (h, 2, kb4)
                    a.wait_ge(m_c.h, pj[key])
                    a_c.inc(a.copy(s_v[:, 512 * kb4:512 * kb4 + 512],
                                   p_pj[(h * 24 + 16 + kb4) % 2][:, :]))
                    assert a_c.n == pje[key]
                if ATTN_STOP == 'proj':
                    continue
                for g in range(ATTN_GROUPS):
                    qc0, qc1 = 2 * g, 2 * g + 1
                    for qc in (qc0, qc1):
                        sgs = _segs(qc)
                        lsg = len(sgs) - 1
                        slot = qc % 2
                        if qc >= 2:
                            a.wait_ge(v_c.h, norm[(h, qc - 2)])
                        elif h > 0:
                            a.wait_ge(v_c.h, norm[(h - 1, 30 + qc)])
                        if ATTN_STOP == 'full':
                            if qc >= 2:
                                gp = (qc - 2) // 2
                                a.wait_ge(m_c.h, ptm[(h, gp, 2 * gp + 1)])
                            elif h > 0:
                                a.wait_ge(m_c.h, ptm[(h - 1, 15, 31)])
                        for sg, (kb, wd) in enumerate(sgs):
                            bank = p_s[exp_pos[(h, qc, sg)] % 4]
                            if sg == lsg:
                                a.wait_ge(v_c.h, cmk[(h, qc)])
                            else:
                                a.wait_ge(m_c.h, smm[(h, qc, sg)])
                            a_c.inc(a.activation(
                                s_P[:, N * slot + kb:N * slot + kb + wd],
                                bank[0:128, 0:wd], AF.Exp,
                                bias=s_esh[:, :], scale=1.0,
                                accum_out=s_zb[:, 8 * (qc % 2) + sg:
                                               8 * (qc % 2) + sg + 1]))
                            assert a_c.n == expm[(h, qc, sg)]
                    if ATTN_STOP == 'smax':
                        continue
                    nki = qc1 + 1
                    for ki in range(nki):
                        both = ki <= qc0
                        a.wait_ge(m_c.h, ptm[(h, g, ki)])
                        if ki >= 16:
                            a.wait_ge(m_c.h, avm[(h, g, ki - 16)])
                        bk = p_pj[ki % 2]
                        slot = 256 * (ki % 16)
                        lo = 0 if both else 128
                        a_c.inc(a.copy(s_pt[:, slot + lo:slot + 256],
                                       bk[:, lo:256]))
                        assert a_c.n == pte[(h, g, ki)]
                    if nki <= 16:
                        a.wait_ge(m_c.h, avm[(h, g, nki - 1)])
                        if g == 0 and h > 0:
                            a.wait_ge(d_g.h, odma[h - 1])
                        a_c.inc(a.copy(
                            s_P[:, OT0 + 256 * g:OT0 + 256 * g + 256],
                            p_o[:, 0:256]))
                        assert ('a', a_c.n) == oev[(h, g)]

        @blk.gpsimd
        def _(g_):
            if ATTN_STOP != 'full':
                return
            for h in range(ATTN_HEADS):
                wait_oev(g_, (h, ATTN_GROUPS - 1))
                g_.dma_start(out=ago_in[128 * h:128 * h + 128, :],
                             in_=s_P[:, OT0:OT0 + N]).then_inc(d_g.h, 16)
                d_g.n += 16
                assert d_g.n == odma[h]
                if h % 2 == 1:
                    g_.wait_ge(d_g.h, d_g.n)
                    k = h // 2
                    g_.collective_compute(
                        "AllGather", ALU.bypass, replica_groups=PAIRS,
                        ins=[ago_in[256 * k:256 * k + 256, :].opt()],
                        outs=[ago_out[k].ap().opt()])\
                        .then_inc(c_ao[k].h, 1)

    aes.close()

    if MAX_PHASE < 3:
        with nc.Block() as blk:
            @blk.gpsimd
            def _(g_):
                for k in range(ATTN_HEADS // 2):
                    g_.wait_ge(c_ao[k].h, 1)
                g_.dma_start(out=out[:, :], in_=y1_scr[:, :])\
                    .then_inc(d_g.h, 16)
                d_g.n += 16
                g_.wait_ge(d_g.h, d_g.n)
        ges.close()
        return nc
    wes = ExitStack()
    s_y1T = sb(wes, "s_y1T", [128, 16 * NT], BF16)
    w2 = ExitStack()
    s_woT = sb(w2, "s_woT", [128, 16 * EMB], BF16)
    s_oTq = sb(w2, "s_oTq", [128, 8192], BF16)
    s_aoT = sb(w2, "s_aoT", [128, 8192], BF16)
    s_nat = sb(w2, "s_nat", [128, EMB], F32)
    s_y1b = sb(w2, "s_y1b", [128, EMB], BF16)
    s_bot = sb(w2, "s_bot", [128, EMB], F32)
    s_stgw = sb(w2, "s_stgw", [128, 2 * EMB], F32)
    s_bvc = sb(w2, "s_bvc", [128, 16], BF16)
    s_bv16 = sb(w2, "s_bv16", [16, 128], F32)
    s_bv16b = sb(w2, "s_bv16b", [16, 128], BF16)
    s_onb = sb(w2, "s_onb", [1, 128], BF16)
    s_hi = sb(w2, "s_hi", [1, EMB], BF16)
    s_lob = sb(w2, "s_lob", [1, EMB], BF16)

    def colof(ci):
        if ci < 8:
            k, s = ci // 2, ci % 2
        else:
            k, s = (ci - 8) // 2, 2 + (ci - 8) % 2
        return 512 * (4 * k + s)

    with nc.Block() as blk:
        v0 = v_c.n
        ld0 = d_x.n
        wdcm = {ci: v0 + ci + 2 for ci in range(16)}

        @blk.sync
        def _(sync):
            sync.dma_start(out=s_stgw[0:1, EMB:2 * EMB], in_=bo_in[:, :]).then_inc(d_c.h, 16)
            sync.dma_start(out=s_bv16[:, :], in_=bvfull[:, :])\
                .then_inc(d_c.h, 16)
            d_c.n += 32
            for ci in range(16):
                if ci >= 1:
                    sync.wait_ge(v_c.h, wdcm[ci - 1])
                sync.dma_start(
                    out=s_stgw[:, 0:EMB],
                    in_=woT[128 * ci:128 * ci + 128, :]).then_inc(d_wo.h, 16)
                d_wo.n += 16

        @blk.vector
        def _(v):
            v.memset(s_onb[:, :], 1.0)
            v.wait_ge(d_c.h, d_c.n)
            v_c.inc(v.tensor_copy(s_bv16b[:, :], s_bv16[:, :]))
            for ci in range(16):
                v.wait_ge(d_wo.h, 16 * (ci + 1))
                v_c.inc(v.tensor_copy(
                    s_woT[:, EMB * ci:EMB * ci + EMB],
                    s_stgw[:, 0:EMB]))
                assert v_c.n == wdcm[ci]

        bt_m0 = m_c.n
        bt_a0 = a_c.n

        @blk.tensor
        def _(t):
            t.wait_ge(v_c.h, v0 + 1)
            m_c.inc(t.matmul(p_pt[:, 0:16], s_bv16b[:, :],
                             s_ident[0:16, 0:16], start=True, stop=True))

        @blk.scalar
        def _(a):
            a.wait_ge(m_c.h, bt_m0 + 1)
            a_c.inc(a.copy(s_bvc[:, :], p_pt[:, 0:16]))

        m0 = m_c.n
        a0 = a_c.n

        @blk.tensor
        def _(t):
            t.wait_ge(v_c.h, wdcm[15])
            t.wait_ge(a_c.h, bt_a0 + 1)
            t.wait_ge(d_c.h, d_c.n)
            for j in range(4):
                if j >= 1:
                    t.wait_ge(a_c.h, a0 + j)
                for ci in range(16):
                    mm = t.matmul(
                        p_pj[0][0:1, :], s_bvc[:, ci:ci + 1],
                        s_woT[:, EMB * ci + 512 * j:EMB * ci + 512 * j + 512],
                        start=(ci == 0), stop=(ci == 15))
                m_c.inc(mm)

        @blk.scalar
        def _(a):
            for j in range(4):
                a.wait_ge(m_c.h, m0 + j + 1)
                a_c.inc(a.copy(s_bot[0:1, 512 * j:512 * j + 512],
                               p_pj[0][0:1, :]))

        @blk.vector
        def _(v):
            v.wait_ge(a_c.h, a0 + 4)
            v.wait_ge(d_c.h, d_c.n)
            vf(v, v.tensor_add(s_stgw[0:1, EMB:2 * EMB],
                               s_stgw[0:1, EMB:2 * EMB], s_bot[0:1, :]))
            vf(v, v.tensor_copy(s_hi[:, :], s_stgw[0:1, EMB:2 * EMB]))
            vf(v, v.tensor_copy(s_nat[0:1, 0:EMB], s_hi[:, :]))
            v_c.inc(v.tensor_tensor(s_lob[:, :], s_stgw[0:1, EMB:2 * EMB],
                                    s_nat[0:1, 0:EMB], ALU.subtract))

        bc_v0 = v_c.n
        bc_m0 = m_c.n
        bc_a0 = a_c.n

        @blk.tensor
        def _(t):
            t.wait_ge(v_c.h, bc_v0)
            for j in range(4):
                if j >= 1:
                    t.wait_ge(a_c.h, bc_a0 + j)
                t.matmul(p_pj[1][:, :], s_onb[:, :],
                         s_hi[0:1, 512 * j:512 * j + 512],
                         start=True, stop=False)
                m_c.inc(t.matmul(p_pj[1][:, :], s_onb[:, :],
                                 s_lob[0:1, 512 * j:512 * j + 512],
                                 start=False, stop=True))

        @blk.scalar
        def _(a):
            for j in range(4):
                a.wait_ge(m_c.h, bc_m0 + j + 1)
                a_c.inc(a.copy(s_bot[:, 512 * j:512 * j + 512],
                               p_pj[1][:, :]))

    with nc.Block() as blk:
        g0 = d_g.n
        m0 = m_c.n
        a0 = a_c.n
        v0 = v_c.n
        x0 = d_x.n
        gdma = {qt: g0 + 128 * qt + 64 for qt in range(4)}
        gq = gdma
        wom, woe, lns = {}, {}, {}
        trm, y1m, tre, y1e = {}, {}, {}, {}
        tg_of = {}
        tgrp_e = {}
        m, a = m0, a0
        tg = 0
        for qt in range(4):
            for do in range(16):
                m += 1
                wom[(qt, do)] = m
            for tt in range(4):
                for gg in range(4):
                    m += 1
                    trm[(qt, tt, gg)] = m
                    tg_of[(qt, tt, 0, gg)] = tg
                    tg += 1
                for gg in range(4):
                    m += 1
                    y1m[(qt, tt, gg)] = m
                    tg_of[(qt, tt, 1, gg)] = tg
                    tg += 1
        for qt in range(4):
            for do in range(16):
                a += 1
                woe[(qt, do)] = a
            for tt in range(4):
                for gg in range(4):
                    a += 1
                    tre[(qt, tt, gg)] = a
                    tgrp_e[tg_of[(qt, tt, 0, gg)]] = a
                a += 1
                lns[(qt, tt)] = a
                for gg in range(4):
                    a += 1
                    y1e[(qt, tt, gg)] = a
                    tgrp_e[tg_of[(qt, tt, 1, gg)]] = a
        v = v0
        lnv, lnm, y1bm = {}, {}, {}
        for qt in range(4):
            for tt in range(4):
                v += 1
                lnv[(qt, tt)] = v
                v += 1
                lnm[(qt, tt)] = v
                v += 1
                y1bm[(qt, tt)] = v
        xrm = {}
        x = 0
        for qt in range(4):
            for tt in range(4):
                x += 16
                xrm[(qt, tt)] = x
        y1d = {}
        for qt in range(4):
            for tt in range(4):
                y1d[(qt, tt)] = g0 + 128 * qt + 64 + 16 * (tt + 1)
        tbank = [p_pt, p_s[0]]

        @blk.gpsimd
        def _(g_):
            for k in range(4):
                g_.wait_ge(c_ao[k].h, 1)
            with g_.register("r_t") as r_t:
                for qt in range(4):
                    if qt >= 1:
                        g_.wait_ge(m_c.h, wom[(qt - 1, 15)])
                    g_.reg_load(r_t, s_offs[0:1, qt:qt + 1])
                    for k in range(4):
                        src = bass.AP(ago_out[k], r_t,
                                      [[N, 128], [128 * N, 4], [1, 512]])
                        dst = bass.AP(s_oTq, 2048 * k,
                                      [[8192, 128], [512, 4], [1, 512]])
                        g_.dma_start(out=dst, in_=src).then_inc(d_g.h, 16)
                        d_g.n += 16
                    assert d_g.n == gdma[qt]
                    for tt in range(4):
                        g_.wait_ge(v_c.h, lnm[(qt, tt)])
                        row = 512 * qt + 128 * tt
                        g_.dma_start(out=y1_scr[row:row + 128, :],
                                     in_=s_nat[:, :]).then_inc(d_g.h, 16)
                        d_g.n += 16
                        assert d_g.n == y1d[(qt, tt)]
            g_.wait_ge(d_g.h, d_g.n)

        @blk.sync
        def _(sync):
            for qt in range(4):
                for tt in range(4):
                    if (qt, tt) != (0, 0):
                        pt_ = (qt, tt - 1) if tt > 0 else (qt - 1, 3)
                        sync.wait_ge(v_c.h, lnm[pt_])
                    else:
                        sync.wait_ge(v_c.h, wdcm[15])
                    row = 512 * qt + 128 * tt
                    sync.dma_start(out=s_stgw[:, 0:EMB],
                                   in_=x_res[row:row + 128, :])\
                        .then_inc(d_xr.h, 16)
                    d_xr.n += 16
                    assert d_xr.n == xrm[(qt, tt)]

        @blk.tensor
        def _(t):
            for qt in range(4):
                t.wait_ge(d_g.h, gq[qt])
                for do in range(16):
                    if do >= 2:
                        t.wait_ge(a_c.h, woe[(qt, do - 2)])
                    for ci in range(16):
                        mm = t.matmul(
                            p_pj[do % 2][:, :],
                            s_woT[:, EMB * ci + 128 * do:
                                  EMB * ci + 128 * do + 128],
                            s_oTq[:, colof(ci):colof(ci) + 512],
                            start=(ci == 0), stop=(ci == 15))
                    m_c.inc(mm)
                    assert m_c.n == wom[(qt, do)]
                t.wait_ge(a_c.h, woe[(qt, 15)])
                for tt in range(4):
                    for gg in range(4):
                        ti = tg_of[(qt, tt, 0, gg)]
                        if ti >= 2:
                            t.wait_ge(a_c.h, tgrp_e[ti - 2])
                        bk = tbank[ti % 2]
                        for i in range(4):
                            do = 4 * gg + i
                            mm = t.matmul(
                                bk[:, 128 * i:128 * i + 128],
                                s_aoT[:, 512 * do + 128 * tt:
                                      512 * do + 128 * tt + 128],
                                s_ident[:, :], start=True, stop=True)
                        m_c.inc(mm)
                        assert m_c.n == trm[(qt, tt, gg)]
                    t.wait_ge(v_c.h, y1bm[(qt, tt)])
                    for gg in range(4):
                        ti = tg_of[(qt, tt, 1, gg)]
                        if ti >= 2:
                            t.wait_ge(a_c.h, tgrp_e[ti - 2])
                        bk = tbank[ti % 2]
                        for i in range(4):
                            dc = 4 * gg + i
                            mm = t.matmul(
                                bk[:, 128 * i:128 * i + 128],
                                s_y1b[:, 128 * dc:128 * dc + 128],
                                s_ident[:, :], start=True, stop=True)
                        m_c.inc(mm)
                        assert m_c.n == y1m[(qt, tt, gg)]

        @blk.scalar
        def _(a):
            for qt in range(4):
                for do in range(16):
                    a.wait_ge(m_c.h, wom[(qt, do)])
                    a_c.inc(a.copy(s_aoT[:, 512 * do:512 * do + 512],
                                   p_pj[do % 2][:, :]))
                    assert a_c.n == woe[(qt, do)]
                for tt in range(4):
                    if (qt, tt) != (0, 0):
                        pv = (qt, tt - 1) if tt > 0 else (qt - 1, 3)
                        a.wait_ge(v_c.h, y1bm[pv])
                        a.wait_ge(d_g.h, y1d[pv])
                    for gg in range(4):
                        a.wait_ge(m_c.h, trm[(qt, tt, gg)])
                        bk = tbank[tg_of[(qt, tt, 0, gg)] % 2]
                        a_c.inc(a.copy(s_nat[:, 512 * gg:512 * gg + 512],
                                       bk[:, :]))
                        assert a_c.n == tre[(qt, tt, gg)]
                    a.wait_ge(v_c.h, lnv[(qt, tt)])
                    a_c.inc(a.activation(s_sd[:, :], s_v1[:, :], AF.Sqrt,
                                         bias=s_lneps[:, :],
                                         scale=1.0 / EMB))
                    assert a_c.n == lns[(qt, tt)]
                    for gg in range(4):
                        a.wait_ge(m_c.h, y1m[(qt, tt, gg)])
                        bk = tbank[tg_of[(qt, tt, 1, gg)] % 2]
                        row = 512 * qt + 128 * tt
                        a_c.inc(a.copy(
                            bass.AP(s_y1T, (4 * gg) * NT + row,
                                    [[16 * NT, 128], [NT, 4], [1, 128]]),
                            bass.AP(bk, 0, [[512, 128], [128, 4],
                                            [1, 128]])))
                        assert a_c.n == y1e[(qt, tt, gg)]

        @blk.vector
        def _(v):
            for qt in range(4):
                for tt in range(4):
                    for gg in range(4):
                        v.wait_ge(a_c.h, tre[(qt, tt, gg)])
                    v.wait_ge(d_xr.h, xrm[(qt, tt)])
                    if (qt, tt) != (0, 0):
                        pv = (qt, tt - 1) if tt > 0 else (qt - 1, 3)
                        v.wait_ge(d_g.h, y1d[pv])
                    v.tensor_add(s_nat[:, :], s_nat[:, :], s_bot[:, :])
                    v.tensor_add(s_nat[:, :], s_nat[:, :], s_stgw[:, 0:EMB])
                    vf(v, v.tensor_reduce(s_m1[:, :], s_nat[:, :], AX.X,
                                          ALU.add))
                    vf(v, v.tensor_scalar(s_m1[:, :], s_m1[:, :], 1.0 / EMB,
                                          None, ALU.mult))
                    v.tensor_scalar(s_nat[:, :], s_nat[:, :], s_m1[:, :],
                                    None, ALU.subtract)
                    v.tensor_tensor(s_stgw[:, EMB:2 * EMB], s_nat[:, :],
                                    s_nat[:, :], ALU.mult)
                    v_c.inc(v.tensor_reduce(s_v1[:, :],
                                            s_stgw[:, EMB:2 * EMB],
                                            AX.X, ALU.add))
                    assert v_c.n == lnv[(qt, tt)]
                    v.wait_ge(a_c.h, lns[(qt, tt)])
                    vf(v, v.reciprocal(s_r1[:, :], s_sd[:, :]))
                    v_c.inc(v.tensor_scalar(s_nat[:, :], s_nat[:, :],
                                            s_r1[:, :], None, ALU.mult))
                    assert v_c.n == lnm[(qt, tt)]
                    if (qt, tt) != (0, 0):
                        pv = (qt, tt - 1) if tt > 0 else (qt - 1, 3)
                        v.wait_ge(m_c.h, y1m[(pv[0], pv[1], 3)])
                    v_c.inc(v.tensor_copy(s_y1b[:, :], s_nat[:, :]))
                    assert v_c.n == y1bm[(qt, tt)]

    w2.close()

    if MAX_PHASE < 4:
        with nc.Block() as blk:
            @blk.gpsimd
            def _(g_):
                g_.dma_start(out=out[:, :], in_=y1_scr[:, :])\
                    .then_inc(d_g.h, 16)
                d_g.n += 16
                g_.wait_ge(d_g.h, d_g.n)
        wes.close()
        ges.close()
        return nc
    fes = ExitStack()
    s_hT = sb(fes, "s_hT", [128, 44 * 512], BF16)
    s_sl1 = sb(fes, "s_sl1", [128, 2 * 16 * 256], BF16)
    s_sl2 = sb(fes, "s_sl2", [128, 2 * 16 * 256], BF16)
    s_w3s = sb(fes, "s_w3s", [128, 2 * 44 * 128], BF16)
    s_foT = sb(fes, "s_foT", [128, 16 * 512], BF16)
    s_silu = sb(fes, "s_silu", [128, 512], BF16)
    s_natf = sb(fes, "s_natf", [128, EMB], F32)
    s_sqf = sb(fes, "s_sqf", [128, EMB], F32)
    s_stgf = sb(fes, "s_stgf", [128, EMB], F32)

    with nc.Block() as blk:
        @blk.gpsimd
        def _(g_):
            for i in range(3):
                g_.wait_ge(c_ag[i].h, 1)

    for tc in range(4):
        with nc.Block() as blk:
            t0 = 512 * tc
            m0, a0, v0, x0, g0 = m_c.n, a_c.n, v_c.n, d_x.n, d_g.n
            abm = {}
            m = m0
            for fc in range(22):
                for ft in range(2):
                    m += 1
                    abm[(fc, ft, 0)] = m
                    m += 1
                    abm[(fc, ft, 1)] = m
            outm = {}
            for do in range(16):
                m += 1
                outm[do] = m
            trm2 = {}
            tg = 0
            tg2_of = {}
            for tt in range(4):
                for gg in range(4):
                    m += 1
                    trm2[(tt, gg)] = m
                    tg2_of[(tt, gg)] = tg
                    tg += 1
            a = a0
            silm = {}
            for fc in range(22):
                for ft in range(2):
                    a += 1
                    silm[(fc, ft)] = a
            oute = {}
            for do in range(16):
                a += 1
                oute[do] = a
            tre2 = {}
            tg2_e = {}
            lns2 = {}
            for tt in range(4):
                for gg in range(4):
                    a += 1
                    tre2[(tt, gg)] = a
                    tg2_e[tg2_of[(tt, gg)]] = a
                a += 1
                lns2[tt] = a
            v = v0
            hm = {}
            for fc in range(22):
                for ft in range(2):
                    v += 1
                    hm[(fc, ft)] = v
            lnv2, ln2m = {}, {}
            for tt in range(4):
                v += 1
                lnv2[tt] = v
                v += 1
                ln2m[tt] = v
            slm = {}
            w3m = {}
            resm = {}
            outd = {}
            g = g0
            for tt in range(4):
                g += 16
                outd[tt] = g
            tbank2 = [p_pt, p_s[0]]

            @blk.sync
            def _(sync, tc=tc):
                for fc in range(22):
                    if fc >= 2:
                        sync.wait_ge(m_c.h, abm[(fc - 2, 1, 1)])
                    f0 = 256 * fc
                    r0, r1 = f0 // 704, (f0 + 255) // 704
                    half = 4096 * (fc % 2)
                    dsem = d_sl[fc % 2]
                    for wsl, agt in ((s_sl1, ag_out[0]), (s_sl2, ag_out[1])):
                        if r0 == r1:
                            fl = f0 - 704 * r0
                            sync.dma_start(
                                out=bass.AP(wsl, half,
                                            [[8192, 128], [256, 16],
                                             [1, 256]]),
                                in_=bass.AP(agt, (2048 * r0) * 704 + fl,
                                            [[704, 128], [704 * 128, 16],
                                             [1, 256]]))\
                                .then_inc(dsem.h, 16)
                            dsem.n += 16
                        else:
                            w0 = 704 * r1 - f0
                            fl = f0 - 704 * r0
                            sync.dma_start(
                                out=bass.AP(wsl, half,
                                            [[8192, 128], [256, 16],
                                             [1, w0]]),
                                in_=bass.AP(agt, (2048 * r0) * 704 + fl,
                                            [[704, 128], [704 * 128, 16],
                                             [1, w0]]))\
                                .then_inc(dsem.h, 16)
                            dsem.n += 16
                            sync.dma_start(
                                out=bass.AP(wsl, half + w0,
                                            [[8192, 128], [256, 16],
                                             [1, 256 - w0]]),
                                in_=bass.AP(agt, (2048 * r1) * 704,
                                            [[704, 128], [704 * 128, 16],
                                             [1, 256 - w0]]))\
                                .then_inc(dsem.h, 16)
                            dsem.n += 16
                    slm[fc] = (fc % 2, dsem.n)
                for do in range(16):
                    if do >= 2:
                        sync.wait_ge(m_c.h, outm[do - 2])
                    r, hf = do // 2, do % 2
                    sync.dma_start(
                        out=bass.AP(s_w3s, 5632 * (do % 2),
                                    [[11264, 128], [128, 44], [1, 128]]),
                        in_=bass.AP(ag_out[2], (5632 * r) * 256 + 128 * hf,
                                    [[256, 128], [128 * 256, 44], [1, 128]]))\
                        .then_inc(d_w3p[do % 2].h, 16)
                    d_w3p[do % 2].n += 16
                    w3m[do] = (do % 2, d_w3p[do % 2].n)
                for tt in range(4):
                    if tt >= 1:
                        sync.wait_ge(v_c.h, ln2m[tt - 1])
                    row = t0 + 128 * tt
                    sync.dma_start(out=s_stgf[:, :],
                                   in_=y1_scr[row:row + 128, :])\
                        .then_inc(d_res.h, 16)
                    d_res.n += 16
                    resm[tt] = d_res.n

            @blk.tensor
            def _(t, tc=tc):
                for fc in range(22):
                    t.wait_ge(d_sl[slm[fc][0]].h, slm[fc][1])
                    half = 4096 * (fc % 2)
                    for ft in range(2):
                        fi = 2 * fc + ft
                        for w, sl in ((0, s_sl1), (1, s_sl2)):
                            bank = p_s[2 * ft + w]
                            if fc >= 1:
                                if w == 0:
                                    t.wait_ge(a_c.h, silm[(fc - 1, ft)])
                                else:
                                    t.wait_ge(v_c.h, hm[(fc - 1, ft)])
                            for di in range(16):
                                mm = t.matmul(
                                    bank[:, :],
                                    sl[:, half + 256 * di + 128 * ft:
                                       half + 256 * di + 128 * ft + 128],
                                    s_y1T[:, NT * di + t0:NT * di + t0 + 512],
                                    start=(di == 0), stop=(di == 15))
                            m_c.inc(mm)
                            assert m_c.n == abm[(fc, ft, w)]
                for do in range(16):
                    t.wait_ge(d_w3p[w3m[do][0]].h, w3m[do][1])
                    t.wait_ge(v_c.h, hm[(21, 1)])
                    if do >= 2:
                        t.wait_ge(a_c.h, oute[do - 2])
                    for fi in range(44):
                        mm = t.matmul(
                            p_pj[do % 2][:, :],
                            s_w3s[:, 5632 * (do % 2) + 128 * fi:
                                  5632 * (do % 2) + 128 * fi + 128],
                            s_hT[:, 512 * fi:512 * fi + 512],
                            start=(fi == 0), stop=(fi == 43))
                    m_c.inc(mm)
                    assert m_c.n == outm[do]
                t.wait_ge(a_c.h, oute[15])
                for tt in range(4):
                    for gg in range(4):
                        ti = tg2_of[(tt, gg)]
                        if ti >= 2:
                            t.wait_ge(a_c.h, tg2_e[ti - 2])
                        bk = tbank2[ti % 2]
                        for i in range(4):
                            do = 4 * gg + i
                            mm = t.matmul(
                                bk[:, 128 * i:128 * i + 128],
                                s_foT[:, 512 * do + 128 * tt:
                                      512 * do + 128 * tt + 128],
                                s_ident[:, :], start=True, stop=True)
                        m_c.inc(mm)
                        assert m_c.n == trm2[(tt, gg)]

            @blk.scalar
            def _(a, tc=tc):
                for fc in range(22):
                    for ft in range(2):
                        a.wait_ge(m_c.h, abm[(fc, ft, 0)])
                        if fc >= 1 or ft >= 1:
                            pv = (fc, ft - 1) if ft > 0 else (fc - 1, 1)
                            a.wait_ge(v_c.h, hm[pv])
                        a_c.inc(a.activation(s_silu[:, :],
                                             p_s[2 * ft + 0][:, :], AF.Silu))
                        assert a_c.n == silm[(fc, ft)]
                for do in range(16):
                    a.wait_ge(m_c.h, outm[do])
                    a_c.inc(a.copy(s_foT[:, 512 * do:512 * do + 512],
                                   p_pj[do % 2][:, :]))
                    assert a_c.n == oute[do]
                for tt in range(4):
                    if tt >= 1:
                        a.wait_ge(v_c.h, ln2m[tt - 1])
                        a.wait_ge(d_g.h, outd[tt - 1])
                    for gg in range(4):
                        a.wait_ge(m_c.h, trm2[(tt, gg)])
                        bk = tbank2[tg2_of[(tt, gg)] % 2]
                        a_c.inc(a.copy(s_natf[:, 512 * gg:512 * gg + 512],
                                       bk[:, :]))
                        assert a_c.n == tre2[(tt, gg)]
                    a.wait_ge(v_c.h, lnv2[tt])
                    a_c.inc(a.activation(s_sd[:, :], s_v1[:, :], AF.Sqrt,
                                         bias=s_lneps[:, :],
                                         scale=1.0 / EMB))
                    assert a_c.n == lns2[tt]

            @blk.vector
            def _(v, tc=tc):
                for fc in range(22):
                    for ft in range(2):
                        fi = 2 * fc + ft
                        v.wait_ge(a_c.h, silm[(fc, ft)])
                        v.wait_ge(m_c.h, abm[(fc, ft, 1)])
                        v_c.inc(v.tensor_tensor(
                            s_hT[:, 512 * fi:512 * fi + 512],
                            s_silu[:, :], p_s[2 * ft + 1][:, :], ALU.mult))
                        assert v_c.n == hm[(fc, ft)]
                for tt in range(4):
                    for gg in range(4):
                        v.wait_ge(a_c.h, tre2[(tt, gg)])
                    v.wait_ge(d_res.h, resm[tt])
                    v.tensor_add(s_natf[:, :], s_natf[:, :], s_stgf[:, :])
                    vf(v, v.tensor_reduce(s_m1[:, :], s_natf[:, :], AX.X,
                                          ALU.add))
                    vf(v, v.tensor_scalar(s_m1[:, :], s_m1[:, :], 1.0 / EMB,
                                          None, ALU.mult))
                    v.tensor_scalar(s_natf[:, :], s_natf[:, :], s_m1[:, :],
                                    None, ALU.subtract)
                    v.tensor_tensor(s_sqf[:, :], s_natf[:, :], s_natf[:, :],
                                    ALU.mult)
                    v_c.inc(v.tensor_reduce(s_v1[:, :], s_sqf[:, :],
                                            AX.X, ALU.add))
                    assert v_c.n == lnv2[tt]
                    v.wait_ge(a_c.h, lns2[tt])
                    vf(v, v.reciprocal(s_r1[:, :], s_sd[:, :]))
                    v_c.inc(v.tensor_scalar(s_natf[:, :], s_natf[:, :],
                                            s_r1[:, :], None, ALU.mult))
                    assert v_c.n == ln2m[tt]

            @blk.gpsimd
            def _(g_, tc=tc):
                for tt in range(4):
                    g_.wait_ge(v_c.h, ln2m[tt])
                    row = t0 + 128 * tt
                    g_.dma_start(out=out[row:row + 128, :],
                                 in_=s_natf[:, :]).then_inc(d_g.h, 16)
                    d_g.n += 16
                    assert d_g.n == outd[tt]

    if DEBUG_TAPS:
        taps = [("ag1_out", ag_out[0], [16384, 704], BF16),
                ("ag3_out", ag_out[2], [45056, 256], BF16),
                ("ago_in", ago_in, [1024, N], BF16),
                ("y1_scr", y1_scr, [NT, EMB], F32),
                ("scl_flat", scl_flat, [3, 22528], F32),
                ("rep_flat", rep_flat, [3, 22528], F32),
                ("ar_out", ar_out, [1, 4], F32)]
        dbg_outs = {nm: nc.dram_tensor("dbg_" + nm, shp, dtp,
                                       kind="ExternalOutput")
                    for nm, _, shp, dtp in taps}
        with nc.Block() as blk:
            @blk.gpsimd
            def _(g_):
                g_.wait_ge(d_g.h, d_g.n)
                for nm, src_t, shp, dtp in taps:
                    g_.dma_start(out=dbg_outs[nm][:, :],
                                 in_=src_t[:, :]).then_inc(d_g.h, 16)
                    d_g.n += 16
                g_.wait_ge(d_g.h, d_g.n)
    with nc.Block() as blk:
        @blk.gpsimd
        def _(g_):
            g_.wait_ge(d_g.h, d_g.n)

    fes.close()
    wes.close()
    ges.close()
    return nc


def kernel(**inputs):
    x = np.asarray(inputs["L0_attn_x"])
    wq = np.asarray(inputs["L0_attn_wq"])
    bq = np.asarray(inputs["L0_attn_bq"])
    wk = np.asarray(inputs["L0_attn_wk"])
    bk = np.asarray(inputs["L0_attn_bk"])
    wv = np.asarray(inputs["L0_attn_wv"])
    bv = np.asarray(inputs["L0_attn_bv"])
    wo = np.asarray(inputs["L0_attn_wo"])
    bo = np.asarray(inputs["L0_attn_bo"])
    w1 = np.asarray(inputs["L0_ffn_weight1"])
    w2 = np.asarray(inputs["L0_ffn_weight2"])
    w3 = np.asarray(inputs["L0_ffn_weight3"])

    ident = np.eye(128, dtype=bf16)
    cmask = np.triu(np.full((128, 128), -1e30, dtype=np.float32), k=1)

    in_maps = []
    for c in range(8):
        b, hh = c // 2, c % 2
        hs = slice(1024 * hh, 1024 * hh + 1024)
        in_maps.append({
            "xT": np.ascontiguousarray(x[b].T),
            "x_res": np.ascontiguousarray(x[b, 2048 * hh:2048 * hh + 2048, :]),
            "wqT": np.ascontiguousarray(wq[hs, :].T),
            "wkT": np.ascontiguousarray(wk[hs, :].T),
            "wvT": np.ascontiguousarray(wv[hs, :].T),
            "woT": np.ascontiguousarray(wo.T),
            "bqkv": np.ascontiguousarray(
                np.stack([bq[hs], bk[hs], bv[hs]], axis=1)),
            "bvfull": np.ascontiguousarray(bv.reshape(16, 128)),
            "bo_in": np.ascontiguousarray(bo.reshape(1, 2048)),
            "w1s": np.ascontiguousarray(w1[704 * c:704 * c + 704, :]),
            "w2s": np.ascontiguousarray(w2[704 * c:704 * c + 704, :]),
            "w3s": np.ascontiguousarray(w3[256 * c:256 * c + 256, :]),
            "identw": ident,
            "cmaskw": cmask,
            "offsw": np.array([[2048 * hh + 512 * qt for qt in range(4)]
                               + [0, 0, 0, 0]], dtype=np.int32),
        })

    nc = build_program()
    res = run_bass_kernel_spmd(nc, in_maps, core_ids=list(range(8)))
    outp = np.empty((4, 4096, 2048), dtype=np.float32)
    for c in range(8):
        b, hh = c // 2, c % 2
        outp[b, 2048 * hh:2048 * hh + 2048, :] = \
            np.asarray(res.results[c]["out"], dtype=np.float32)
    return outp


if __name__ == "__main__":
    p = build_program()
    n = sum(1 for _ in p.all_instructions())
    print("built, instructions:", n)


# version 08
# speedup vs baseline: 1.0007x; 1.0007x over previous
import numpy as np
import ml_dtypes
import concourse.bass as bass
import concourse.mybir as mybir
from concourse.bass_utils import run_bass_kernel_spmd
from contextlib import ExitStack

F32 = mybir.dt.float32
BF16 = mybir.dt.bfloat16
I32 = mybir.dt.int32
AF = mybir.ActivationFunctionType
ALU = mybir.AluOpType
AX = mybir.AxisListType
bf16 = ml_dtypes.bfloat16

EMB = 2048
N = 4096
NH = 8
DFF = 5632
NT = 2048
QSCALE = float(1.0 / np.sqrt(np.float32(128.0)))
ESHIFT = -12.0
NSCAL = 180224.0
LN_EPS = 1e-5

_NF4F = np.array([-1.0, -0.6962, -0.5251, -0.3949, -0.2844, -0.1848, -0.0911,
                  0.0, 0.0796, 0.1609, 0.2461, 0.3379, 0.4407, 0.5626, 0.723,
                  1.0], dtype=np.float32)
_NF4 = _NF4F.astype(bf16)
_LF = _NF4.astype(np.float32)
_DL = np.diff(_LF)


def _nf4_taus():
    def classify(x):
        v = np.array(x, dtype=bf16)
        d = np.abs((np.full(16, v, dtype=bf16) - _NF4).astype(bf16))
        return int(np.argmin(d.astype(np.float32)))
    taus = []
    for j in range(15):
        lo, hi = float(_LF[j]), float(_LF[j + 1])
        for _ in range(80):
            m = (lo + hi) / 2
            if classify(m) <= j:
                lo = m
            else:
                hi = m
        taus.append(np.float32((lo + hi) / 2))
    return np.array(taus, dtype=np.float32)


_TAUS = _nf4_taus()

DEBUG_TAPS = False
MAX_PHASE = 4
ATTN_HEADS = NH
ATTN_STOP = 'full'
ATTN_GROUPS = 16

WCFG = [(704, 2048, 0), (704, 2048, 1), (256, 5632, 2)]


def _segs(qc):
    kr = 128 * (qc + 1)
    return [(kb, min(512, kr - kb)) for kb in range(0, kr, 512)]


class Sem:
    def __init__(self, es, nc, name):
        self.h = es.enter_context(nc.semaphore(name))
        self.n = 0

    def inc(self, instr, k=1):
        instr.then_inc(self.h, k)
        self.n += k
        return self.n


def attn_schedule(v0, a0, m0, dg0):
    wdc, pj, pje, smm, cmk, expm, norm = {}, {}, {}, {}, {}, {}, {}
    ptm, pte, avm, oev, odma, oad = {}, {}, {}, {}, {}, {}
    v, a, m, dg = v0, a0, m0, dg0
    exp_seq = []
    for h in range(ATTN_HEADS):
        for qc in range(32):
            for sg in range(len(_segs(qc))):
                exp_seq.append((h, qc, sg))
    exp_pos = {k: i for i, k in enumerate(exp_seq)}
    for h in range(ATTN_HEADS):
        for w in range(3):
            v += 1
            wdc[(h, w)] = v
        if ATTN_STOP == 'proj':
            continue
        for g in range(ATTN_GROUPS):
            if ATTN_STOP == 'full' and g >= 1 and 2 * (g - 1) + 2 > 16:
                v += 1
                oad[(h, g - 1, 0)] = v
                v += 1
                oad[(h, g - 1, 1)] = v
                v += 1
                oev[(h, g - 1)] = ('v', v)
            for qc in (2 * g, 2 * g + 1):
                v += 1
                cmk[(h, qc)] = v
                v += 1
                norm[(h, qc)] = v
        g = ATTN_GROUPS - 1
        if ATTN_STOP == 'full' and 2 * g + 2 > 16:
            v += 1
            oad[(h, g, 0)] = v
            v += 1
            oad[(h, g, 1)] = v
            v += 1
            oev[(h, g)] = ('v', v)
    for h in range(ATTN_HEADS):
        for w in (1, 0):
            for tb in range(8):
                m += 1
                pj[(h, w, tb)] = m
        for kb4 in range(8):
            m += 1
            pj[(h, 2, kb4)] = m
        if ATTN_STOP == 'proj':
            continue
        for g in range(ATTN_GROUPS):
            qc0, qc1 = 2 * g, 2 * g + 1
            for qc in (qc0, qc1):
                for sg in range(len(_segs(qc))):
                    m += 1
                    smm[(h, qc, sg)] = m
            if ATTN_STOP == 'smax':
                continue
            nki = qc1 + 1
            for sub0 in range(0, nki, 16):
                sks = list(range(sub0, min(sub0 + 16, nki)))
                for ki in sks:
                    m += 1
                    ptm[(h, g, ki)] = m
                for ki in sks:
                    m += 1
                    avm[(h, g, ki)] = m
    for h in range(ATTN_HEADS):
        for w in (1, 0):
            for tb in range(8):
                a += 1
                pje[(h, w, tb)] = a
        for kb4 in range(8):
            a += 1
            pje[(h, 2, kb4)] = a
        if ATTN_STOP == 'proj':
            continue
        for g in range(ATTN_GROUPS):
            qc0, qc1 = 2 * g, 2 * g + 1
            for qc in (qc0, qc1):
                for sg in range(len(_segs(qc))):
                    a += 1
                    expm[(h, qc, sg)] = a
            if ATTN_STOP == 'smax':
                continue
            nki = qc1 + 1
            for ki in range(nki):
                a += 1
                pte[(h, g, ki)] = a
            if nki <= 16:
                a += 1
                oev[(h, g)] = ('a', a)
    for h in range(ATTN_HEADS):
        dg += 16
        odma[h] = dg
    return dict(wdc=wdc, pj=pj, pje=pje, smm=smm, cmk=cmk, expm=expm,
                norm=norm, ptm=ptm, pte=pte, avm=avm, oev=oev, odma=odma,
                oad=oad, exp_seq=exp_seq, exp_pos=exp_pos,
                v_end=v, a_end=a, m_end=m, dg_end=dg)


def build_program():
    nc = bass.Bass("TRN2", target_bir_lowering=False, debug=False,
                   detect_race_conditions=False)

    xT = nc.dram_tensor("xT", [EMB, N], F32, kind="ExternalInput")
    x_res = nc.dram_tensor("x_res", [NT, EMB], F32, kind="ExternalInput")
    wqT = nc.dram_tensor("wqT", [EMB, 1024], F32, kind="ExternalInput")
    wkT = nc.dram_tensor("wkT", [EMB, 1024], F32, kind="ExternalInput")
    wvT = nc.dram_tensor("wvT", [EMB, 1024], F32, kind="ExternalInput")
    woT = nc.dram_tensor("woT", [EMB, EMB], F32, kind="ExternalInput")
    bqkv = nc.dram_tensor("bqkv", [1024, 3], F32, kind="ExternalInput")
    bvfull = nc.dram_tensor("bvfull", [16, 128], F32, kind="ExternalInput")
    bo_in = nc.dram_tensor("bo_in", [1, EMB], F32, kind="ExternalInput")
    w1s = nc.dram_tensor("w1s", [704, 2048], BF16, kind="ExternalInput")
    w2s = nc.dram_tensor("w2s", [704, 2048], BF16, kind="ExternalInput")
    w3s = nc.dram_tensor("w3s", [256, 5632], BF16, kind="ExternalInput")
    identw = nc.dram_tensor("identw", [128, 128], BF16, kind="ExternalInput")
    cmaskw = nc.dram_tensor("cmaskw", [128, 128], F32, kind="ExternalInput")
    offsw = nc.dram_tensor("offsw", [1, 8], I32, kind="ExternalInput")
    out = nc.dram_tensor("out", [NT, EMB], F32, kind="ExternalOutput")

    scl_flat = nc.dram_tensor("scl_flat", [3, 22528], F32)
    rep_flat = nc.dram_tensor("rep_flat", [3, 22528], F32)
    ar_in = nc.dram_tensor("ar_in", [1, 4], F32)
    ar_out = nc.dram_tensor("ar_out", [1, 4], F32)
    wsrcs = [w1s, w2s, w3s]
    ag_in = [nc.dram_tensor("ag1_in", [2048, 704], BF16),
             nc.dram_tensor("ag2_in", [2048, 704], BF16),
             nc.dram_tensor("ag3_in", [5632, 256], BF16)]
    ag_out = [nc.dram_tensor("ag1_out", [16384, 704], BF16),
              nc.dram_tensor("ag2_out", [16384, 704], BF16),
              nc.dram_tensor("ag3_out", [45056, 256], BF16)]
    ago_in = nc.dram_tensor("ago_in", [1024, N], BF16)
    ago_out = [nc.dram_tensor(f"ago_out{k}", [512, N], BF16) for k in range(4)]
    y1_scr = nc.dram_tensor("y1_scr", [NT, EMB], F32)
    dbg_deqw = nc.dram_tensor("dbg_deqw", [128, DFF], BF16,
                              kind="ExternalOutput") if DEBUG_TAPS else None
    dbg_acc = nc.dram_tensor("dbg_acc", [128, DFF], F32,
                             kind="ExternalOutput") if DEBUG_TAPS else None

    WORLD = [list(range(8))]
    PAIRS = [[0, 1], [2, 3], [4, 5], [6, 7]]

    ges = ExitStack()

    def sb(stk, name, shape, dt=BF16):
        return stk.enter_context(nc.sbuf_tensor(name, shape, dt))

    s_ident = sb(ges, "s_ident", [128, 128], BF16)
    s_cmask = sb(ges, "s_cmask", [128, 128], F32)
    s_bqkv = sb(ges, "s_bqkv", [128, 24], F32)
    s_bqs = sb(ges, "s_bqs", [128, 8], F32)
    s_offs = ges.enter_context(nc.sbuf_tensor("s_offs", [1, 8], I32))
    s_zb = sb(ges, "s_zb", [128, 16], F32)
    s_esh = sb(ges, "s_esh", [128, 1], F32)
    s_lneps = sb(ges, "s_lneps", [128, 1], F32)
    s_sd = sb(ges, "s_sd", [128, 1], F32)
    s_z = sb(ges, "s_z", [128, 1], F32)
    s_zr = sb(ges, "s_zr", [128, 1], F32)
    s_m1 = sb(ges, "s_m1", [128, 1], F32)
    s_v1 = sb(ges, "s_v1", [128, 1], F32)
    s_r1 = sb(ges, "s_r1", [128, 1], F32)

    p_s = [ges.enter_context(nc.psum_tensor(f"p_s{i}", [128, 512], F32))
           for i in range(4)]
    p_pt = ges.enter_context(nc.psum_tensor("p_pt", [128, 512], F32))
    p_o = ges.enter_context(nc.psum_tensor("p_o", [128, 512], F32))
    p_pj = [ges.enter_context(nc.psum_tensor(f"p_pj{i}", [128, 512], F32))
            for i in range(2)]

    mk = lambda name: Sem(ges, nc, name)
    d_x = mk("d_x")
    d_g = mk("d_g")
    d_c = mk("d_c")
    d_wr = [mk(f"d_wr{i}") for i in range(3)]
    d_xl = mk("d_xl")
    d_xl2 = mk("d_xl2")
    d_wh = mk("d_wh")
    d_wo = mk("d_wo")
    d_xr = mk("d_xr")
    d_sl = [mk(f"d_sl{i}") for i in range(2)]
    d_w3p = [mk(f"d_w3p{i}") for i in range(2)]
    d_res = mk("d_res")
    v_c = mk("v_c")
    a_c = mk("a_c")
    m_c = mk("m_c")
    c_ar = mk("c_ar")
    c_ag = [mk(f"c_ag{i}") for i in range(3)]
    c_ao = [mk(f"c_ao{i}") for i in range(4)]

    f_v = mk("f_v")

    def vf(v, instr):
        f_v.inc(instr, 1)
        v.wait_ge(f_v.h, f_v.n)
        return instr

    with nc.Block() as blk:
        @blk.sync
        def _(sync):
            sync.dma_start(out=s_ident[:, :], in_=identw[:, :]).then_inc(d_c.h, 16)
            sync.dma_start(out=s_cmask[:, :], in_=cmaskw[:, :]).then_inc(d_c.h, 16)
            sync.dma_start(
                out=bass.AP(s_bqkv, 0, [[24, 128], [3, 8], [1, 3]]),
                in_=bass.AP(bqkv, 0, [[3, 128], [384, 8], [1, 3]]))\
                .then_inc(d_c.h, 16)
            sync.dma_start(out=s_offs[:, :], in_=offsw[:, :]).then_inc(d_c.h, 16)
            d_c.n += 64

        @blk.vector
        def _(v):
            v.wait_ge(d_c.h, d_c.n)
            v.memset(s_esh[:, :], ESHIFT)
            v.memset(s_lneps[:, :], LN_EPS)
            v.tensor_scalar(s_bqs[:, :],
                            bass.AP(s_bqkv, 0, [[24, 128], [3, 8]]),
                            QSCALE, None, ALU.mult)

    des = ExitStack()
    s_wraw = sb(des, "s_wraw", [128, 12288], BF16)
    s_s1nat = sb(des, "s_s1nat", [128, 192], F32)
    s_repnat = sb(des, "s_repnat", [128, 192], F32)
    s_thr = sb(des, "s_thr", [128, 1320], F32)
    s_wf32 = sb(des, "s_wf32", [128, DFF], F32)
    s_tmp = sb(des, "s_tmp", [128, DFF], F32)
    s_acc = sb(des, "s_acc", [128, DFF], F32)
    s_deqw = sb(des, "s_deqw", [128, DFF], BF16)
    s_slab = sb(des, "s_slab", [128, 11264], BF16)
    s_s1sum = sb(des, "s_s1sum", [128, 1], F32)
    s_ones = sb(des, "s_ones", [128, 1], F32)
    s_means = sb(des, "s_means", [1, 4], F32)
    s_gmean = sb(des, "s_gmean", [1, 4], F32)
    s_k88 = sb(des, "s_k88", [1, 88], BF16)
    s_mean1 = sb(des, "s_mean1", [1, 1], F32)
    s_mb1 = sb(des, "s_mb1", [1, 1], BF16)
    s_meanf = sb(des, "s_meanf", [88, 1], F32)
    s_meanb = sb(des, "s_meanb", [88, 1], BF16)
    s_meanbf = sb(des, "s_meanbf", [88, 1], F32)
    s_s1g = sb(des, "s_s1g", [88, 256], F32)
    s_cb = sb(des, "s_cb", [88, 256], BF16)
    s_cf = sb(des, "s_cf", [88, 256], F32)
    s_gabs = sb(des, "s_gabs", [88, 1], F32)
    s_twob = sb(des, "s_twob", [88, 1], BF16)
    s_twof = sb(des, "s_twof", [88, 1], F32)
    s_rqf = sb(des, "s_rqf", [88, 1], F32)
    s_c256 = sb(des, "s_c256", [88, 1], F32)
    s_qfb = sb(des, "s_qfb", [88, 1], BF16)
    s_qff = sb(des, "s_qff", [88, 1], F32)
    s_prodb = sb(des, "s_prodb", [88, 256], BF16)
    s_qi = des.enter_context(nc.sbuf_tensor("s_qi", [88, 256], I32))
    s_qc = sb(des, "s_qc", [88, 256], F32)
    s_dqb = sb(des, "s_dqb", [88, 256], BF16)
    s_repb = sb(des, "s_repb", [88, 256], BF16)
    s_repf = sb(des, "s_repf", [88, 256], F32)

    with nc.Block() as blk:
        ld0 = d_x.n
        v0, a0, m0, g0 = v_c.n, a_c.n, m_c.n, d_g.n
        vsum = {wi: v0 + wi + 1 for _, _, wi in WCFG}

        @blk.sync
        def _(sync):
            for F, D, wi in WCFG:
                if wi > 0:
                    sync.wait_ge(v_c.h, vsum[wi - 1])
                for ft in range((F + 127) // 128):
                    rows = min(128, F - 128 * ft)
                    sync.dma_start(out=s_wraw[0:rows, D * ft:D * ft + D],
                                   in_=wsrcs[wi][128 * ft:128 * ft + rows, :])\
                        .then_inc(d_wr[wi].h, 16)
                    d_wr[wi].n += 16

        @blk.vector
        def _(v):
            v.memset(s_ones[:, :], 1.0)
            v.memset(s_k88[:, :], 1.0)
            v.memset(s_c256[:, :], 256.0)
            v.memset(s_means[:, :], 0.0)
            dmp = g0
            for F, D, wi in WCFG:
                nblk = D // 64
                natc = 32 if wi < 2 else 88
                nft = (F + 127) // 128
                if wi > 0:
                    v.wait_ge(d_g.h, dmp)
                dmp += 16 * nft
                v.memset(s_s1nat[:, :], 0.0)
                v.wait_ge(d_wr[wi].h, 16 * nft)
                for ft in range(nft):
                    rows = min(128, F - 128 * ft)
                    v.tensor_reduce(
                        s_s1nat[0:rows, natc * ft:natc * ft + nblk],
                        bass.AP(s_wraw, D * ft,
                                [[12288, rows], [64, nblk], [1, 64]]),
                        AX.X, ALU.max, apply_absolute_value=True)
                v.tensor_reduce(s_s1sum[:, :], s_s1nat[:, 0:natc * nft],
                                AX.X, ALU.add)
                v_c.inc(v.tensor_copy(s_tmp[0:128, 0:1], s_s1sum[:, 0:1]))
                assert v_c.n == vsum[wi]

        @blk.tensor
        def _(t):
            for F, D, wi in WCFG:
                t.wait_ge(v_c.h, vsum[wi])
                m_c.inc(t.matmul(p_pj[0][0:1, wi:wi + 1], s_s1sum[:, :],
                                 s_ones[:, :], start=True, stop=True))

        @blk.scalar
        def _(a):
            a.wait_ge(m_c.h, m_c.n)
            a_c.inc(a.copy(s_means[0:1, 0:3], p_pj[0][0:1, 0:3]))

        @blk.gpsimd
        def _(g):
            for F, D, wi in WCFG:
                g.wait_ge(v_c.h, vsum[wi])
                nblk = D // 64
                natc = 32 if wi < 2 else 88
                for ft in range((F + 127) // 128):
                    rows = min(128, F - 128 * ft)
                    g.dma_start(
                        out=bass.AP(scl_flat, 22528 * wi + 128 * nblk * ft,
                                    [[nblk, rows], [1, nblk]]),
                        in_=s_s1nat[0:rows, natc * ft:natc * ft + nblk])\
                        .then_inc(d_g.h, 16)
                    d_g.n += 16
            g.wait_ge(a_c.h, a_c.n)
            g.dma_start(out=ar_in[:, :], in_=s_means[:, :]).then_inc(d_g.h, 16)
            d_g.n += 16
            g.wait_ge(d_g.h, d_g.n)
            g.collective_compute(
                "AllReduce", ALU.add, replica_groups=WORLD,
                ins=[ar_in.ap().opt()], outs=[ar_out.ap().opt()])\
                .then_inc(c_ar.h, 1)
            g.wait_ge(c_ar.h, 1)
            g.dma_start(out=s_gmean[:, :], in_=ar_out[:, :]).then_inc(d_g.h, 16)
            d_g.n += 16

    for F, D, wi in WCFG:
        nblk = D // 64
        nft = (F + 127) // 128
        natc = 32 if wi < 2 else 88
        ndc = D // 128

        with nc.Block() as blk:
            v_mb = v_c.n + 1
            m_mb = m_c.n + 1
            a_mb = a_c.n + 1

            @blk.gpsimd
            def _(g, wi=wi):
                g.wait_ge(d_g.h, d_g.n)
                g.dma_start(
                    out=s_s1g[:, :],
                    in_=bass.AP(scl_flat, 22528 * wi, [[256, 88], [1, 256]]))\
                    .then_inc(d_g.h, 16)
                d_g.n += 16

            @blk.tensor
            def _(t, wi=wi, v_mb=v_mb):
                t.wait_ge(v_c.h, v_mb)
                m_c.inc(t.matmul(p_pj[1][0:88, 0:1], s_k88[:, :],
                                 s_mb1[:, :], start=True, stop=True))

            @blk.scalar
            def _(a, m_mb=m_mb):
                a.wait_ge(m_c.h, m_mb)
                a.copy(s_meanf[:, :], p_pj[1][0:88, 0:1])
                a_c.inc(a.copy(s_meanb[:, :], s_meanf[:, :]))

            @blk.vector
            def _(v, wi=wi, a_mb=a_mb):
                v.wait_ge(d_g.h, d_g.n)
                vf(v, v.tensor_scalar(s_mean1[:, :], s_gmean[0:1, wi:wi + 1],
                                      1.0 / NSCAL, None, ALU.mult))
                v_c.inc(v.tensor_copy(s_mb1[:, :], s_mean1[:, :]))
                v.wait_ge(a_c.h, a_mb)
                vf(v, v.tensor_copy(s_meanbf[:, :], s_meanb[:, :]))
                v.tensor_scalar(s_cb[:, :], s_s1g[:, :], s_meanbf[:, :], None,
                                ALU.subtract)
                v.tensor_copy(s_cf[:, :], s_cb[:, :])
                vf(v, v.tensor_reduce(s_gabs[:, :], s_cf[:, :], AX.X, ALU.max,
                                      apply_absolute_value=True))
                vf(v, v.tensor_scalar(s_twob[:, :], s_gabs[:, :], 2.0, None,
                                      ALU.mult))
                vf(v, v.tensor_copy(s_twof[:, :], s_twob[:, :]))
                vf(v, v.reciprocal(s_rqf[:, :], s_twof[:, :]))
                vf(v, v.tensor_scalar(s_qfb[:, :], s_rqf[:, :], 256.0, None,
                                      ALU.mult))
                vf(v, v.tensor_copy(s_qff[:, :], s_qfb[:, :]))
                vf(v, v.reciprocal(s_rqf[:, :], s_qff[:, :]))
                v.tensor_scalar(s_prodb[:, :], s_cf[:, :], s_qff[:, :], None,
                                ALU.mult)
                v.tensor_copy(s_qi[:, :], s_prodb[:, :])
                v.tensor_copy(s_qc[:, :], s_qi[:, :])
                v.tensor_scalar_min(s_qc[:, :], s_qc[:, :], 127.0)
                v.tensor_scalar_max(s_qc[:, :], s_qc[:, :], -128.0)
                v.tensor_scalar(s_dqb[:, :], s_qc[:, :], s_rqf[:, :], None,
                                ALU.mult)
                v.tensor_scalar(s_repb[:, :], s_dqb[:, :], s_meanbf[:, :],
                                None, ALU.add)
                v_c.inc(v.tensor_copy(s_repf[:, :], s_repb[:, :]))

            @blk.gpsimd
            def _(g, wi=wi, nblk=nblk, nft=nft, F=F, natc=natc):
                g.wait_ge(v_c.h, v_c.n)
                g.dma_start(
                    out=bass.AP(rep_flat, 22528 * wi, [[256, 88], [1, 256]]),
                    in_=s_repf[:, :]).then_inc(d_g.h, 16)
                d_g.n += 16
                g.wait_ge(d_g.h, d_g.n)
                for ft in range(nft):
                    rows = min(128, F - 128 * ft)
                    g.dma_start(
                        out=s_repnat[0:rows, natc * ft:natc * ft + nblk],
                        in_=bass.AP(rep_flat, 22528 * wi + 128 * nblk * ft,
                                    [[nblk, rows], [1, nblk]]))\
                        .then_inc(d_g.h, 16)
                    d_g.n += 16
                    g.dma_start(
                        out=s_s1nat[0:rows, natc * ft:natc * ft + nblk],
                        in_=bass.AP(scl_flat, 22528 * wi + 128 * nblk * ft,
                                    [[nblk, rows], [1, nblk]]))\
                        .then_inc(d_g.h, 16)
                    d_g.n += 16

        with nc.Block() as blk:
            v0, a0, m0 = v_c.n, a_c.n, m_c.n
            wr_tot = d_wr[wi].n + 16 * nft

            @blk.sync
            def _(sync, wi=wi, F=F, D=D, nft=nft):
                for ft in range(nft):
                    rows = min(128, F - 128 * ft)
                    sync.dma_start(out=s_wraw[0:rows, D * ft:D * ft + D],
                                   in_=wsrcs[wi][128 * ft:128 * ft + rows, :])\
                        .then_inc(d_wr[wi].h, 16)
                    d_wr[wi].n += 16

            grps_per_ft = (ndc + 3) // 4
            vm = {ft: v0 + ft + 1 for ft in range(nft)}
            tm = {i: m0 + i + 1 for i in range(nft * grps_per_ft)}
            am = {i: a0 + i + 1 for i in range(nft * grps_per_ft)}

            @blk.vector
            def _(v, wi=wi, nblk=nblk, nft=nft, F=F, D=D, natc=natc,
                  wr_tot=wr_tot):
                v.wait_ge(d_g.h, d_g.n)
                v.wait_ge(d_wr[wi].h, wr_tot)
                for ft in range(nft):
                    rows = min(128, F - 128 * ft)
                    for j in range(15):
                        ti = v.tensor_scalar(
                            s_thr[0:rows, nblk * j:nblk * j + nblk],
                            s_s1nat[0:rows, natc * ft:natc * ft + nblk],
                            float(_TAUS[j]), None, ALU.mult)
                    vf(v, ti)
                    v.tensor_copy(s_wf32[0:rows, 0:D],
                                  s_wraw[0:rows, D * ft:D * ft + D])
                    v.memset(s_acc[0:rows, 0:D], float(_LF[0]))
                    for j in range(15):
                        v.tensor_tensor(
                            s_tmp[0:rows, 0:D], s_wf32[0:rows, 0:D],
                            bass.AP(s_thr, nblk * j,
                                    [[1320, rows], [1, nblk], [0, 64]]),
                            ALU.is_gt)
                        v.scalar_tensor_tensor(
                            s_acc[0:rows, 0:D], s_tmp[0:rows, 0:D],
                            float(_DL[j]), s_acc[0:rows, 0:D],
                            ALU.mult, ALU.add)
                    v_c.inc(v.tensor_tensor(
                        s_deqw[0:rows, 0:D], s_acc[0:rows, 0:D],
                        bass.AP(s_repnat, natc * ft,
                                [[192, rows], [1, nblk], [0, 64]]),
                        ALU.mult))

            @blk.tensor
            def _(t, nft=nft, F=F, D=D, ndc=ndc, grps_per_ft=grps_per_ft):
                grp = 0
                for ft in range(nft):
                    rows = min(128, F - 128 * ft)
                    t.wait_ge(v_c.h, vm[ft])
                    for dc0 in range(0, ndc, 4):
                        if grp >= 2:
                            t.wait_ge(a_c.h, am[grp - 2])
                        bk = [p_pt, p_s[0]][grp % 2]
                        nd = min(4, ndc - dc0)
                        for i in range(nd):
                            mm = t.matmul(
                                bk[:, 128 * i:128 * i + rows],
                                s_deqw[0:rows,
                                       128 * (dc0 + i):128 * (dc0 + i) + 128],
                                s_ident[0:rows, 0:rows],
                                start=True, stop=True)
                            if i == nd - 1:
                                m_c.inc(mm)
                        assert m_c.n == tm[grp]
                        grp += 1

            @blk.scalar
            def _(a, wi=wi, nft=nft, F=F, D=D, ndc=ndc,
                  grps_per_ft=grps_per_ft):
                stride = 704 if wi < 2 else 256
                grp = 0
                for ft in range(nft):
                    rows = min(128, F - 128 * ft)
                    for dc0 in range(0, ndc, 4):
                        a.wait_ge(m_c.h, tm[grp])
                        bk = [p_pt, p_s[0]][grp % 2]
                        nd = min(4, ndc - dc0)
                        a_c.inc(a.copy(
                            bass.AP(s_slab, stride * dc0 + 128 * ft,
                                    [[11264, 128], [stride, nd], [1, rows]]),
                            bass.AP(bk, 0, [[512, 128], [128, nd],
                                            [1, rows]])))
                        assert a_c.n == am[grp]
                        grp += 1

            @blk.gpsimd
            def _(g, wi=wi, D=D, vm=vm):
                if DEBUG_TAPS and wi == 0:
                    g.wait_ge(v_c.h, vm[0])
                    g.dma_start(out=dbg_deqw[:, 0:2048],
                                in_=s_deqw[:, 0:2048]).then_inc(d_g.h, 16)
                    d_g.n += 16
                    g.dma_start(out=dbg_acc[:, 0:2048],
                                in_=s_acc[:, 0:2048]).then_inc(d_g.h, 16)
                    d_g.n += 16
                g.wait_ge(a_c.h, a_c.n)
                stride = 704 if wi < 2 else 256
                for dc in range(D // 128):
                    g.dma_start(
                        out=ag_in[wi][128 * dc:128 * dc + 128, 0:stride],
                        in_=s_slab[:, stride * dc:stride * dc + stride])\
                        .then_inc(d_g.h, 16)
                    d_g.n += 16

    with nc.Block() as blk:
        @blk.gpsimd
        def _(g):
            for i in range(3):
                g.collective_compute(
                    "AllGather", ALU.bypass, replica_groups=WORLD,
                    ins=[ag_in[i].ap().opt()], outs=[ag_out[i].ap().opt()])\
                    .then_inc(c_ag[i].h, 1)

    des.close()

    if MAX_PHASE < 2:
        with nc.Block() as blk:
            @blk.gpsimd
            def _(g_):
                for i in range(3):
                    g_.wait_ge(c_ag[i].h, 1)
                g_.dma_start(out=out[:, :], in_=y1_scr[:, :])\
                    .then_inc(d_g.h, 16)
                d_g.n += 16
                g_.wait_ge(d_g.h, d_g.n)
        ges.close()
        return nc
    aes = ExitStack()
    s_xT = sb(aes, "s_xT", [128, 16 * N], BF16)
    s_P = sb(aes, "s_P", [128, 3 * N], BF16)
    s_kT = sb(aes, "s_kT", [128, N], BF16)
    s_qT = sb(aes, "s_qT", [128, N], BF16)
    s_v = sb(aes, "s_v", [128, N], BF16)
    s_wh = sb(aes, "s_wh", [128, 3 * EMB], BF16)
    s_stg = sb(aes, "s_stg", [128, 2048], F32)
    s_pt = sb(aes, "s_pt", [128, 4096], BF16)
    s_oacc = sb(aes, "s_oacc", [128, 256], F32)
    OT0 = 2 * N

    with nc.Block() as blk:
        v0 = v_c.n
        xm = {i: v0 + i + 1 for i in range(64)}

        @blk.sync
        def _(sync):
            for i in range(64):
                k, qf = i // 4, i % 4
                if i >= 2:
                    sync.wait_ge(v_c.h, xm[i - 2])
                sem = d_xl if i % 2 == 0 else d_xl2
                sync.dma_start(out=s_stg[:, 1024 * (i % 2):
                                         1024 * (i % 2) + 1024],
                               in_=xT[128 * k:128 * k + 128,
                                      1024 * qf:1024 * qf + 1024])\
                    .then_inc(sem.h, 16)
                sem.n += 16

        @blk.vector
        def _(v):
            for i in range(64):
                k, qf = i // 4, i % 4
                sem = d_xl if i % 2 == 0 else d_xl2
                v.wait_ge(sem.h, 16 * (i // 2 + 1))
                v_c.inc(v.tensor_copy(
                    s_xT[:, N * k + 1024 * qf:N * k + 1024 * qf + 1024],
                    s_stg[:, 1024 * (i % 2):1024 * (i % 2) + 1024]))

    if MAX_PHASE < 2.5:
        with nc.Block() as blk:
            @blk.gpsimd
            def _(g_):
                for i in range(3):
                    g_.wait_ge(c_ag[i].h, 1)
                g_.dma_start(out=out[:, :], in_=y1_scr[:, :])\
                    .then_inc(d_g.h, 16)
                d_g.n += 16
                g_.wait_ge(d_g.h, d_g.n)
        aes.close()
        ges.close()
        return nc

    with nc.Block() as blk:
        SC = attn_schedule(v_c.n, a_c.n, m_c.n, d_g.n)
        wdc, pj, pje = SC["wdc"], SC["pj"], SC["pje"]
        smm, cmk, expm, norm = SC["smm"], SC["cmk"], SC["expm"], SC["norm"]
        ptm, pte, avm, oev = SC["ptm"], SC["pte"], SC["avm"], SC["oev"]
        odma, oad = SC["odma"], SC["oad"]
        exp_seq, exp_pos = SC["exp_seq"], SC["exp_pos"]

        def wait_oev(eng, key):
            kind, val = oev[key]
            eng.wait_ge(a_c.h if kind == 'a' else v_c.h, val)

        @blk.sync
        def _(sync):
            wts = [wqT, wkT, wvT]
            for h in range(ATTN_HEADS):
                for w in range(3):
                    idx = h * 3 + w
                    if idx >= 1:
                        pidx = idx - 1
                        sync.wait_ge(v_c.h, wdc[(pidx // 3, pidx % 3)])
                    sync.dma_start(
                        out=bass.AP(s_stg, 0,
                                    [[2048, 128], [128, 16], [1, 128]]),
                        in_=bass.AP(wts[w], 128 * h,
                                    [[1024, 128], [131072, 16], [1, 128]]))\
                        .then_inc(d_wh.h, 16)
                    d_wh.n += 16

        @blk.vector
        def _(v):
            for h in range(ATTN_HEADS):
                for w in range(3):
                    idx = h * 3 + w
                    v.wait_ge(d_wh.h, 16 * (idx + 1))
                    if h > 0:
                        v.wait_ge(m_c.h, pj[(h - 1, 2, 7)])
                    v_c.inc(v.tensor_copy(
                        s_wh[:, EMB * w:EMB * w + EMB],
                        s_stg[:, :]))
                    assert v_c.n == wdc[(h, w)]
                if ATTN_STOP == 'proj':
                    continue

                def emit_oad(g):
                    nki = 2 * g + 2
                    v.wait_ge(m_c.h, avm[(h, g, 15)])
                    v_c.inc(v.tensor_copy(s_oacc[:, :], p_o[:, 0:256]))
                    assert v_c.n == oad[(h, g, 0)]
                    v.wait_ge(m_c.h, avm[(h, g, nki - 1)])
                    v_c.inc(v.tensor_tensor(s_oacc[:, :], s_oacc[:, :],
                                            p_o[:, 0:256], ALU.add))
                    assert v_c.n == oad[(h, g, 1)]
                    v_c.inc(v.tensor_copy(
                        s_P[:, OT0 + 256 * g:OT0 + 256 * g + 256],
                        s_oacc[:, :]))
                    assert ('v', v_c.n) == oev[(h, g)]

                for g in range(ATTN_GROUPS):
                    if (ATTN_STOP == 'full' and g >= 1
                            and 2 * (g - 1) + 2 > 16):
                        emit_oad(g - 1)
                    for qc in (2 * g, 2 * g + 1):
                        sgs = _segs(qc)
                        lsg = len(sgs) - 1
                        kb, wd = sgs[lsg]
                        bank = p_s[exp_pos[(h, qc, lsg)] % 4]
                        v.wait_ge(m_c.h, smm[(h, qc, lsg)])
                        v_c.inc(v.tensor_tensor(bank[0:128, wd - 128:wd],
                                                bank[0:128, wd - 128:wd],
                                                s_cmask[:, :], ALU.add))
                        assert v_c.n == cmk[(h, qc)]
                        v.wait_ge(a_c.h, expm[(h, qc, lsg)])
                        zoff = 8 * (qc % 2)
                        vf(v, v.tensor_reduce(s_z[:, :],
                                              s_zb[:, zoff:zoff + len(sgs)],
                                              AX.X, ALU.add))
                        vf(v, v.reciprocal(s_zr[:, :], s_z[:, :]))
                        slot = qc % 2
                        kr = 128 * (qc + 1)
                        v_c.inc(v.tensor_scalar(
                            s_P[:, N * slot:N * slot + kr],
                            s_P[:, N * slot:N * slot + kr],
                            s_zr[:, :], None, ALU.mult))
                        assert v_c.n == norm[(h, qc)]
                if ATTN_STOP == 'smax':
                    continue
                if 2 * (ATTN_GROUPS - 1) + 2 > 16:
                    emit_oad(ATTN_GROUPS - 1)

        @blk.tensor
        def _(t):
            pjseq = []
            for h in range(ATTN_HEADS):
                for w in (1, 0):
                    for tb in range(8):
                        pjseq.append((h, w, tb))
                for kb4 in range(8):
                    pjseq.append((h, 2, kb4))
            pjpos = {k: i for i, k in enumerate(pjseq)}

            for h in range(ATTN_HEADS):
                for w, dst in ((1, s_kT), (0, s_qT)):
                    t.wait_ge(v_c.h, wdc[(h, w)])
                    for tb in range(8):
                        key = (h, w, tb)
                        si = pjpos[key]
                        if si >= 2:
                            t.wait_ge(a_c.h, pje[pjseq[si - 2]])
                        for di in range(16):
                            mm = t.matmul(
                                p_pj[si % 2][:, :],
                                s_wh[:, EMB * w + 128 * di:
                                     EMB * w + 128 * di + 128],
                                s_xT[:, N * di + 512 * tb:
                                     N * di + 512 * tb + 512],
                                start=(di == 0), stop=(di == 15))
                        m_c.inc(mm)
                        assert m_c.n == pj[key]
                t.wait_ge(v_c.h, wdc[(h, 2)])
                for kb4 in range(8):
                    key = (h, 2, kb4)
                    si = pjpos[key]
                    if si >= 2:
                        t.wait_ge(a_c.h, pje[pjseq[si - 2]])
                    for sub in range(4):
                        ki = 4 * kb4 + sub
                        for di in range(16):
                            mm = t.matmul(
                                p_pj[si % 2][:, 128 * sub:128 * sub + 128],
                                s_xT[:, N * di + 128 * ki:
                                     N * di + 128 * ki + 128],
                                s_wh[:, 2 * EMB + 128 * di:
                                     2 * EMB + 128 * di + 128],
                                start=(di == 0), stop=(di == 15))
                    m_c.inc(mm)
                    assert m_c.n == pj[key]
                if ATTN_STOP == 'proj':
                    continue
                t.wait_ge(a_c.h, pje[(h, 2, 7)])
                for g in range(ATTN_GROUPS):
                    qc0, qc1 = 2 * g, 2 * g + 1
                    for qc in (qc0, qc1):
                        for sg, (kb, wd) in enumerate(_segs(qc)):
                            gi = exp_pos[(h, qc, sg)]
                            if gi >= 4:
                                t.wait_ge(a_c.h, expm[exp_seq[gi - 4]])
                            bank = p_s[gi % 4]
                            m_c.inc(t.matmul(
                                bank[0:128, 0:wd],
                                s_qT[:, 128 * qc:128 * qc + 128],
                                s_kT[:, kb:kb + wd],
                                start=True, stop=True))
                            assert m_c.n == smm[(h, qc, sg)]
                    if ATTN_STOP == 'smax':
                        continue
                    nki = qc1 + 1
                    t.wait_ge(v_c.h, norm[(h, qc1)])
                    for sub0 in range(0, nki, 16):
                        sks = list(range(sub0, min(sub0 + 16, nki)))
                        for ki in sks:
                            both = ki <= qc0
                            if ki >= 2:
                                t.wait_ge(a_c.h, pte[(h, g, ki - 2)])
                            bk = p_pj[ki % 2]
                            if both:
                                t.matmul(bk[:, 0:128],
                                         s_P[:, 128 * ki:128 * ki + 128],
                                         s_ident[:, :],
                                         start=True, stop=True)
                            m_c.inc(t.matmul(
                                bk[:, 128:256],
                                s_P[:, N + 128 * ki:N + 128 * ki + 128],
                                s_ident[:, :], start=True, stop=True))
                            assert m_c.n == ptm[(h, g, ki)]
                        t.wait_ge(a_c.h, pte[(h, g, sks[-1])])
                        if sub0 == 0:
                            if (h, g) != (0, 0):
                                prev = (h, g - 1) if g > 0 else (h - 1, ATTN_GROUPS - 1)
                                wait_oev(t, prev)
                        else:
                            t.wait_ge(v_c.h, oad[(h, g, 0)])
                        for ki in sks:
                            both = ki <= qc0
                            slot = 256 * (ki % 16)
                            if both:
                                mm = t.matmul(
                                    p_o[:, 0:256],
                                    s_v[:, 128 * ki:128 * ki + 128],
                                    s_pt[:, slot:slot + 256],
                                    start=(ki == sks[0]),
                                    stop=(ki == sks[-1]))
                            else:
                                mm = t.matmul(
                                    p_o[:, 128:256],
                                    s_v[:, 128 * ki:128 * ki + 128],
                                    s_pt[:, slot + 128:slot + 256],
                                    start=(ki == sks[0]),
                                    stop=(ki == sks[-1]))
                            m_c.inc(mm)
                            assert m_c.n == avm[(h, g, ki)]

        @blk.scalar
        def _(a):
            for h in range(ATTN_HEADS):
                for w, dst in ((1, s_kT), (0, s_qT)):
                    for tb in range(8):
                        key = (h, w, tb)
                        a.wait_ge(m_c.h, pj[key])
                        if w == 0:
                            bias, scale = s_bqs[:, h:h + 1], QSCALE
                        else:
                            bias, scale = s_bqkv[:, 3 * h + 1:3 * h + 2], 1.0
                        si = h * 24 + (0 if w == 1 else 8) + tb
                        a_c.inc(a.activation(
                            dst[:, 512 * tb:512 * tb + 512],
                            p_pj[si % 2][:, :],
                            AF.Identity, bias=bias, scale=scale))
                        assert a_c.n == pje[key]
                for kb4 in range(8):
                    key = (h, 2, kb4)
                    a.wait_ge(m_c.h, pj[key])
                    a_c.inc(a.copy(s_v[:, 512 * kb4:512 * kb4 + 512],
                                   p_pj[(h * 24 + 16 + kb4) % 2][:, :]))
                    assert a_c.n == pje[key]
                if ATTN_STOP == 'proj':
                    continue
                for g in range(ATTN_GROUPS):
                    qc0, qc1 = 2 * g, 2 * g + 1
                    for qc in (qc0, qc1):
                        sgs = _segs(qc)
                        lsg = len(sgs) - 1
                        slot = qc % 2
                        if qc >= 2:
                            a.wait_ge(v_c.h, norm[(h, qc - 2)])
                        elif h > 0:
                            a.wait_ge(v_c.h, norm[(h - 1, 30 + qc)])
                        if ATTN_STOP == 'full':
                            if qc >= 2:
                                gp = (qc - 2) // 2
                                a.wait_ge(m_c.h, ptm[(h, gp, 2 * gp + 1)])
                            elif h > 0:
                                a.wait_ge(m_c.h, ptm[(h - 1, 15, 31)])
                        for sg, (kb, wd) in enumerate(sgs):
                            bank = p_s[exp_pos[(h, qc, sg)] % 4]
                            if sg == lsg:
                                a.wait_ge(v_c.h, cmk[(h, qc)])
                            else:
                                a.wait_ge(m_c.h, smm[(h, qc, sg)])
                            a_c.inc(a.activation(
                                s_P[:, N * slot + kb:N * slot + kb + wd],
                                bank[0:128, 0:wd], AF.Exp,
                                bias=s_esh[:, :], scale=1.0,
                                accum_out=s_zb[:, 8 * (qc % 2) + sg:
                                               8 * (qc % 2) + sg + 1]))
                            assert a_c.n == expm[(h, qc, sg)]
                    if ATTN_STOP == 'smax':
                        continue
                    nki = qc1 + 1
                    for ki in range(nki):
                        both = ki <= qc0
                        a.wait_ge(m_c.h, ptm[(h, g, ki)])
                        if ki >= 16:
                            a.wait_ge(m_c.h, avm[(h, g, ki - 16)])
                        bk = p_pj[ki % 2]
                        slot = 256 * (ki % 16)
                        lo = 0 if both else 128
                        a_c.inc(a.copy(s_pt[:, slot + lo:slot + 256],
                                       bk[:, lo:256]))
                        assert a_c.n == pte[(h, g, ki)]
                    if nki <= 16:
                        a.wait_ge(m_c.h, avm[(h, g, nki - 1)])
                        if g == 0 and h > 0:
                            a.wait_ge(d_g.h, odma[h - 1])
                        a_c.inc(a.copy(
                            s_P[:, OT0 + 256 * g:OT0 + 256 * g + 256],
                            p_o[:, 0:256]))
                        assert ('a', a_c.n) == oev[(h, g)]

        @blk.gpsimd
        def _(g_):
            if ATTN_STOP != 'full':
                return
            for h in range(ATTN_HEADS):
                wait_oev(g_, (h, ATTN_GROUPS - 1))
                g_.dma_start(out=ago_in[128 * h:128 * h + 128, :],
                             in_=s_P[:, OT0:OT0 + N]).then_inc(d_g.h, 16)
                d_g.n += 16
                assert d_g.n == odma[h]
                if h % 2 == 1:
                    g_.wait_ge(d_g.h, d_g.n)
                    k = h // 2
                    g_.collective_compute(
                        "AllGather", ALU.bypass, replica_groups=PAIRS,
                        ins=[ago_in[256 * k:256 * k + 256, :].opt()],
                        outs=[ago_out[k].ap().opt()])\
                        .then_inc(c_ao[k].h, 1)

    aes.close()

    if MAX_PHASE < 3:
        with nc.Block() as blk:
            @blk.gpsimd
            def _(g_):
                for k in range(ATTN_HEADS // 2):
                    g_.wait_ge(c_ao[k].h, 1)
                g_.dma_start(out=out[:, :], in_=y1_scr[:, :])\
                    .then_inc(d_g.h, 16)
                d_g.n += 16
                g_.wait_ge(d_g.h, d_g.n)
        ges.close()
        return nc
    wes = ExitStack()
    s_y1T = sb(wes, "s_y1T", [128, 16 * NT], BF16)
    w2 = ExitStack()
    s_woT = sb(w2, "s_woT", [128, 16 * EMB], BF16)
    s_oTq = sb(w2, "s_oTq", [128, 8192], BF16)
    s_aoT = sb(w2, "s_aoT", [128, 8192], BF16)
    s_nat = sb(w2, "s_nat", [128, EMB], F32)
    s_y1b = sb(w2, "s_y1b", [128, EMB], BF16)
    s_bot = sb(w2, "s_bot", [128, EMB], F32)
    s_stgw = sb(w2, "s_stgw", [128, 2 * EMB], F32)
    s_bvc = sb(w2, "s_bvc", [128, 16], BF16)
    s_bv16 = sb(w2, "s_bv16", [16, 128], F32)
    s_bv16b = sb(w2, "s_bv16b", [16, 128], BF16)
    s_onb = sb(w2, "s_onb", [1, 128], BF16)
    s_hi = sb(w2, "s_hi", [1, EMB], BF16)
    s_lob = sb(w2, "s_lob", [1, EMB], BF16)

    def colof(ci):
        if ci < 8:
            k, s = ci // 2, ci % 2
        else:
            k, s = (ci - 8) // 2, 2 + (ci - 8) % 2
        return 512 * (4 * k + s)

    with nc.Block() as blk:
        v0 = v_c.n
        ld0 = d_x.n
        wdcm = {ci: v0 + ci + 2 for ci in range(16)}

        @blk.sync
        def _(sync):
            sync.dma_start(out=s_stgw[0:1, EMB:2 * EMB], in_=bo_in[:, :]).then_inc(d_c.h, 16)
            sync.dma_start(out=s_bv16[:, :], in_=bvfull[:, :])\
                .then_inc(d_c.h, 16)
            d_c.n += 32
            for ci in range(16):
                if ci >= 1:
                    sync.wait_ge(v_c.h, wdcm[ci - 1])
                sync.dma_start(
                    out=s_stgw[:, 0:EMB],
                    in_=woT[128 * ci:128 * ci + 128, :]).then_inc(d_wo.h, 16)
                d_wo.n += 16

        @blk.vector
        def _(v):
            v.memset(s_onb[:, :], 1.0)
            v.wait_ge(d_c.h, d_c.n)
            v_c.inc(v.tensor_copy(s_bv16b[:, :], s_bv16[:, :]))
            for ci in range(16):
                v.wait_ge(d_wo.h, 16 * (ci + 1))
                v_c.inc(v.tensor_copy(
                    s_woT[:, EMB * ci:EMB * ci + EMB],
                    s_stgw[:, 0:EMB]))
                assert v_c.n == wdcm[ci]

        bt_m0 = m_c.n
        bt_a0 = a_c.n

        @blk.tensor
        def _(t):
            t.wait_ge(v_c.h, v0 + 1)
            m_c.inc(t.matmul(p_pt[:, 0:16], s_bv16b[:, :],
                             s_ident[0:16, 0:16], start=True, stop=True))

        @blk.scalar
        def _(a):
            a.wait_ge(m_c.h, bt_m0 + 1)
            a_c.inc(a.copy(s_bvc[:, :], p_pt[:, 0:16]))

        m0 = m_c.n
        a0 = a_c.n

        @blk.tensor
        def _(t):
            t.wait_ge(v_c.h, wdcm[15])
            t.wait_ge(a_c.h, bt_a0 + 1)
            t.wait_ge(d_c.h, d_c.n)
            for j in range(4):
                if j >= 1:
                    t.wait_ge(a_c.h, a0 + j)
                for ci in range(16):
                    mm = t.matmul(
                        p_pj[0][0:1, :], s_bvc[:, ci:ci + 1],
                        s_woT[:, EMB * ci + 512 * j:EMB * ci + 512 * j + 512],
                        start=(ci == 0), stop=(ci == 15))
                m_c.inc(mm)

        @blk.scalar
        def _(a):
            for j in range(4):
                a.wait_ge(m_c.h, m0 + j + 1)
                a_c.inc(a.copy(s_bot[0:1, 512 * j:512 * j + 512],
                               p_pj[0][0:1, :]))

        @blk.vector
        def _(v):
            v.wait_ge(a_c.h, a0 + 4)
            v.wait_ge(d_c.h, d_c.n)
            vf(v, v.tensor_add(s_stgw[0:1, EMB:2 * EMB],
                               s_stgw[0:1, EMB:2 * EMB], s_bot[0:1, :]))
            vf(v, v.tensor_copy(s_hi[:, :], s_stgw[0:1, EMB:2 * EMB]))
            vf(v, v.tensor_copy(s_nat[0:1, 0:EMB], s_hi[:, :]))
            v_c.inc(v.tensor_tensor(s_lob[:, :], s_stgw[0:1, EMB:2 * EMB],
                                    s_nat[0:1, 0:EMB], ALU.subtract))

        bc_v0 = v_c.n
        bc_m0 = m_c.n
        bc_a0 = a_c.n

        @blk.tensor
        def _(t):
            t.wait_ge(v_c.h, bc_v0)
            for j in range(4):
                if j >= 1:
                    t.wait_ge(a_c.h, bc_a0 + j)
                t.matmul(p_pj[1][:, :], s_onb[:, :],
                         s_hi[0:1, 512 * j:512 * j + 512],
                         start=True, stop=False)
                m_c.inc(t.matmul(p_pj[1][:, :], s_onb[:, :],
                                 s_lob[0:1, 512 * j:512 * j + 512],
                                 start=False, stop=True))

        @blk.scalar
        def _(a):
            for j in range(4):
                a.wait_ge(m_c.h, bc_m0 + j + 1)
                a_c.inc(a.copy(s_bot[:, 512 * j:512 * j + 512],
                               p_pj[1][:, :]))

    with nc.Block() as blk:
        g0 = d_g.n
        m0 = m_c.n
        a0 = a_c.n
        v0 = v_c.n
        x0 = d_x.n
        gdma = {qt: g0 + 128 * qt + 64 for qt in range(4)}
        gq = gdma
        wom, woe, lns = {}, {}, {}
        trm, y1m, tre, y1e = {}, {}, {}, {}
        tg_of = {}
        tgrp_e = {}
        m, a = m0, a0
        tg = 0
        for qt in range(4):
            for do in range(16):
                m += 1
                wom[(qt, do)] = m
            for tt in range(4):
                for gg in range(4):
                    m += 1
                    trm[(qt, tt, gg)] = m
                    tg_of[(qt, tt, 0, gg)] = tg
                    tg += 1
                for gg in range(4):
                    m += 1
                    y1m[(qt, tt, gg)] = m
                    tg_of[(qt, tt, 1, gg)] = tg
                    tg += 1
        for qt in range(4):
            for do in range(16):
                a += 1
                woe[(qt, do)] = a
            for tt in range(4):
                for gg in range(4):
                    a += 1
                    tre[(qt, tt, gg)] = a
                    tgrp_e[tg_of[(qt, tt, 0, gg)]] = a
                a += 1
                lns[(qt, tt)] = a
                for gg in range(4):
                    a += 1
                    y1e[(qt, tt, gg)] = a
                    tgrp_e[tg_of[(qt, tt, 1, gg)]] = a
        v = v0
        lnv, lnm, y1bm = {}, {}, {}
        for qt in range(4):
            for tt in range(4):
                v += 1
                lnv[(qt, tt)] = v
                v += 1
                lnm[(qt, tt)] = v
                v += 1
                y1bm[(qt, tt)] = v
        xrm = {}
        x = 0
        for qt in range(4):
            for tt in range(4):
                x += 16
                xrm[(qt, tt)] = x
        y1d = {}
        for qt in range(4):
            for tt in range(4):
                y1d[(qt, tt)] = g0 + 128 * qt + 64 + 16 * (tt + 1)
        tbank = [p_pt, p_s[0]]

        @blk.gpsimd
        def _(g_):
            for k in range(4):
                g_.wait_ge(c_ao[k].h, 1)
            with g_.register("r_t") as r_t:
                for qt in range(4):
                    if qt >= 1:
                        g_.wait_ge(m_c.h, wom[(qt - 1, 15)])
                    g_.reg_load(r_t, s_offs[0:1, qt:qt + 1])
                    for k in range(4):
                        src = bass.AP(ago_out[k], r_t,
                                      [[N, 128], [128 * N, 4], [1, 512]])
                        dst = bass.AP(s_oTq, 2048 * k,
                                      [[8192, 128], [512, 4], [1, 512]])
                        g_.dma_start(out=dst, in_=src).then_inc(d_g.h, 16)
                        d_g.n += 16
                    assert d_g.n == gdma[qt]
                    for tt in range(4):
                        g_.wait_ge(v_c.h, lnm[(qt, tt)])
                        row = 512 * qt + 128 * tt
                        g_.dma_start(out=y1_scr[row:row + 128, :],
                                     in_=s_nat[:, :]).then_inc(d_g.h, 16)
                        d_g.n += 16
                        assert d_g.n == y1d[(qt, tt)]
            g_.wait_ge(d_g.h, d_g.n)

        @blk.sync
        def _(sync):
            for qt in range(4):
                for tt in range(4):
                    if (qt, tt) != (0, 0):
                        pt_ = (qt, tt - 1) if tt > 0 else (qt - 1, 3)
                        sync.wait_ge(v_c.h, lnm[pt_])
                    else:
                        sync.wait_ge(v_c.h, wdcm[15])
                    row = 512 * qt + 128 * tt
                    sync.dma_start(out=s_stgw[:, 0:EMB],
                                   in_=x_res[row:row + 128, :])\
                        .then_inc(d_xr.h, 16)
                    d_xr.n += 16
                    assert d_xr.n == xrm[(qt, tt)]

        @blk.tensor
        def _(t):
            for qt in range(4):
                t.wait_ge(d_g.h, gq[qt])
                for do in range(16):
                    if do >= 2:
                        t.wait_ge(a_c.h, woe[(qt, do - 2)])
                    for ci in range(16):
                        mm = t.matmul(
                            p_pj[do % 2][:, :],
                            s_woT[:, EMB * ci + 128 * do:
                                  EMB * ci + 128 * do + 128],
                            s_oTq[:, colof(ci):colof(ci) + 512],
                            start=(ci == 0), stop=(ci == 15))
                    m_c.inc(mm)
                    assert m_c.n == wom[(qt, do)]
                t.wait_ge(a_c.h, woe[(qt, 15)])
                for tt in range(4):
                    for gg in range(4):
                        ti = tg_of[(qt, tt, 0, gg)]
                        if ti >= 2:
                            t.wait_ge(a_c.h, tgrp_e[ti - 2])
                        bk = tbank[ti % 2]
                        for i in range(4):
                            do = 4 * gg + i
                            mm = t.matmul(
                                bk[:, 128 * i:128 * i + 128],
                                s_aoT[:, 512 * do + 128 * tt:
                                      512 * do + 128 * tt + 128],
                                s_ident[:, :], start=True, stop=True)
                        m_c.inc(mm)
                        assert m_c.n == trm[(qt, tt, gg)]
                    t.wait_ge(v_c.h, y1bm[(qt, tt)])
                    for gg in range(4):
                        ti = tg_of[(qt, tt, 1, gg)]
                        if ti >= 2:
                            t.wait_ge(a_c.h, tgrp_e[ti - 2])
                        bk = tbank[ti % 2]
                        for i in range(4):
                            dc = 4 * gg + i
                            mm = t.matmul(
                                bk[:, 128 * i:128 * i + 128],
                                s_y1b[:, 128 * dc:128 * dc + 128],
                                s_ident[:, :], start=True, stop=True)
                        m_c.inc(mm)
                        assert m_c.n == y1m[(qt, tt, gg)]

        @blk.scalar
        def _(a):
            for qt in range(4):
                for do in range(16):
                    a.wait_ge(m_c.h, wom[(qt, do)])
                    a_c.inc(a.copy(s_aoT[:, 512 * do:512 * do + 512],
                                   p_pj[do % 2][:, :]))
                    assert a_c.n == woe[(qt, do)]
                for tt in range(4):
                    if (qt, tt) != (0, 0):
                        pv = (qt, tt - 1) if tt > 0 else (qt - 1, 3)
                        a.wait_ge(v_c.h, y1bm[pv])
                        a.wait_ge(d_g.h, y1d[pv])
                    for gg in range(4):
                        a.wait_ge(m_c.h, trm[(qt, tt, gg)])
                        bk = tbank[tg_of[(qt, tt, 0, gg)] % 2]
                        a_c.inc(a.copy(s_nat[:, 512 * gg:512 * gg + 512],
                                       bk[:, :]))
                        assert a_c.n == tre[(qt, tt, gg)]
                    a.wait_ge(v_c.h, lnv[(qt, tt)])
                    a_c.inc(a.activation(s_sd[:, :], s_v1[:, :], AF.Sqrt,
                                         bias=s_lneps[:, :],
                                         scale=1.0 / EMB))
                    assert a_c.n == lns[(qt, tt)]
                    for gg in range(4):
                        a.wait_ge(m_c.h, y1m[(qt, tt, gg)])
                        bk = tbank[tg_of[(qt, tt, 1, gg)] % 2]
                        row = 512 * qt + 128 * tt
                        a_c.inc(a.copy(
                            bass.AP(s_y1T, (4 * gg) * NT + row,
                                    [[16 * NT, 128], [NT, 4], [1, 128]]),
                            bass.AP(bk, 0, [[512, 128], [128, 4],
                                            [1, 128]])))
                        assert a_c.n == y1e[(qt, tt, gg)]

        @blk.vector
        def _(v):
            for qt in range(4):
                for tt in range(4):
                    for gg in range(4):
                        v.wait_ge(a_c.h, tre[(qt, tt, gg)])
                    v.wait_ge(d_xr.h, xrm[(qt, tt)])
                    if (qt, tt) != (0, 0):
                        pv = (qt, tt - 1) if tt > 0 else (qt - 1, 3)
                        v.wait_ge(d_g.h, y1d[pv])
                    v.tensor_add(s_nat[:, :], s_nat[:, :], s_bot[:, :])
                    v.tensor_add(s_nat[:, :], s_nat[:, :], s_stgw[:, 0:EMB])
                    vf(v, v.tensor_reduce(s_m1[:, :], s_nat[:, :], AX.X,
                                          ALU.add))
                    vf(v, v.tensor_scalar(s_m1[:, :], s_m1[:, :], 1.0 / EMB,
                                          None, ALU.mult))
                    v.tensor_scalar(s_nat[:, :], s_nat[:, :], s_m1[:, :],
                                    None, ALU.subtract)
                    v.tensor_tensor(s_stgw[:, EMB:2 * EMB], s_nat[:, :],
                                    s_nat[:, :], ALU.mult)
                    v_c.inc(v.tensor_reduce(s_v1[:, :],
                                            s_stgw[:, EMB:2 * EMB],
                                            AX.X, ALU.add))
                    assert v_c.n == lnv[(qt, tt)]
                    v.wait_ge(a_c.h, lns[(qt, tt)])
                    vf(v, v.reciprocal(s_r1[:, :], s_sd[:, :]))
                    v_c.inc(v.tensor_scalar(s_nat[:, :], s_nat[:, :],
                                            s_r1[:, :], None, ALU.mult))
                    assert v_c.n == lnm[(qt, tt)]
                    if (qt, tt) != (0, 0):
                        pv = (qt, tt - 1) if tt > 0 else (qt - 1, 3)
                        v.wait_ge(m_c.h, y1m[(pv[0], pv[1], 3)])
                    v_c.inc(v.tensor_copy(s_y1b[:, :], s_nat[:, :]))
                    assert v_c.n == y1bm[(qt, tt)]

    w2.close()

    if MAX_PHASE < 4:
        with nc.Block() as blk:
            @blk.gpsimd
            def _(g_):
                g_.dma_start(out=out[:, :], in_=y1_scr[:, :])\
                    .then_inc(d_g.h, 16)
                d_g.n += 16
                g_.wait_ge(d_g.h, d_g.n)
        wes.close()
        ges.close()
        return nc
    fes = ExitStack()
    s_hT = sb(fes, "s_hT", [128, 44 * 512], BF16)
    s_sl1 = sb(fes, "s_sl1", [128, 2 * 16 * 256], BF16)
    s_sl2 = sb(fes, "s_sl2", [128, 2 * 16 * 256], BF16)
    s_w3s = sb(fes, "s_w3s", [128, 2 * 44 * 128], BF16)
    s_foT = sb(fes, "s_foT", [128, 16 * 512], BF16)
    s_silu = sb(fes, "s_silu", [128, 512], BF16)
    s_natf = sb(fes, "s_natf", [128, EMB], F32)
    s_sqf = sb(fes, "s_sqf", [128, EMB], F32)
    s_stgf = sb(fes, "s_stgf", [128, EMB], F32)

    with nc.Block() as blk:
        @blk.gpsimd
        def _(g_):
            for i in range(3):
                g_.wait_ge(c_ag[i].h, 1)

    for tc in range(4):
        with nc.Block() as blk:
            t0 = 512 * tc
            m0, a0, v0, x0, g0 = m_c.n, a_c.n, v_c.n, d_x.n, d_g.n
            abm = {}
            m = m0
            for fc in range(22):
                for ft in range(2):
                    m += 1
                    abm[(fc, ft, 0)] = m
                    m += 1
                    abm[(fc, ft, 1)] = m
            outm = {}
            for do in range(16):
                m += 1
                outm[do] = m
            trm2 = {}
            tg = 0
            tg2_of = {}
            for tt in range(4):
                for gg in range(4):
                    m += 1
                    trm2[(tt, gg)] = m
                    tg2_of[(tt, gg)] = tg
                    tg += 1
            a = a0
            silm = {}
            for fc in range(22):
                for ft in range(2):
                    a += 1
                    silm[(fc, ft)] = a
            oute = {}
            for do in range(16):
                a += 1
                oute[do] = a
            tre2 = {}
            tg2_e = {}
            lns2 = {}
            for tt in range(4):
                for gg in range(4):
                    a += 1
                    tre2[(tt, gg)] = a
                    tg2_e[tg2_of[(tt, gg)]] = a
                a += 1
                lns2[tt] = a
            v = v0
            hm = {}
            for fc in range(22):
                for ft in range(2):
                    v += 1
                    hm[(fc, ft)] = v
            lnv2, ln2m = {}, {}
            for tt in range(4):
                v += 1
                lnv2[tt] = v
                v += 1
                ln2m[tt] = v
            slm = {}
            w3m = {}
            resm = {}
            outd = {}
            g = g0
            for tt in range(4):
                g += 16
                outd[tt] = g
            tbank2 = [p_pt, p_s[0]]

            @blk.sync
            def _(sync, tc=tc):
                for fc in range(22):
                    if fc >= 2:
                        sync.wait_ge(m_c.h, abm[(fc - 2, 1, 1)])
                    f0 = 256 * fc
                    r0, r1 = f0 // 704, (f0 + 255) // 704
                    half = 4096 * (fc % 2)
                    dsem = d_sl[fc % 2]
                    for wsl, agt in ((s_sl1, ag_out[0]), (s_sl2, ag_out[1])):
                        if r0 == r1:
                            fl = f0 - 704 * r0
                            sync.dma_start(
                                out=bass.AP(wsl, half,
                                            [[8192, 128], [256, 16],
                                             [1, 256]]),
                                in_=bass.AP(agt, (2048 * r0) * 704 + fl,
                                            [[704, 128], [704 * 128, 16],
                                             [1, 256]]))\
                                .then_inc(dsem.h, 16)
                            dsem.n += 16
                        else:
                            w0 = 704 * r1 - f0
                            fl = f0 - 704 * r0
                            sync.dma_start(
                                out=bass.AP(wsl, half,
                                            [[8192, 128], [256, 16],
                                             [1, w0]]),
                                in_=bass.AP(agt, (2048 * r0) * 704 + fl,
                                            [[704, 128], [704 * 128, 16],
                                             [1, w0]]))\
                                .then_inc(dsem.h, 16)
                            dsem.n += 16
                            sync.dma_start(
                                out=bass.AP(wsl, half + w0,
                                            [[8192, 128], [256, 16],
                                             [1, 256 - w0]]),
                                in_=bass.AP(agt, (2048 * r1) * 704,
                                            [[704, 128], [704 * 128, 16],
                                             [1, 256 - w0]]))\
                                .then_inc(dsem.h, 16)
                            dsem.n += 16
                    slm[fc] = (fc % 2, dsem.n)
                for do in range(16):
                    if do >= 2:
                        sync.wait_ge(m_c.h, outm[do - 2])
                    r, hf = do // 2, do % 2
                    sync.dma_start(
                        out=bass.AP(s_w3s, 5632 * (do % 2),
                                    [[11264, 128], [128, 44], [1, 128]]),
                        in_=bass.AP(ag_out[2], (5632 * r) * 256 + 128 * hf,
                                    [[256, 128], [128 * 256, 44], [1, 128]]))\
                        .then_inc(d_w3p[do % 2].h, 16)
                    d_w3p[do % 2].n += 16
                    w3m[do] = (do % 2, d_w3p[do % 2].n)
                for tt in range(4):
                    if tt >= 1:
                        sync.wait_ge(v_c.h, ln2m[tt - 1])
                    row = t0 + 128 * tt
                    sync.dma_start(out=s_stgf[:, :],
                                   in_=y1_scr[row:row + 128, :])\
                        .then_inc(d_res.h, 16)
                    d_res.n += 16
                    resm[tt] = d_res.n

            @blk.tensor
            def _(t, tc=tc):
                for fc in range(22):
                    t.wait_ge(d_sl[slm[fc][0]].h, slm[fc][1])
                    half = 4096 * (fc % 2)
                    for ft in range(2):
                        fi = 2 * fc + ft
                        for w, sl in ((0, s_sl1), (1, s_sl2)):
                            bank = p_s[2 * ft + w]
                            if fc >= 1:
                                if w == 0:
                                    t.wait_ge(a_c.h, silm[(fc - 1, ft)])
                                else:
                                    t.wait_ge(v_c.h, hm[(fc - 1, ft)])
                            for di in range(16):
                                mm = t.matmul(
                                    bank[:, :],
                                    sl[:, half + 256 * di + 128 * ft:
                                       half + 256 * di + 128 * ft + 128],
                                    s_y1T[:, NT * di + t0:NT * di + t0 + 512],
                                    start=(di == 0), stop=(di == 15))
                            m_c.inc(mm)
                            assert m_c.n == abm[(fc, ft, w)]
                for do in range(16):
                    t.wait_ge(d_w3p[w3m[do][0]].h, w3m[do][1])
                    t.wait_ge(v_c.h, hm[(21, 1)])
                    if do >= 2:
                        t.wait_ge(a_c.h, oute[do - 2])
                    for fi in range(44):
                        mm = t.matmul(
                            p_pj[do % 2][:, :],
                            s_w3s[:, 5632 * (do % 2) + 128 * fi:
                                  5632 * (do % 2) + 128 * fi + 128],
                            s_hT[:, 512 * fi:512 * fi + 512],
                            start=(fi == 0), stop=(fi == 43))
                    m_c.inc(mm)
                    assert m_c.n == outm[do]
                t.wait_ge(a_c.h, oute[15])
                for tt in range(4):
                    for gg in range(4):
                        ti = tg2_of[(tt, gg)]
                        if ti >= 2:
                            t.wait_ge(a_c.h, tg2_e[ti - 2])
                        bk = tbank2[ti % 2]
                        for i in range(4):
                            do = 4 * gg + i
                            mm = t.matmul(
                                bk[:, 128 * i:128 * i + 128],
                                s_foT[:, 512 * do + 128 * tt:
                                      512 * do + 128 * tt + 128],
                                s_ident[:, :], start=True, stop=True)
                        m_c.inc(mm)
                        assert m_c.n == trm2[(tt, gg)]

            @blk.scalar
            def _(a, tc=tc):
                for fc in range(22):
                    for ft in range(2):
                        a.wait_ge(m_c.h, abm[(fc, ft, 0)])
                        if fc >= 1 or ft >= 1:
                            pv = (fc, ft - 1) if ft > 0 else (fc - 1, 1)
                            a.wait_ge(v_c.h, hm[pv])
                        a_c.inc(a.activation(s_silu[:, :],
                                             p_s[2 * ft + 0][:, :], AF.Silu))
                        assert a_c.n == silm[(fc, ft)]
                for do in range(16):
                    a.wait_ge(m_c.h, outm[do])
                    a_c.inc(a.copy(s_foT[:, 512 * do:512 * do + 512],
                                   p_pj[do % 2][:, :]))
                    assert a_c.n == oute[do]
                for tt in range(4):
                    if tt >= 1:
                        a.wait_ge(v_c.h, ln2m[tt - 1])
                        a.wait_ge(d_g.h, outd[tt - 1])
                    for gg in range(4):
                        a.wait_ge(m_c.h, trm2[(tt, gg)])
                        bk = tbank2[tg2_of[(tt, gg)] % 2]
                        a_c.inc(a.copy(s_natf[:, 512 * gg:512 * gg + 512],
                                       bk[:, :]))
                        assert a_c.n == tre2[(tt, gg)]
                    a.wait_ge(v_c.h, lnv2[tt])
                    a_c.inc(a.activation(s_sd[:, :], s_v1[:, :], AF.Sqrt,
                                         bias=s_lneps[:, :],
                                         scale=1.0 / EMB))
                    assert a_c.n == lns2[tt]

            @blk.vector
            def _(v, tc=tc):
                for fc in range(22):
                    for ft in range(2):
                        fi = 2 * fc + ft
                        v.wait_ge(a_c.h, silm[(fc, ft)])
                        v.wait_ge(m_c.h, abm[(fc, ft, 1)])
                        v_c.inc(v.tensor_tensor(
                            s_hT[:, 512 * fi:512 * fi + 512],
                            s_silu[:, :], p_s[2 * ft + 1][:, :], ALU.mult))
                        assert v_c.n == hm[(fc, ft)]
                for tt in range(4):
                    for gg in range(4):
                        v.wait_ge(a_c.h, tre2[(tt, gg)])
                    v.wait_ge(d_res.h, resm[tt])
                    v.tensor_add(s_natf[:, :], s_natf[:, :], s_stgf[:, :])
                    vf(v, v.tensor_reduce(s_m1[:, :], s_natf[:, :], AX.X,
                                          ALU.add))
                    vf(v, v.tensor_scalar(s_m1[:, :], s_m1[:, :], 1.0 / EMB,
                                          None, ALU.mult))
                    v.tensor_scalar(s_natf[:, :], s_natf[:, :], s_m1[:, :],
                                    None, ALU.subtract)
                    v.tensor_tensor(s_sqf[:, :], s_natf[:, :], s_natf[:, :],
                                    ALU.mult)
                    v_c.inc(v.tensor_reduce(s_v1[:, :], s_sqf[:, :],
                                            AX.X, ALU.add))
                    assert v_c.n == lnv2[tt]
                    v.wait_ge(a_c.h, lns2[tt])
                    vf(v, v.reciprocal(s_r1[:, :], s_sd[:, :]))
                    v_c.inc(v.tensor_scalar(s_natf[:, :], s_natf[:, :],
                                            s_r1[:, :], None, ALU.mult))
                    assert v_c.n == ln2m[tt]

            @blk.gpsimd
            def _(g_, tc=tc):
                for tt in range(4):
                    g_.wait_ge(v_c.h, ln2m[tt])
                    row = t0 + 128 * tt
                    g_.dma_start(out=out[row:row + 128, :],
                                 in_=s_natf[:, :]).then_inc(d_g.h, 16)
                    d_g.n += 16
                    assert d_g.n == outd[tt]

    if DEBUG_TAPS:
        taps = [("ag1_out", ag_out[0], [16384, 704], BF16),
                ("ag3_out", ag_out[2], [45056, 256], BF16),
                ("ago_in", ago_in, [1024, N], BF16),
                ("y1_scr", y1_scr, [NT, EMB], F32),
                ("scl_flat", scl_flat, [3, 22528], F32),
                ("rep_flat", rep_flat, [3, 22528], F32),
                ("ar_out", ar_out, [1, 4], F32)]
        dbg_outs = {nm: nc.dram_tensor("dbg_" + nm, shp, dtp,
                                       kind="ExternalOutput")
                    for nm, _, shp, dtp in taps}
        with nc.Block() as blk:
            @blk.gpsimd
            def _(g_):
                g_.wait_ge(d_g.h, d_g.n)
                for nm, src_t, shp, dtp in taps:
                    g_.dma_start(out=dbg_outs[nm][:, :],
                                 in_=src_t[:, :]).then_inc(d_g.h, 16)
                    d_g.n += 16
                g_.wait_ge(d_g.h, d_g.n)
    with nc.Block() as blk:
        @blk.gpsimd
        def _(g_):
            g_.wait_ge(d_g.h, d_g.n)

    fes.close()
    wes.close()
    ges.close()
    return nc


def kernel(**inputs):
    x = np.asarray(inputs["L0_attn_x"])
    wq = np.asarray(inputs["L0_attn_wq"])
    bq = np.asarray(inputs["L0_attn_bq"])
    wk = np.asarray(inputs["L0_attn_wk"])
    bk = np.asarray(inputs["L0_attn_bk"])
    wv = np.asarray(inputs["L0_attn_wv"])
    bv = np.asarray(inputs["L0_attn_bv"])
    wo = np.asarray(inputs["L0_attn_wo"])
    bo = np.asarray(inputs["L0_attn_bo"])
    w1 = np.asarray(inputs["L0_ffn_weight1"])
    w2 = np.asarray(inputs["L0_ffn_weight2"])
    w3 = np.asarray(inputs["L0_ffn_weight3"])

    ident = np.eye(128, dtype=bf16)
    cmask = np.triu(np.full((128, 128), -1e30, dtype=np.float32), k=1)

    in_maps = []
    for c in range(8):
        b, hh = c // 2, c % 2
        hs = slice(1024 * hh, 1024 * hh + 1024)
        in_maps.append({
            "xT": np.ascontiguousarray(x[b].T),
            "x_res": np.ascontiguousarray(x[b, 2048 * hh:2048 * hh + 2048, :]),
            "wqT": np.ascontiguousarray(wq[hs, :].T),
            "wkT": np.ascontiguousarray(wk[hs, :].T),
            "wvT": np.ascontiguousarray(wv[hs, :].T),
            "woT": np.ascontiguousarray(wo.T),
            "bqkv": np.ascontiguousarray(
                np.stack([bq[hs], bk[hs], bv[hs]], axis=1)),
            "bvfull": np.ascontiguousarray(bv.reshape(16, 128)),
            "bo_in": np.ascontiguousarray(bo.reshape(1, 2048)),
            "w1s": np.ascontiguousarray(w1[704 * c:704 * c + 704, :]),
            "w2s": np.ascontiguousarray(w2[704 * c:704 * c + 704, :]),
            "w3s": np.ascontiguousarray(w3[256 * c:256 * c + 256, :]),
            "identw": ident,
            "cmaskw": cmask,
            "offsw": np.array([[2048 * hh + 512 * qt for qt in range(4)]
                               + [0, 0, 0, 0]], dtype=np.int32),
        })

    nc = build_program()
    res = run_bass_kernel_spmd(nc, in_maps, core_ids=list(range(8)))
    outp = np.empty((4, 4096, 2048), dtype=np.float32)
    for c in range(8):
        b, hh = c // 2, c % 2
        outp[b, 2048 * hh:2048 * hh + 2048, :] = \
            np.asarray(res.results[c]["out"], dtype=np.float32)
    return outp


if __name__ == "__main__":
    p = build_program()
    n = sum(1 for _ in p.all_instructions())
    print("built, instructions:", n)


# version 09 (final)
# speedup vs baseline: 1.0078x; 1.0071x over previous
import numpy as np
import ml_dtypes
import concourse.bass as bass
import concourse.mybir as mybir
from concourse.bass_utils import run_bass_kernel_spmd
from contextlib import ExitStack

F32 = mybir.dt.float32
BF16 = mybir.dt.bfloat16
I32 = mybir.dt.int32
AF = mybir.ActivationFunctionType
ALU = mybir.AluOpType
AX = mybir.AxisListType
bf16 = ml_dtypes.bfloat16

EMB = 2048
N = 4096
NH = 8
DFF = 5632
NT = 2048
QSCALE = float(1.0 / np.sqrt(np.float32(128.0)))
ESHIFT = -12.0
NSCAL = 180224.0
LN_EPS = 1e-5

_NF4F = np.array([-1.0, -0.6962, -0.5251, -0.3949, -0.2844, -0.1848, -0.0911,
                  0.0, 0.0796, 0.1609, 0.2461, 0.3379, 0.4407, 0.5626, 0.723,
                  1.0], dtype=np.float32)
_NF4 = _NF4F.astype(bf16)
_LF = _NF4.astype(np.float32)
_DL = np.diff(_LF)


def _nf4_taus():
    def classify(x):
        v = np.array(x, dtype=bf16)
        d = np.abs((np.full(16, v, dtype=bf16) - _NF4).astype(bf16))
        return int(np.argmin(d.astype(np.float32)))
    taus = []
    for j in range(15):
        lo, hi = float(_LF[j]), float(_LF[j + 1])
        for _ in range(80):
            m = (lo + hi) / 2
            if classify(m) <= j:
                lo = m
            else:
                hi = m
        taus.append(np.float32((lo + hi) / 2))
    return np.array(taus, dtype=np.float32)


_TAUS = _nf4_taus()

DEBUG_TAPS = False
MAX_PHASE = 4
ATTN_HEADS = NH
ATTN_STOP = 'full'
ATTN_GROUPS = 16

WCFG = [(704, 2048, 0), (704, 2048, 1), (256, 5632, 2)]


def _segs(qc):
    kr = 128 * (qc + 1)
    return [(kb, min(512, kr - kb)) for kb in range(0, kr, 512)]


class Sem:
    def __init__(self, es, nc, name):
        self.h = es.enter_context(nc.semaphore(name))
        self.n = 0

    def inc(self, instr, k=1):
        instr.then_inc(self.h, k)
        self.n += k
        return self.n


def attn_schedule(v0, a0, m0, dg0):
    wdc, pj, pje, smm, cmk, expm, norm = {}, {}, {}, {}, {}, {}, {}
    ptm, pte, avm, oev, odma, oad = {}, {}, {}, {}, {}, {}
    v, a, m, dg = v0, a0, m0, dg0
    exp_seq = []
    for h in range(ATTN_HEADS):
        for qc in range(32):
            for sg in range(len(_segs(qc))):
                exp_seq.append((h, qc, sg))
    exp_pos = {k: i for i, k in enumerate(exp_seq)}
    for h in range(ATTN_HEADS):
        for w in range(3):
            v += 1
            wdc[(h, w)] = v
        if ATTN_STOP == 'proj':
            continue
        for g in range(ATTN_GROUPS):
            if ATTN_STOP == 'full' and g >= 1 and 2 * (g - 1) + 2 > 16:
                v += 1
                oad[(h, g - 1, 0)] = v
                v += 1
                oad[(h, g - 1, 1)] = v
                v += 1
                oev[(h, g - 1)] = ('v', v)
            for qc in (2 * g, 2 * g + 1):
                v += 1
                cmk[(h, qc)] = v
                v += 1
                norm[(h, qc)] = v
        g = ATTN_GROUPS - 1
        if ATTN_STOP == 'full' and 2 * g + 2 > 16:
            v += 1
            oad[(h, g, 0)] = v
            v += 1
            oad[(h, g, 1)] = v
            v += 1
            oev[(h, g)] = ('v', v)
    for h in range(ATTN_HEADS):
        for w in (1, 0):
            for tb in range(8):
                m += 1
                pj[(h, w, tb)] = m
        for kb4 in range(8):
            m += 1
            pj[(h, 2, kb4)] = m
        if ATTN_STOP == 'proj':
            continue
        for g in range(ATTN_GROUPS):
            qc0, qc1 = 2 * g, 2 * g + 1
            for qc in (qc0, qc1):
                for sg in range(len(_segs(qc))):
                    m += 1
                    smm[(h, qc, sg)] = m
            if ATTN_STOP == 'smax':
                continue
            nki = qc1 + 1
            for sub0 in range(0, nki, 16):
                sks = list(range(sub0, min(sub0 + 16, nki)))
                for ki in sks:
                    m += 1
                    ptm[(h, g, ki)] = m
                for ki in sks:
                    m += 1
                    avm[(h, g, ki)] = m
    for h in range(ATTN_HEADS):
        for w in (1, 0):
            for tb in range(8):
                a += 1
                pje[(h, w, tb)] = a
        for kb4 in range(8):
            a += 1
            pje[(h, 2, kb4)] = a
        if ATTN_STOP == 'proj':
            continue
        for g in range(ATTN_GROUPS):
            qc0, qc1 = 2 * g, 2 * g + 1
            for qc in (qc0, qc1):
                for sg in range(len(_segs(qc))):
                    a += 1
                    expm[(h, qc, sg)] = a
            if ATTN_STOP == 'smax':
                continue
            nki = qc1 + 1
            for ki in range(nki):
                a += 1
                pte[(h, g, ki)] = a
            if nki <= 16:
                a += 1
                oev[(h, g)] = ('a', a)
    for h in range(ATTN_HEADS):
        dg += 16
        odma[h] = dg
    return dict(wdc=wdc, pj=pj, pje=pje, smm=smm, cmk=cmk, expm=expm,
                norm=norm, ptm=ptm, pte=pte, avm=avm, oev=oev, odma=odma,
                oad=oad, exp_seq=exp_seq, exp_pos=exp_pos,
                v_end=v, a_end=a, m_end=m, dg_end=dg)


def build_program():
    nc = bass.Bass("TRN2", target_bir_lowering=False, debug=False,
                   detect_race_conditions=False)

    xT = nc.dram_tensor("xT", [EMB, N], F32, kind="ExternalInput")
    x_res = nc.dram_tensor("x_res", [NT, EMB], F32, kind="ExternalInput")
    wqT = nc.dram_tensor("wqT", [EMB, 1024], F32, kind="ExternalInput")
    wkT = nc.dram_tensor("wkT", [EMB, 1024], F32, kind="ExternalInput")
    wvT = nc.dram_tensor("wvT", [EMB, 1024], F32, kind="ExternalInput")
    woT = nc.dram_tensor("woT", [EMB, EMB], F32, kind="ExternalInput")
    bqkv = nc.dram_tensor("bqkv", [1024, 3], F32, kind="ExternalInput")
    bvfull = nc.dram_tensor("bvfull", [16, 128], F32, kind="ExternalInput")
    bo_in = nc.dram_tensor("bo_in", [1, EMB], F32, kind="ExternalInput")
    w1s = nc.dram_tensor("w1s", [704, 2048], BF16, kind="ExternalInput")
    w2s = nc.dram_tensor("w2s", [704, 2048], BF16, kind="ExternalInput")
    w3s = nc.dram_tensor("w3s", [256, 5632], BF16, kind="ExternalInput")
    identw = nc.dram_tensor("identw", [128, 128], BF16, kind="ExternalInput")
    cmaskw = nc.dram_tensor("cmaskw", [128, 128], F32, kind="ExternalInput")
    offsw = nc.dram_tensor("offsw", [1, 8], I32, kind="ExternalInput")
    out = nc.dram_tensor("out", [NT, EMB], F32, kind="ExternalOutput")

    scl_flat = nc.dram_tensor("scl_flat", [3, 22528], F32)
    rep_flat = nc.dram_tensor("rep_flat", [3, 22528], F32)
    ar_in = nc.dram_tensor("ar_in", [1, 4], F32)
    ar_out = nc.dram_tensor("ar_out", [1, 4], F32)
    wsrcs = [w1s, w2s, w3s]
    ag_in = [nc.dram_tensor("ag1_in", [2048, 704], BF16),
             nc.dram_tensor("ag2_in", [2048, 704], BF16),
             nc.dram_tensor("ag3_in", [5632, 256], BF16)]
    ag_out = [nc.dram_tensor("ag1_out", [16384, 704], BF16),
              nc.dram_tensor("ag2_out", [16384, 704], BF16),
              nc.dram_tensor("ag3_out", [45056, 256], BF16)]
    ago_in = nc.dram_tensor("ago_in", [1024, N], BF16)
    ago_out = [nc.dram_tensor(f"ago_out{k}", [512, N], BF16) for k in range(4)]
    y1_scr = nc.dram_tensor("y1_scr", [NT, EMB], F32)
    dbg_deqw = nc.dram_tensor("dbg_deqw", [128, DFF], BF16,
                              kind="ExternalOutput") if DEBUG_TAPS else None
    dbg_acc = nc.dram_tensor("dbg_acc", [128, DFF], F32,
                             kind="ExternalOutput") if DEBUG_TAPS else None

    WORLD = [list(range(8))]
    PAIRS = [[0, 1], [2, 3], [4, 5], [6, 7]]

    ges = ExitStack()

    def sb(stk, name, shape, dt=BF16):
        return stk.enter_context(nc.sbuf_tensor(name, shape, dt))

    s_ident = sb(ges, "s_ident", [128, 128], BF16)
    s_cmask = sb(ges, "s_cmask", [128, 128], F32)
    s_bqkv = sb(ges, "s_bqkv", [128, 24], F32)
    s_bqs = sb(ges, "s_bqs", [128, 8], F32)
    s_offs = ges.enter_context(nc.sbuf_tensor("s_offs", [1, 8], I32))
    s_zb = sb(ges, "s_zb", [128, 16], F32)
    s_esh = sb(ges, "s_esh", [128, 1], F32)
    s_lneps = sb(ges, "s_lneps", [128, 1], F32)
    s_sd = sb(ges, "s_sd", [128, 1], F32)
    s_z = sb(ges, "s_z", [128, 1], F32)
    s_zr = sb(ges, "s_zr", [128, 1], F32)
    s_m1 = sb(ges, "s_m1", [128, 1], F32)
    s_v1 = sb(ges, "s_v1", [128, 1], F32)
    s_r1 = sb(ges, "s_r1", [128, 1], F32)

    p_s = [ges.enter_context(nc.psum_tensor(f"p_s{i}", [128, 512], F32))
           for i in range(4)]
    p_pt = ges.enter_context(nc.psum_tensor("p_pt", [128, 512], F32))
    p_o = ges.enter_context(nc.psum_tensor("p_o", [128, 512], F32))
    p_pj = [ges.enter_context(nc.psum_tensor(f"p_pj{i}", [128, 512], F32))
            for i in range(2)]

    mk = lambda name: Sem(ges, nc, name)
    d_x = mk("d_x")
    d_g = mk("d_g")
    d_c = mk("d_c")
    d_wr = [mk(f"d_wr{i}") for i in range(3)]
    d_xl = mk("d_xl")
    d_xl2 = mk("d_xl2")
    d_wh = mk("d_wh")
    d_wo = mk("d_wo")
    d_xr = mk("d_xr")
    d_sl = [mk(f"d_sl{i}") for i in range(2)]
    d_w3p = [mk(f"d_w3p{i}") for i in range(2)]
    d_res = mk("d_res")
    v_c = mk("v_c")
    a_c = mk("a_c")
    m_c = mk("m_c")
    c_ar = mk("c_ar")
    c_ag = [mk(f"c_ag{i}") for i in range(3)]
    c_ao = [mk(f"c_ao{i}") for i in range(4)]

    f_v = mk("f_v")

    def vf(v, instr):
        f_v.inc(instr, 1)
        v.wait_ge(f_v.h, f_v.n)
        return instr

    with nc.Block() as blk:
        @blk.sync
        def _(sync):
            sync.dma_start(out=s_ident[:, :], in_=identw[:, :]).then_inc(d_c.h, 16)
            sync.dma_start(out=s_cmask[:, :], in_=cmaskw[:, :]).then_inc(d_c.h, 16)
            sync.dma_start(
                out=bass.AP(s_bqkv, 0, [[24, 128], [3, 8], [1, 3]]),
                in_=bass.AP(bqkv, 0, [[3, 128], [384, 8], [1, 3]]))\
                .then_inc(d_c.h, 16)
            sync.dma_start(out=s_offs[:, :], in_=offsw[:, :]).then_inc(d_c.h, 16)
            d_c.n += 64

        @blk.vector
        def _(v):
            v.wait_ge(d_c.h, d_c.n)
            v.memset(s_esh[:, :], ESHIFT)
            v.memset(s_lneps[:, :], LN_EPS)
            v.tensor_scalar(s_bqs[:, :],
                            bass.AP(s_bqkv, 0, [[24, 128], [3, 8]]),
                            QSCALE, None, ALU.mult)

    des = ExitStack()
    s_wraw = sb(des, "s_wraw", [128, 12288], BF16)
    s_s1nat = sb(des, "s_s1nat", [128, 192], F32)
    s_repnat = sb(des, "s_repnat", [128, 192], F32)
    s_thr = sb(des, "s_thr", [128, 1320], F32)
    s_wf32 = sb(des, "s_wf32", [128, DFF], F32)
    s_tmp = sb(des, "s_tmp", [128, DFF], F32)
    s_acc = sb(des, "s_acc", [128, DFF], F32)
    s_deqw = sb(des, "s_deqw", [128, DFF], BF16)
    s_slab = sb(des, "s_slab", [128, 11264], BF16)
    s_s1sum = sb(des, "s_s1sum", [128, 1], F32)
    s_ones = sb(des, "s_ones", [128, 1], F32)
    s_means = sb(des, "s_means", [1, 4], F32)
    s_gmean = sb(des, "s_gmean", [1, 4], F32)
    s_k88 = sb(des, "s_k88", [1, 88], BF16)
    s_mean1 = sb(des, "s_mean1", [1, 1], F32)
    s_mb1 = sb(des, "s_mb1", [1, 1], BF16)
    s_meanf = sb(des, "s_meanf", [88, 1], F32)
    s_meanb = sb(des, "s_meanb", [88, 1], BF16)
    s_meanbf = sb(des, "s_meanbf", [88, 1], F32)
    s_s1g = sb(des, "s_s1g", [88, 256], F32)
    s_cb = sb(des, "s_cb", [88, 256], BF16)
    s_cf = sb(des, "s_cf", [88, 256], F32)
    s_gabs = sb(des, "s_gabs", [88, 1], F32)
    s_twob = sb(des, "s_twob", [88, 1], BF16)
    s_twof = sb(des, "s_twof", [88, 1], F32)
    s_rqf = sb(des, "s_rqf", [88, 1], F32)
    s_c256 = sb(des, "s_c256", [88, 1], F32)
    s_qfb = sb(des, "s_qfb", [88, 1], BF16)
    s_qff = sb(des, "s_qff", [88, 1], F32)
    s_prodb = sb(des, "s_prodb", [88, 256], BF16)
    s_qi = des.enter_context(nc.sbuf_tensor("s_qi", [88, 256], I32))
    s_qc = sb(des, "s_qc", [88, 256], F32)
    s_dqb = sb(des, "s_dqb", [88, 256], BF16)
    s_repb = sb(des, "s_repb", [88, 256], BF16)
    s_repf = sb(des, "s_repf", [88, 256], F32)

    with nc.Block() as blk:
        ld0 = d_x.n
        v0, a0, m0, g0 = v_c.n, a_c.n, m_c.n, d_g.n
        vsum = {wi: v0 + wi + 1 for _, _, wi in WCFG}

        @blk.sync
        def _(sync):
            for F, D, wi in WCFG:
                if wi > 0:
                    sync.wait_ge(v_c.h, vsum[wi - 1])
                for ft in range((F + 127) // 128):
                    rows = min(128, F - 128 * ft)
                    sync.dma_start(out=s_wraw[0:rows, D * ft:D * ft + D],
                                   in_=wsrcs[wi][128 * ft:128 * ft + rows, :])\
                        .then_inc(d_wr[wi].h, 16)
                    d_wr[wi].n += 16

        @blk.vector
        def _(v):
            v.memset(s_ones[:, :], 1.0)
            v.memset(s_k88[:, :], 1.0)
            v.memset(s_c256[:, :], 256.0)
            v.memset(s_means[:, :], 0.0)
            dmp = g0
            for F, D, wi in WCFG:
                nblk = D // 64
                natc = 32 if wi < 2 else 88
                nft = (F + 127) // 128
                if wi > 0:
                    v.wait_ge(d_g.h, dmp)
                dmp += 16 * nft
                v.memset(s_s1nat[:, :], 0.0)
                v.wait_ge(d_wr[wi].h, 16 * nft)
                for ft in range(nft):
                    rows = min(128, F - 128 * ft)
                    v.tensor_reduce(
                        s_s1nat[0:rows, natc * ft:natc * ft + nblk],
                        bass.AP(s_wraw, D * ft,
                                [[12288, rows], [64, nblk], [1, 64]]),
                        AX.X, ALU.max, apply_absolute_value=True)
                v.tensor_reduce(s_s1sum[:, :], s_s1nat[:, 0:natc * nft],
                                AX.X, ALU.add)
                v_c.inc(v.tensor_copy(s_tmp[0:128, 0:1], s_s1sum[:, 0:1]))
                assert v_c.n == vsum[wi]

        @blk.tensor
        def _(t):
            for F, D, wi in WCFG:
                t.wait_ge(v_c.h, vsum[wi])
                m_c.inc(t.matmul(p_pj[0][0:1, wi:wi + 1], s_s1sum[:, :],
                                 s_ones[:, :], start=True, stop=True))

        @blk.scalar
        def _(a):
            a.wait_ge(m_c.h, m_c.n)
            a_c.inc(a.copy(s_means[0:1, 0:3], p_pj[0][0:1, 0:3]))

        @blk.gpsimd
        def _(g):
            for F, D, wi in WCFG:
                g.wait_ge(v_c.h, vsum[wi])
                nblk = D // 64
                natc = 32 if wi < 2 else 88
                for ft in range((F + 127) // 128):
                    rows = min(128, F - 128 * ft)
                    g.dma_start(
                        out=bass.AP(scl_flat, 22528 * wi + 128 * nblk * ft,
                                    [[nblk, rows], [1, nblk]]),
                        in_=s_s1nat[0:rows, natc * ft:natc * ft + nblk])\
                        .then_inc(d_g.h, 16)
                    d_g.n += 16
            g.wait_ge(a_c.h, a_c.n)
            g.dma_start(out=ar_in[:, :], in_=s_means[:, :]).then_inc(d_g.h, 16)
            d_g.n += 16
            g.wait_ge(d_g.h, d_g.n)
            g.collective_compute(
                "AllReduce", ALU.add, replica_groups=WORLD,
                ins=[ar_in.ap().opt()], outs=[ar_out.ap().opt()])\
                .then_inc(c_ar.h, 1)
            g.wait_ge(c_ar.h, 1)
            g.dma_start(out=s_gmean[:, :], in_=ar_out[:, :]).then_inc(d_g.h, 16)
            d_g.n += 16

    for F, D, wi in WCFG:
        nblk = D // 64
        nft = (F + 127) // 128
        natc = 32 if wi < 2 else 88
        ndc = D // 128

        with nc.Block() as blk:
            v_mb = v_c.n + 1
            m_mb = m_c.n + 1
            a_mb = a_c.n + 1

            @blk.gpsimd
            def _(g, wi=wi):
                g.wait_ge(d_g.h, d_g.n)
                g.dma_start(
                    out=s_s1g[:, :],
                    in_=bass.AP(scl_flat, 22528 * wi, [[256, 88], [1, 256]]))\
                    .then_inc(d_g.h, 16)
                d_g.n += 16

            @blk.tensor
            def _(t, wi=wi, v_mb=v_mb):
                t.wait_ge(v_c.h, v_mb)
                m_c.inc(t.matmul(p_pj[1][0:88, 0:1], s_k88[:, :],
                                 s_mb1[:, :], start=True, stop=True))

            @blk.scalar
            def _(a, m_mb=m_mb):
                a.wait_ge(m_c.h, m_mb)
                a.copy(s_meanf[:, :], p_pj[1][0:88, 0:1])
                a_c.inc(a.copy(s_meanb[:, :], s_meanf[:, :]))

            @blk.vector
            def _(v, wi=wi, a_mb=a_mb):
                v.wait_ge(d_g.h, d_g.n)
                vf(v, v.tensor_scalar(s_mean1[:, :], s_gmean[0:1, wi:wi + 1],
                                      1.0 / NSCAL, None, ALU.mult))
                v_c.inc(v.tensor_copy(s_mb1[:, :], s_mean1[:, :]))
                v.wait_ge(a_c.h, a_mb)
                vf(v, v.tensor_copy(s_meanbf[:, :], s_meanb[:, :]))
                v.tensor_scalar(s_cb[:, :], s_s1g[:, :], s_meanbf[:, :], None,
                                ALU.subtract)
                v.tensor_copy(s_cf[:, :], s_cb[:, :])
                vf(v, v.tensor_reduce(s_gabs[:, :], s_cf[:, :], AX.X, ALU.max,
                                      apply_absolute_value=True))
                vf(v, v.tensor_scalar(s_twob[:, :], s_gabs[:, :], 2.0, None,
                                      ALU.mult))
                vf(v, v.tensor_copy(s_twof[:, :], s_twob[:, :]))
                vf(v, v.reciprocal(s_rqf[:, :], s_twof[:, :]))
                vf(v, v.tensor_scalar(s_qfb[:, :], s_rqf[:, :], 256.0, None,
                                      ALU.mult))
                vf(v, v.tensor_copy(s_qff[:, :], s_qfb[:, :]))
                vf(v, v.reciprocal(s_rqf[:, :], s_qff[:, :]))
                v.tensor_scalar(s_prodb[:, :], s_cf[:, :], s_qff[:, :], None,
                                ALU.mult)
                v.tensor_copy(s_qi[:, :], s_prodb[:, :])
                v.tensor_copy(s_qc[:, :], s_qi[:, :])
                v.tensor_scalar_min(s_qc[:, :], s_qc[:, :], 127.0)
                v.tensor_scalar_max(s_qc[:, :], s_qc[:, :], -128.0)
                v.tensor_scalar(s_dqb[:, :], s_qc[:, :], s_rqf[:, :], None,
                                ALU.mult)
                v.tensor_scalar(s_repb[:, :], s_dqb[:, :], s_meanbf[:, :],
                                None, ALU.add)
                v_c.inc(v.tensor_copy(s_repf[:, :], s_repb[:, :]))

            @blk.gpsimd
            def _(g, wi=wi, nblk=nblk, nft=nft, F=F, natc=natc):
                g.wait_ge(v_c.h, v_c.n)
                g.dma_start(
                    out=bass.AP(rep_flat, 22528 * wi, [[256, 88], [1, 256]]),
                    in_=s_repf[:, :]).then_inc(d_g.h, 16)
                d_g.n += 16
                g.wait_ge(d_g.h, d_g.n)
                for ft in range(nft):
                    rows = min(128, F - 128 * ft)
                    g.dma_start(
                        out=s_repnat[0:rows, natc * ft:natc * ft + nblk],
                        in_=bass.AP(rep_flat, 22528 * wi + 128 * nblk * ft,
                                    [[nblk, rows], [1, nblk]]))\
                        .then_inc(d_g.h, 16)
                    d_g.n += 16
                    g.dma_start(
                        out=s_s1nat[0:rows, natc * ft:natc * ft + nblk],
                        in_=bass.AP(scl_flat, 22528 * wi + 128 * nblk * ft,
                                    [[nblk, rows], [1, nblk]]))\
                        .then_inc(d_g.h, 16)
                    d_g.n += 16

        with nc.Block() as blk:
            v0, a0, m0 = v_c.n, a_c.n, m_c.n
            wr_tot = d_wr[wi].n + 16 * nft

            @blk.sync
            def _(sync, wi=wi, F=F, D=D, nft=nft):
                for ft in range(nft):
                    rows = min(128, F - 128 * ft)
                    sync.dma_start(out=s_wraw[0:rows, D * ft:D * ft + D],
                                   in_=wsrcs[wi][128 * ft:128 * ft + rows, :])\
                        .then_inc(d_wr[wi].h, 16)
                    d_wr[wi].n += 16

            grps_per_ft = (ndc + 3) // 4
            vm = {ft: v0 + ft + 1 for ft in range(nft)}
            tm = {i: m0 + i + 1 for i in range(nft * grps_per_ft)}
            am = {i: a0 + i + 1 for i in range(nft * grps_per_ft)}

            @blk.vector
            def _(v, wi=wi, nblk=nblk, nft=nft, F=F, D=D, natc=natc,
                  wr_tot=wr_tot):
                v.wait_ge(d_g.h, d_g.n)
                v.wait_ge(d_wr[wi].h, wr_tot)
                for ft in range(nft):
                    rows = min(128, F - 128 * ft)
                    for j in range(15):
                        ti = v.tensor_scalar(
                            s_thr[0:rows, nblk * j:nblk * j + nblk],
                            s_s1nat[0:rows, natc * ft:natc * ft + nblk],
                            float(_TAUS[j]), None, ALU.mult)
                    vf(v, ti)
                    v.tensor_copy(s_wf32[0:rows, 0:D],
                                  s_wraw[0:rows, D * ft:D * ft + D])
                    v.memset(s_acc[0:rows, 0:D], float(_LF[0]))
                    for j in range(15):
                        v.tensor_tensor(
                            s_tmp[0:rows, 0:D], s_wf32[0:rows, 0:D],
                            bass.AP(s_thr, nblk * j,
                                    [[1320, rows], [1, nblk], [0, 64]]),
                            ALU.is_gt)
                        v.scalar_tensor_tensor(
                            s_acc[0:rows, 0:D], s_tmp[0:rows, 0:D],
                            float(_DL[j]), s_acc[0:rows, 0:D],
                            ALU.mult, ALU.add)
                    v_c.inc(v.tensor_tensor(
                        s_deqw[0:rows, 0:D], s_acc[0:rows, 0:D],
                        bass.AP(s_repnat, natc * ft,
                                [[192, rows], [1, nblk], [0, 64]]),
                        ALU.mult))

            @blk.tensor
            def _(t, nft=nft, F=F, D=D, ndc=ndc, grps_per_ft=grps_per_ft):
                grp = 0
                for ft in range(nft):
                    rows = min(128, F - 128 * ft)
                    t.wait_ge(v_c.h, vm[ft])
                    for dc0 in range(0, ndc, 4):
                        if grp >= 2:
                            t.wait_ge(a_c.h, am[grp - 2])
                        bk = [p_pt, p_s[0]][grp % 2]
                        nd = min(4, ndc - dc0)
                        for i in range(nd):
                            mm = t.matmul(
                                bk[:, 128 * i:128 * i + rows],
                                s_deqw[0:rows,
                                       128 * (dc0 + i):128 * (dc0 + i) + 128],
                                s_ident[0:rows, 0:rows],
                                start=True, stop=True)
                            if i == nd - 1:
                                m_c.inc(mm)
                        assert m_c.n == tm[grp]
                        grp += 1

            @blk.scalar
            def _(a, wi=wi, nft=nft, F=F, D=D, ndc=ndc,
                  grps_per_ft=grps_per_ft):
                stride = 704 if wi < 2 else 256
                grp = 0
                for ft in range(nft):
                    rows = min(128, F - 128 * ft)
                    for dc0 in range(0, ndc, 4):
                        a.wait_ge(m_c.h, tm[grp])
                        bk = [p_pt, p_s[0]][grp % 2]
                        nd = min(4, ndc - dc0)
                        a_c.inc(a.copy(
                            bass.AP(s_slab, stride * dc0 + 128 * ft,
                                    [[11264, 128], [stride, nd], [1, rows]]),
                            bass.AP(bk, 0, [[512, 128], [128, nd],
                                            [1, rows]])))
                        assert a_c.n == am[grp]
                        grp += 1

            @blk.gpsimd
            def _(g, wi=wi, D=D, vm=vm):
                if DEBUG_TAPS and wi == 0:
                    g.wait_ge(v_c.h, vm[0])
                    g.dma_start(out=dbg_deqw[:, 0:2048],
                                in_=s_deqw[:, 0:2048]).then_inc(d_g.h, 16)
                    d_g.n += 16
                    g.dma_start(out=dbg_acc[:, 0:2048],
                                in_=s_acc[:, 0:2048]).then_inc(d_g.h, 16)
                    d_g.n += 16
                g.wait_ge(a_c.h, a_c.n)
                stride = 704 if wi < 2 else 256
                for dc in range(D // 128):
                    g.dma_start(
                        out=ag_in[wi][128 * dc:128 * dc + 128, 0:stride],
                        in_=s_slab[:, stride * dc:stride * dc + stride])\
                        .then_inc(d_g.h, 16)
                    d_g.n += 16
                g.wait_ge(d_g.h, d_g.n)
                g.collective_compute(
                    "AllGather", ALU.bypass, replica_groups=WORLD,
                    ins=[ag_in[wi].ap().opt()],
                    outs=[ag_out[wi].ap().opt()])\
                    .then_inc(c_ag[wi].h, 1)

    des.close()

    if MAX_PHASE < 2:
        with nc.Block() as blk:
            @blk.gpsimd
            def _(g_):
                for i in range(3):
                    g_.wait_ge(c_ag[i].h, 1)
                g_.dma_start(out=out[:, :], in_=y1_scr[:, :])\
                    .then_inc(d_g.h, 16)
                d_g.n += 16
                g_.wait_ge(d_g.h, d_g.n)
        ges.close()
        return nc
    aes = ExitStack()
    s_xT = sb(aes, "s_xT", [128, 16 * N], BF16)
    s_P = sb(aes, "s_P", [128, 3 * N], BF16)
    s_kT = sb(aes, "s_kT", [128, N], BF16)
    s_qT = sb(aes, "s_qT", [128, N], BF16)
    s_v = sb(aes, "s_v", [128, N], BF16)
    s_wh = sb(aes, "s_wh", [128, 3 * EMB], BF16)
    s_stg = sb(aes, "s_stg", [128, 2048], F32)
    s_pt = sb(aes, "s_pt", [128, 4096], BF16)
    s_oacc = sb(aes, "s_oacc", [128, 256], F32)
    OT0 = 2 * N

    with nc.Block() as blk:
        v0 = v_c.n
        xm = {i: v0 + i + 1 for i in range(64)}

        @blk.sync
        def _(sync):
            for i in range(64):
                k, qf = i // 4, i % 4
                if i >= 2:
                    sync.wait_ge(v_c.h, xm[i - 2])
                sem = d_xl if i % 2 == 0 else d_xl2
                sync.dma_start(out=s_stg[:, 1024 * (i % 2):
                                         1024 * (i % 2) + 1024],
                               in_=xT[128 * k:128 * k + 128,
                                      1024 * qf:1024 * qf + 1024])\
                    .then_inc(sem.h, 16)
                sem.n += 16

        @blk.vector
        def _(v):
            for i in range(64):
                k, qf = i // 4, i % 4
                sem = d_xl if i % 2 == 0 else d_xl2
                v.wait_ge(sem.h, 16 * (i // 2 + 1))
                v_c.inc(v.tensor_copy(
                    s_xT[:, N * k + 1024 * qf:N * k + 1024 * qf + 1024],
                    s_stg[:, 1024 * (i % 2):1024 * (i % 2) + 1024]))

    if MAX_PHASE < 2.5:
        with nc.Block() as blk:
            @blk.gpsimd
            def _(g_):
                for i in range(3):
                    g_.wait_ge(c_ag[i].h, 1)
                g_.dma_start(out=out[:, :], in_=y1_scr[:, :])\
                    .then_inc(d_g.h, 16)
                d_g.n += 16
                g_.wait_ge(d_g.h, d_g.n)
        aes.close()
        ges.close()
        return nc

    with nc.Block() as blk:
        SC = attn_schedule(v_c.n, a_c.n, m_c.n, d_g.n)
        wdc, pj, pje = SC["wdc"], SC["pj"], SC["pje"]
        smm, cmk, expm, norm = SC["smm"], SC["cmk"], SC["expm"], SC["norm"]
        ptm, pte, avm, oev = SC["ptm"], SC["pte"], SC["avm"], SC["oev"]
        odma, oad = SC["odma"], SC["oad"]
        exp_seq, exp_pos = SC["exp_seq"], SC["exp_pos"]

        def wait_oev(eng, key):
            kind, val = oev[key]
            eng.wait_ge(a_c.h if kind == 'a' else v_c.h, val)

        @blk.sync
        def _(sync):
            wts = [wqT, wkT, wvT]
            for h in range(ATTN_HEADS):
                for w in range(3):
                    idx = h * 3 + w
                    if idx >= 1:
                        pidx = idx - 1
                        sync.wait_ge(v_c.h, wdc[(pidx // 3, pidx % 3)])
                    sync.dma_start(
                        out=bass.AP(s_stg, 0,
                                    [[2048, 128], [128, 16], [1, 128]]),
                        in_=bass.AP(wts[w], 128 * h,
                                    [[1024, 128], [131072, 16], [1, 128]]))\
                        .then_inc(d_wh.h, 16)
                    d_wh.n += 16

        @blk.vector
        def _(v):
            for h in range(ATTN_HEADS):
                for w in range(3):
                    idx = h * 3 + w
                    v.wait_ge(d_wh.h, 16 * (idx + 1))
                    if h > 0:
                        v.wait_ge(m_c.h, pj[(h - 1, 2, 7)])
                    v_c.inc(v.tensor_copy(
                        s_wh[:, EMB * w:EMB * w + EMB],
                        s_stg[:, :]))
                    assert v_c.n == wdc[(h, w)]
                if ATTN_STOP == 'proj':
                    continue

                def emit_oad(g):
                    nki = 2 * g + 2
                    v.wait_ge(m_c.h, avm[(h, g, 15)])
                    v_c.inc(v.tensor_copy(s_oacc[:, :], p_o[:, 0:256]))
                    assert v_c.n == oad[(h, g, 0)]
                    v.wait_ge(m_c.h, avm[(h, g, nki - 1)])
                    v_c.inc(v.tensor_tensor(s_oacc[:, :], s_oacc[:, :],
                                            p_o[:, 0:256], ALU.add))
                    assert v_c.n == oad[(h, g, 1)]
                    v_c.inc(v.tensor_copy(
                        s_P[:, OT0 + 256 * g:OT0 + 256 * g + 256],
                        s_oacc[:, :]))
                    assert ('v', v_c.n) == oev[(h, g)]

                for g in range(ATTN_GROUPS):
                    if (ATTN_STOP == 'full' and g >= 1
                            and 2 * (g - 1) + 2 > 16):
                        emit_oad(g - 1)
                    for qc in (2 * g, 2 * g + 1):
                        sgs = _segs(qc)
                        lsg = len(sgs) - 1
                        kb, wd = sgs[lsg]
                        bank = p_s[exp_pos[(h, qc, lsg)] % 4]
                        v.wait_ge(m_c.h, smm[(h, qc, lsg)])
                        v_c.inc(v.tensor_tensor(bank[0:128, wd - 128:wd],
                                                bank[0:128, wd - 128:wd],
                                                s_cmask[:, :], ALU.add))
                        assert v_c.n == cmk[(h, qc)]
                        v.wait_ge(a_c.h, expm[(h, qc, lsg)])
                        zoff = 8 * (qc % 2)
                        vf(v, v.tensor_reduce(s_z[:, :],
                                              s_zb[:, zoff:zoff + len(sgs)],
                                              AX.X, ALU.add))
                        vf(v, v.reciprocal(s_zr[:, :], s_z[:, :]))
                        slot = qc % 2
                        kr = 128 * (qc + 1)
                        v_c.inc(v.tensor_scalar(
                            s_P[:, N * slot:N * slot + kr],
                            s_P[:, N * slot:N * slot + kr],
                            s_zr[:, :], None, ALU.mult))
                        assert v_c.n == norm[(h, qc)]
                if ATTN_STOP == 'smax':
                    continue
                if 2 * (ATTN_GROUPS - 1) + 2 > 16:
                    emit_oad(ATTN_GROUPS - 1)

        @blk.tensor
        def _(t):
            pjseq = []
            for h in range(ATTN_HEADS):
                for w in (1, 0):
                    for tb in range(8):
                        pjseq.append((h, w, tb))
                for kb4 in range(8):
                    pjseq.append((h, 2, kb4))
            pjpos = {k: i for i, k in enumerate(pjseq)}

            for h in range(ATTN_HEADS):
                for w, dst in ((1, s_kT), (0, s_qT)):
                    t.wait_ge(v_c.h, wdc[(h, w)])
                    for tb in range(8):
                        key = (h, w, tb)
                        si = pjpos[key]
                        if si >= 2:
                            t.wait_ge(a_c.h, pje[pjseq[si - 2]])
                        for di in range(16):
                            mm = t.matmul(
                                p_pj[si % 2][:, :],
                                s_wh[:, EMB * w + 128 * di:
                                     EMB * w + 128 * di + 128],
                                s_xT[:, N * di + 512 * tb:
                                     N * di + 512 * tb + 512],
                                start=(di == 0), stop=(di == 15))
                        m_c.inc(mm)
                        assert m_c.n == pj[key]
                t.wait_ge(v_c.h, wdc[(h, 2)])
                for kb4 in range(8):
                    key = (h, 2, kb4)
                    si = pjpos[key]
                    if si >= 2:
                        t.wait_ge(a_c.h, pje[pjseq[si - 2]])
                    for sub in range(4):
                        ki = 4 * kb4 + sub
                        for di in range(16):
                            mm = t.matmul(
                                p_pj[si % 2][:, 128 * sub:128 * sub + 128],
                                s_xT[:, N * di + 128 * ki:
                                     N * di + 128 * ki + 128],
                                s_wh[:, 2 * EMB + 128 * di:
                                     2 * EMB + 128 * di + 128],
                                start=(di == 0), stop=(di == 15))
                    m_c.inc(mm)
                    assert m_c.n == pj[key]
                if ATTN_STOP == 'proj':
                    continue
                t.wait_ge(a_c.h, pje[(h, 2, 7)])
                for g in range(ATTN_GROUPS):
                    qc0, qc1 = 2 * g, 2 * g + 1
                    for qc in (qc0, qc1):
                        for sg, (kb, wd) in enumerate(_segs(qc)):
                            gi = exp_pos[(h, qc, sg)]
                            if gi >= 4:
                                t.wait_ge(a_c.h, expm[exp_seq[gi - 4]])
                            bank = p_s[gi % 4]
                            m_c.inc(t.matmul(
                                bank[0:128, 0:wd],
                                s_qT[:, 128 * qc:128 * qc + 128],
                                s_kT[:, kb:kb + wd],
                                start=True, stop=True))
                            assert m_c.n == smm[(h, qc, sg)]
                    if ATTN_STOP == 'smax':
                        continue
                    nki = qc1 + 1
                    t.wait_ge(v_c.h, norm[(h, qc1)])
                    for sub0 in range(0, nki, 16):
                        sks = list(range(sub0, min(sub0 + 16, nki)))
                        for ki in sks:
                            both = ki <= qc0
                            if ki >= 2:
                                t.wait_ge(a_c.h, pte[(h, g, ki - 2)])
                            bk = p_pj[ki % 2]
                            if both:
                                t.matmul(bk[:, 0:128],
                                         s_P[:, 128 * ki:128 * ki + 128],
                                         s_ident[:, :],
                                         start=True, stop=True)
                            m_c.inc(t.matmul(
                                bk[:, 128:256],
                                s_P[:, N + 128 * ki:N + 128 * ki + 128],
                                s_ident[:, :], start=True, stop=True))
                            assert m_c.n == ptm[(h, g, ki)]
                        t.wait_ge(a_c.h, pte[(h, g, sks[-1])])
                        if sub0 == 0:
                            if (h, g) != (0, 0):
                                prev = (h, g - 1) if g > 0 else (h - 1, ATTN_GROUPS - 1)
                                wait_oev(t, prev)
                        else:
                            t.wait_ge(v_c.h, oad[(h, g, 0)])
                        for ki in sks:
                            both = ki <= qc0
                            slot = 256 * (ki % 16)
                            if both:
                                mm = t.matmul(
                                    p_o[:, 0:256],
                                    s_v[:, 128 * ki:128 * ki + 128],
                                    s_pt[:, slot:slot + 256],
                                    start=(ki == sks[0]),
                                    stop=(ki == sks[-1]))
                            else:
                                mm = t.matmul(
                                    p_o[:, 128:256],
                                    s_v[:, 128 * ki:128 * ki + 128],
                                    s_pt[:, slot + 128:slot + 256],
                                    start=(ki == sks[0]),
                                    stop=(ki == sks[-1]))
                            m_c.inc(mm)
                            assert m_c.n == avm[(h, g, ki)]

        @blk.scalar
        def _(a):
            for h in range(ATTN_HEADS):
                for w, dst in ((1, s_kT), (0, s_qT)):
                    for tb in range(8):
                        key = (h, w, tb)
                        a.wait_ge(m_c.h, pj[key])
                        if w == 0:
                            bias, scale = s_bqs[:, h:h + 1], QSCALE
                        else:
                            bias, scale = s_bqkv[:, 3 * h + 1:3 * h + 2], 1.0
                        si = h * 24 + (0 if w == 1 else 8) + tb
                        a_c.inc(a.activation(
                            dst[:, 512 * tb:512 * tb + 512],
                            p_pj[si % 2][:, :],
                            AF.Identity, bias=bias, scale=scale))
                        assert a_c.n == pje[key]
                for kb4 in range(8):
                    key = (h, 2, kb4)
                    a.wait_ge(m_c.h, pj[key])
                    a_c.inc(a.copy(s_v[:, 512 * kb4:512 * kb4 + 512],
                                   p_pj[(h * 24 + 16 + kb4) % 2][:, :]))
                    assert a_c.n == pje[key]
                if ATTN_STOP == 'proj':
                    continue
                for g in range(ATTN_GROUPS):
                    qc0, qc1 = 2 * g, 2 * g + 1
                    for qc in (qc0, qc1):
                        sgs = _segs(qc)
                        lsg = len(sgs) - 1
                        slot = qc % 2
                        if qc >= 2:
                            a.wait_ge(v_c.h, norm[(h, qc - 2)])
                        elif h > 0:
                            a.wait_ge(v_c.h, norm[(h - 1, 30 + qc)])
                        if ATTN_STOP == 'full':
                            if qc >= 2:
                                gp = (qc - 2) // 2
                                a.wait_ge(m_c.h, ptm[(h, gp, 2 * gp + 1)])
                            elif h > 0:
                                a.wait_ge(m_c.h, ptm[(h - 1, 15, 31)])
                        for sg, (kb, wd) in enumerate(sgs):
                            bank = p_s[exp_pos[(h, qc, sg)] % 4]
                            if sg == lsg:
                                a.wait_ge(v_c.h, cmk[(h, qc)])
                            else:
                                a.wait_ge(m_c.h, smm[(h, qc, sg)])
                            a_c.inc(a.activation(
                                s_P[:, N * slot + kb:N * slot + kb + wd],
                                bank[0:128, 0:wd], AF.Exp,
                                bias=s_esh[:, :], scale=1.0,
                                accum_out=s_zb[:, 8 * (qc % 2) + sg:
                                               8 * (qc % 2) + sg + 1]))
                            assert a_c.n == expm[(h, qc, sg)]
                    if ATTN_STOP == 'smax':
                        continue
                    nki = qc1 + 1
                    for ki in range(nki):
                        both = ki <= qc0
                        a.wait_ge(m_c.h, ptm[(h, g, ki)])
                        if ki >= 16:
                            a.wait_ge(m_c.h, avm[(h, g, ki - 16)])
                        bk = p_pj[ki % 2]
                        slot = 256 * (ki % 16)
                        lo = 0 if both else 128
                        a_c.inc(a.copy(s_pt[:, slot + lo:slot + 256],
                                       bk[:, lo:256]))
                        assert a_c.n == pte[(h, g, ki)]
                    if nki <= 16:
                        a.wait_ge(m_c.h, avm[(h, g, nki - 1)])
                        if g == 0 and h > 0:
                            a.wait_ge(d_g.h, odma[h - 1])
                        a_c.inc(a.copy(
                            s_P[:, OT0 + 256 * g:OT0 + 256 * g + 256],
                            p_o[:, 0:256]))
                        assert ('a', a_c.n) == oev[(h, g)]

        @blk.gpsimd
        def _(g_):
            if ATTN_STOP != 'full':
                return
            for h in range(ATTN_HEADS):
                wait_oev(g_, (h, ATTN_GROUPS - 1))
                g_.dma_start(out=ago_in[128 * h:128 * h + 128, :],
                             in_=s_P[:, OT0:OT0 + N]).then_inc(d_g.h, 16)
                d_g.n += 16
                assert d_g.n == odma[h]
                if h % 2 == 1:
                    g_.wait_ge(d_g.h, d_g.n)
                    k = h // 2
                    g_.collective_compute(
                        "AllGather", ALU.bypass, replica_groups=PAIRS,
                        ins=[ago_in[256 * k:256 * k + 256, :].opt()],
                        outs=[ago_out[k].ap().opt()])\
                        .then_inc(c_ao[k].h, 1)

    aes.close()

    if MAX_PHASE < 3:
        with nc.Block() as blk:
            @blk.gpsimd
            def _(g_):
                for k in range(ATTN_HEADS // 2):
                    g_.wait_ge(c_ao[k].h, 1)
                g_.dma_start(out=out[:, :], in_=y1_scr[:, :])\
                    .then_inc(d_g.h, 16)
                d_g.n += 16
                g_.wait_ge(d_g.h, d_g.n)
        ges.close()
        return nc
    wes = ExitStack()
    s_y1T = sb(wes, "s_y1T", [128, 16 * NT], BF16)
    w2 = ExitStack()
    s_woT = sb(w2, "s_woT", [128, 16 * EMB], BF16)
    s_oTq = sb(w2, "s_oTq", [128, 8192], BF16)
    s_aoT = sb(w2, "s_aoT", [128, 8192], BF16)
    s_nat = sb(w2, "s_nat", [128, EMB], F32)
    s_y1b = sb(w2, "s_y1b", [128, EMB], BF16)
    s_bot = sb(w2, "s_bot", [128, EMB], F32)
    s_stgw = sb(w2, "s_stgw", [128, 2 * EMB], F32)
    s_bvc = sb(w2, "s_bvc", [128, 16], BF16)
    s_bv16 = sb(w2, "s_bv16", [16, 128], F32)
    s_bv16b = sb(w2, "s_bv16b", [16, 128], BF16)
    s_onb = sb(w2, "s_onb", [1, 128], BF16)
    s_hi = sb(w2, "s_hi", [1, EMB], BF16)
    s_lob = sb(w2, "s_lob", [1, EMB], BF16)

    def colof(ci):
        if ci < 8:
            k, s = ci // 2, ci % 2
        else:
            k, s = (ci - 8) // 2, 2 + (ci - 8) % 2
        return 512 * (4 * k + s)

    with nc.Block() as blk:
        v0 = v_c.n
        ld0 = d_x.n
        wdcm = {ci: v0 + ci + 2 for ci in range(16)}

        @blk.sync
        def _(sync):
            sync.dma_start(out=s_stgw[0:1, EMB:2 * EMB], in_=bo_in[:, :]).then_inc(d_c.h, 16)
            sync.dma_start(out=s_bv16[:, :], in_=bvfull[:, :])\
                .then_inc(d_c.h, 16)
            d_c.n += 32
            for ci in range(16):
                if ci >= 1:
                    sync.wait_ge(v_c.h, wdcm[ci - 1])
                sync.dma_start(
                    out=s_stgw[:, 0:EMB],
                    in_=woT[128 * ci:128 * ci + 128, :]).then_inc(d_wo.h, 16)
                d_wo.n += 16

        @blk.vector
        def _(v):
            v.memset(s_onb[:, :], 1.0)
            v.wait_ge(d_c.h, d_c.n)
            v_c.inc(v.tensor_copy(s_bv16b[:, :], s_bv16[:, :]))
            for ci in range(16):
                v.wait_ge(d_wo.h, 16 * (ci + 1))
                v_c.inc(v.tensor_copy(
                    s_woT[:, EMB * ci:EMB * ci + EMB],
                    s_stgw[:, 0:EMB]))
                assert v_c.n == wdcm[ci]

        bt_m0 = m_c.n
        bt_a0 = a_c.n

        @blk.tensor
        def _(t):
            t.wait_ge(v_c.h, v0 + 1)
            m_c.inc(t.matmul(p_pt[:, 0:16], s_bv16b[:, :],
                             s_ident[0:16, 0:16], start=True, stop=True))

        @blk.scalar
        def _(a):
            a.wait_ge(m_c.h, bt_m0 + 1)
            a_c.inc(a.copy(s_bvc[:, :], p_pt[:, 0:16]))

        m0 = m_c.n
        a0 = a_c.n

        @blk.tensor
        def _(t):
            t.wait_ge(v_c.h, wdcm[15])
            t.wait_ge(a_c.h, bt_a0 + 1)
            t.wait_ge(d_c.h, d_c.n)
            for j in range(4):
                if j >= 1:
                    t.wait_ge(a_c.h, a0 + j)
                for ci in range(16):
                    mm = t.matmul(
                        p_pj[0][0:1, :], s_bvc[:, ci:ci + 1],
                        s_woT[:, EMB * ci + 512 * j:EMB * ci + 512 * j + 512],
                        start=(ci == 0), stop=(ci == 15))
                m_c.inc(mm)

        @blk.scalar
        def _(a):
            for j in range(4):
                a.wait_ge(m_c.h, m0 + j + 1)
                a_c.inc(a.copy(s_bot[0:1, 512 * j:512 * j + 512],
                               p_pj[0][0:1, :]))

        @blk.vector
        def _(v):
            v.wait_ge(a_c.h, a0 + 4)
            v.wait_ge(d_c.h, d_c.n)
            vf(v, v.tensor_add(s_stgw[0:1, EMB:2 * EMB],
                               s_stgw[0:1, EMB:2 * EMB], s_bot[0:1, :]))
            vf(v, v.tensor_copy(s_hi[:, :], s_stgw[0:1, EMB:2 * EMB]))
            vf(v, v.tensor_copy(s_nat[0:1, 0:EMB], s_hi[:, :]))
            v_c.inc(v.tensor_tensor(s_lob[:, :], s_stgw[0:1, EMB:2 * EMB],
                                    s_nat[0:1, 0:EMB], ALU.subtract))

        bc_v0 = v_c.n
        bc_m0 = m_c.n
        bc_a0 = a_c.n

        @blk.tensor
        def _(t):
            t.wait_ge(v_c.h, bc_v0)
            for j in range(4):
                if j >= 1:
                    t.wait_ge(a_c.h, bc_a0 + j)
                t.matmul(p_pj[1][:, :], s_onb[:, :],
                         s_hi[0:1, 512 * j:512 * j + 512],
                         start=True, stop=False)
                m_c.inc(t.matmul(p_pj[1][:, :], s_onb[:, :],
                                 s_lob[0:1, 512 * j:512 * j + 512],
                                 start=False, stop=True))

        @blk.scalar
        def _(a):
            for j in range(4):
                a.wait_ge(m_c.h, bc_m0 + j + 1)
                a_c.inc(a.copy(s_bot[:, 512 * j:512 * j + 512],
                               p_pj[1][:, :]))

    with nc.Block() as blk:
        g0 = d_g.n
        m0 = m_c.n
        a0 = a_c.n
        v0 = v_c.n
        x0 = d_x.n
        gdma = {qt: g0 + 128 * qt + 64 for qt in range(4)}
        gq = gdma
        wom, woe, lns = {}, {}, {}
        trm, y1m, tre, y1e = {}, {}, {}, {}
        tg_of = {}
        tgrp_e = {}
        m, a = m0, a0
        tg = 0
        for qt in range(4):
            for do in range(16):
                m += 1
                wom[(qt, do)] = m
            for tt in range(4):
                for gg in range(4):
                    m += 1
                    trm[(qt, tt, gg)] = m
                    tg_of[(qt, tt, 0, gg)] = tg
                    tg += 1
                for gg in range(4):
                    m += 1
                    y1m[(qt, tt, gg)] = m
                    tg_of[(qt, tt, 1, gg)] = tg
                    tg += 1
        for qt in range(4):
            for do in range(16):
                a += 1
                woe[(qt, do)] = a
            for tt in range(4):
                for gg in range(4):
                    a += 1
                    tre[(qt, tt, gg)] = a
                    tgrp_e[tg_of[(qt, tt, 0, gg)]] = a
                a += 1
                lns[(qt, tt)] = a
                for gg in range(4):
                    a += 1
                    y1e[(qt, tt, gg)] = a
                    tgrp_e[tg_of[(qt, tt, 1, gg)]] = a
        v = v0
        lnv, lnm, y1bm = {}, {}, {}
        for qt in range(4):
            for tt in range(4):
                v += 1
                lnv[(qt, tt)] = v
                v += 1
                lnm[(qt, tt)] = v
                v += 1
                y1bm[(qt, tt)] = v
        xrm = {}
        x = 0
        for qt in range(4):
            for tt in range(4):
                x += 16
                xrm[(qt, tt)] = x
        y1d = {}
        for qt in range(4):
            for tt in range(4):
                y1d[(qt, tt)] = g0 + 128 * qt + 64 + 16 * (tt + 1)
        tbank = [p_pt, p_s[0]]

        @blk.gpsimd
        def _(g_):
            for k in range(4):
                g_.wait_ge(c_ao[k].h, 1)
            with g_.register("r_t") as r_t:
                for qt in range(4):
                    if qt >= 1:
                        g_.wait_ge(m_c.h, wom[(qt - 1, 15)])
                    g_.reg_load(r_t, s_offs[0:1, qt:qt + 1])
                    for k in range(4):
                        src = bass.AP(ago_out[k], r_t,
                                      [[N, 128], [128 * N, 4], [1, 512]])
                        dst = bass.AP(s_oTq, 2048 * k,
                                      [[8192, 128], [512, 4], [1, 512]])
                        g_.dma_start(out=dst, in_=src).then_inc(d_g.h, 16)
                        d_g.n += 16
                    assert d_g.n == gdma[qt]
                    for tt in range(4):
                        g_.wait_ge(v_c.h, lnm[(qt, tt)])
                        row = 512 * qt + 128 * tt
                        g_.dma_start(out=y1_scr[row:row + 128, :],
                                     in_=s_nat[:, :]).then_inc(d_g.h, 16)
                        d_g.n += 16
                        assert d_g.n == y1d[(qt, tt)]
            g_.wait_ge(d_g.h, d_g.n)

        @blk.sync
        def _(sync):
            for qt in range(4):
                for tt in range(4):
                    if (qt, tt) != (0, 0):
                        pt_ = (qt, tt - 1) if tt > 0 else (qt - 1, 3)
                        sync.wait_ge(v_c.h, lnm[pt_])
                    else:
                        sync.wait_ge(v_c.h, wdcm[15])
                    row = 512 * qt + 128 * tt
                    sync.dma_start(out=s_stgw[:, 0:EMB],
                                   in_=x_res[row:row + 128, :])\
                        .then_inc(d_xr.h, 16)
                    d_xr.n += 16
                    assert d_xr.n == xrm[(qt, tt)]

        @blk.tensor
        def _(t):
            for qt in range(4):
                t.wait_ge(d_g.h, gq[qt])
                for do in range(16):
                    if do >= 2:
                        t.wait_ge(a_c.h, woe[(qt, do - 2)])
                    for ci in range(16):
                        mm = t.matmul(
                            p_pj[do % 2][:, :],
                            s_woT[:, EMB * ci + 128 * do:
                                  EMB * ci + 128 * do + 128],
                            s_oTq[:, colof(ci):colof(ci) + 512],
                            start=(ci == 0), stop=(ci == 15))
                    m_c.inc(mm)
                    assert m_c.n == wom[(qt, do)]
                t.wait_ge(a_c.h, woe[(qt, 15)])
                for tt in range(4):
                    for gg in range(4):
                        ti = tg_of[(qt, tt, 0, gg)]
                        if ti >= 2:
                            t.wait_ge(a_c.h, tgrp_e[ti - 2])
                        bk = tbank[ti % 2]
                        for i in range(4):
                            do = 4 * gg + i
                            mm = t.matmul(
                                bk[:, 128 * i:128 * i + 128],
                                s_aoT[:, 512 * do + 128 * tt:
                                      512 * do + 128 * tt + 128],
                                s_ident[:, :], start=True, stop=True)
                        m_c.inc(mm)
                        assert m_c.n == trm[(qt, tt, gg)]
                    t.wait_ge(v_c.h, y1bm[(qt, tt)])
                    for gg in range(4):
                        ti = tg_of[(qt, tt, 1, gg)]
                        if ti >= 2:
                            t.wait_ge(a_c.h, tgrp_e[ti - 2])
                        bk = tbank[ti % 2]
                        for i in range(4):
                            dc = 4 * gg + i
                            mm = t.matmul(
                                bk[:, 128 * i:128 * i + 128],
                                s_y1b[:, 128 * dc:128 * dc + 128],
                                s_ident[:, :], start=True, stop=True)
                        m_c.inc(mm)
                        assert m_c.n == y1m[(qt, tt, gg)]

        @blk.scalar
        def _(a):
            for qt in range(4):
                for do in range(16):
                    a.wait_ge(m_c.h, wom[(qt, do)])
                    a_c.inc(a.copy(s_aoT[:, 512 * do:512 * do + 512],
                                   p_pj[do % 2][:, :]))
                    assert a_c.n == woe[(qt, do)]
                for tt in range(4):
                    if (qt, tt) != (0, 0):
                        pv = (qt, tt - 1) if tt > 0 else (qt - 1, 3)
                        a.wait_ge(v_c.h, y1bm[pv])
                        a.wait_ge(d_g.h, y1d[pv])
                    for gg in range(4):
                        a.wait_ge(m_c.h, trm[(qt, tt, gg)])
                        bk = tbank[tg_of[(qt, tt, 0, gg)] % 2]
                        a_c.inc(a.copy(s_nat[:, 512 * gg:512 * gg + 512],
                                       bk[:, :]))
                        assert a_c.n == tre[(qt, tt, gg)]
                    a.wait_ge(v_c.h, lnv[(qt, tt)])
                    a_c.inc(a.activation(s_sd[:, :], s_v1[:, :], AF.Sqrt,
                                         bias=s_lneps[:, :],
                                         scale=1.0 / EMB))
                    assert a_c.n == lns[(qt, tt)]
                    for gg in range(4):
                        a.wait_ge(m_c.h, y1m[(qt, tt, gg)])
                        bk = tbank[tg_of[(qt, tt, 1, gg)] % 2]
                        row = 512 * qt + 128 * tt
                        a_c.inc(a.copy(
                            bass.AP(s_y1T, (4 * gg) * NT + row,
                                    [[16 * NT, 128], [NT, 4], [1, 128]]),
                            bass.AP(bk, 0, [[512, 128], [128, 4],
                                            [1, 128]])))
                        assert a_c.n == y1e[(qt, tt, gg)]

        @blk.vector
        def _(v):
            for qt in range(4):
                for tt in range(4):
                    for gg in range(4):
                        v.wait_ge(a_c.h, tre[(qt, tt, gg)])
                    v.wait_ge(d_xr.h, xrm[(qt, tt)])
                    if (qt, tt) != (0, 0):
                        pv = (qt, tt - 1) if tt > 0 else (qt - 1, 3)
                        v.wait_ge(d_g.h, y1d[pv])
                    v.tensor_add(s_nat[:, :], s_nat[:, :], s_bot[:, :])
                    v.tensor_add(s_nat[:, :], s_nat[:, :], s_stgw[:, 0:EMB])
                    vf(v, v.tensor_reduce(s_m1[:, :], s_nat[:, :], AX.X,
                                          ALU.add))
                    vf(v, v.tensor_scalar(s_m1[:, :], s_m1[:, :], 1.0 / EMB,
                                          None, ALU.mult))
                    v.tensor_scalar(s_nat[:, :], s_nat[:, :], s_m1[:, :],
                                    None, ALU.subtract)
                    v.tensor_tensor(s_stgw[:, EMB:2 * EMB], s_nat[:, :],
                                    s_nat[:, :], ALU.mult)
                    v_c.inc(v.tensor_reduce(s_v1[:, :],
                                            s_stgw[:, EMB:2 * EMB],
                                            AX.X, ALU.add))
                    assert v_c.n == lnv[(qt, tt)]
                    v.wait_ge(a_c.h, lns[(qt, tt)])
                    vf(v, v.reciprocal(s_r1[:, :], s_sd[:, :]))
                    v_c.inc(v.tensor_scalar(s_nat[:, :], s_nat[:, :],
                                            s_r1[:, :], None, ALU.mult))
                    assert v_c.n == lnm[(qt, tt)]
                    if (qt, tt) != (0, 0):
                        pv = (qt, tt - 1) if tt > 0 else (qt - 1, 3)
                        v.wait_ge(m_c.h, y1m[(pv[0], pv[1], 3)])
                    v_c.inc(v.tensor_copy(s_y1b[:, :], s_nat[:, :]))
                    assert v_c.n == y1bm[(qt, tt)]

    w2.close()

    if MAX_PHASE < 4:
        with nc.Block() as blk:
            @blk.gpsimd
            def _(g_):
                g_.dma_start(out=out[:, :], in_=y1_scr[:, :])\
                    .then_inc(d_g.h, 16)
                d_g.n += 16
                g_.wait_ge(d_g.h, d_g.n)
        wes.close()
        ges.close()
        return nc
    fes = ExitStack()
    s_hT = sb(fes, "s_hT", [128, 44 * 512], BF16)
    s_sl1 = sb(fes, "s_sl1", [128, 2 * 16 * 256], BF16)
    s_sl2 = sb(fes, "s_sl2", [128, 2 * 16 * 256], BF16)
    s_w3s = sb(fes, "s_w3s", [128, 2 * 44 * 128], BF16)
    s_foT = sb(fes, "s_foT", [128, 16 * 512], BF16)
    s_silu = sb(fes, "s_silu", [128, 512], BF16)
    s_natf = sb(fes, "s_natf", [128, EMB], F32)
    s_sqf = sb(fes, "s_sqf", [128, EMB], F32)
    s_stgf = sb(fes, "s_stgf", [128, EMB], F32)

    with nc.Block() as blk:
        @blk.gpsimd
        def _(g_):
            for i in range(3):
                g_.wait_ge(c_ag[i].h, 1)

    for tc in range(4):
        with nc.Block() as blk:
            t0 = 512 * tc
            m0, a0, v0, x0, g0 = m_c.n, a_c.n, v_c.n, d_x.n, d_g.n
            abm = {}
            m = m0
            for fc in range(22):
                for ft in range(2):
                    m += 1
                    abm[(fc, ft, 0)] = m
                    m += 1
                    abm[(fc, ft, 1)] = m
            outm = {}
            for do in range(16):
                m += 1
                outm[do] = m
            trm2 = {}
            tg = 0
            tg2_of = {}
            for tt in range(4):
                for gg in range(4):
                    m += 1
                    trm2[(tt, gg)] = m
                    tg2_of[(tt, gg)] = tg
                    tg += 1
            a = a0
            silm = {}
            for fc in range(22):
                for ft in range(2):
                    a += 1
                    silm[(fc, ft)] = a
            oute = {}
            for do in range(16):
                a += 1
                oute[do] = a
            tre2 = {}
            tg2_e = {}
            lns2 = {}
            for tt in range(4):
                for gg in range(4):
                    a += 1
                    tre2[(tt, gg)] = a
                    tg2_e[tg2_of[(tt, gg)]] = a
                a += 1
                lns2[tt] = a
            v = v0
            hm = {}
            for fc in range(22):
                for ft in range(2):
                    v += 1
                    hm[(fc, ft)] = v
            lnv2, ln2m = {}, {}
            for tt in range(4):
                v += 1
                lnv2[tt] = v
                v += 1
                ln2m[tt] = v
            slm = {}
            w3m = {}
            resm = {}
            outd = {}
            g = g0
            for tt in range(4):
                g += 16
                outd[tt] = g
            tbank2 = [p_pt, p_s[0]]

            @blk.sync
            def _(sync, tc=tc):
                for fc in range(22):
                    if fc >= 2:
                        sync.wait_ge(m_c.h, abm[(fc - 2, 1, 1)])
                    f0 = 256 * fc
                    r0, r1 = f0 // 704, (f0 + 255) // 704
                    half = 4096 * (fc % 2)
                    dsem = d_sl[fc % 2]
                    for wsl, agt in ((s_sl1, ag_out[0]), (s_sl2, ag_out[1])):
                        if r0 == r1:
                            fl = f0 - 704 * r0
                            sync.dma_start(
                                out=bass.AP(wsl, half,
                                            [[8192, 128], [256, 16],
                                             [1, 256]]),
                                in_=bass.AP(agt, (2048 * r0) * 704 + fl,
                                            [[704, 128], [704 * 128, 16],
                                             [1, 256]]))\
                                .then_inc(dsem.h, 16)
                            dsem.n += 16
                        else:
                            w0 = 704 * r1 - f0
                            fl = f0 - 704 * r0
                            sync.dma_start(
                                out=bass.AP(wsl, half,
                                            [[8192, 128], [256, 16],
                                             [1, w0]]),
                                in_=bass.AP(agt, (2048 * r0) * 704 + fl,
                                            [[704, 128], [704 * 128, 16],
                                             [1, w0]]))\
                                .then_inc(dsem.h, 16)
                            dsem.n += 16
                            sync.dma_start(
                                out=bass.AP(wsl, half + w0,
                                            [[8192, 128], [256, 16],
                                             [1, 256 - w0]]),
                                in_=bass.AP(agt, (2048 * r1) * 704,
                                            [[704, 128], [704 * 128, 16],
                                             [1, 256 - w0]]))\
                                .then_inc(dsem.h, 16)
                            dsem.n += 16
                    slm[fc] = (fc % 2, dsem.n)
                for do in range(16):
                    if do >= 2:
                        sync.wait_ge(m_c.h, outm[do - 2])
                    r, hf = do // 2, do % 2
                    sync.dma_start(
                        out=bass.AP(s_w3s, 5632 * (do % 2),
                                    [[11264, 128], [128, 44], [1, 128]]),
                        in_=bass.AP(ag_out[2], (5632 * r) * 256 + 128 * hf,
                                    [[256, 128], [128 * 256, 44], [1, 128]]))\
                        .then_inc(d_w3p[do % 2].h, 16)
                    d_w3p[do % 2].n += 16
                    w3m[do] = (do % 2, d_w3p[do % 2].n)
                for tt in range(4):
                    if tt >= 1:
                        sync.wait_ge(v_c.h, ln2m[tt - 1])
                    row = t0 + 128 * tt
                    sync.dma_start(out=s_stgf[:, :],
                                   in_=y1_scr[row:row + 128, :])\
                        .then_inc(d_res.h, 16)
                    d_res.n += 16
                    resm[tt] = d_res.n

            @blk.tensor
            def _(t, tc=tc):
                for fc in range(22):
                    t.wait_ge(d_sl[slm[fc][0]].h, slm[fc][1])
                    half = 4096 * (fc % 2)
                    for ft in range(2):
                        fi = 2 * fc + ft
                        for w, sl in ((0, s_sl1), (1, s_sl2)):
                            bank = p_s[2 * ft + w]
                            if fc >= 1:
                                if w == 0:
                                    t.wait_ge(a_c.h, silm[(fc - 1, ft)])
                                else:
                                    t.wait_ge(v_c.h, hm[(fc - 1, ft)])
                            for di in range(16):
                                mm = t.matmul(
                                    bank[:, :],
                                    sl[:, half + 256 * di + 128 * ft:
                                       half + 256 * di + 128 * ft + 128],
                                    s_y1T[:, NT * di + t0:NT * di + t0 + 512],
                                    start=(di == 0), stop=(di == 15))
                            m_c.inc(mm)
                            assert m_c.n == abm[(fc, ft, w)]
                for do in range(16):
                    t.wait_ge(d_w3p[w3m[do][0]].h, w3m[do][1])
                    t.wait_ge(v_c.h, hm[(21, 1)])
                    if do >= 2:
                        t.wait_ge(a_c.h, oute[do - 2])
                    for fi in range(44):
                        mm = t.matmul(
                            p_pj[do % 2][:, :],
                            s_w3s[:, 5632 * (do % 2) + 128 * fi:
                                  5632 * (do % 2) + 128 * fi + 128],
                            s_hT[:, 512 * fi:512 * fi + 512],
                            start=(fi == 0), stop=(fi == 43))
                    m_c.inc(mm)
                    assert m_c.n == outm[do]
                t.wait_ge(a_c.h, oute[15])
                for tt in range(4):
                    for gg in range(4):
                        ti = tg2_of[(tt, gg)]
                        if ti >= 2:
                            t.wait_ge(a_c.h, tg2_e[ti - 2])
                        bk = tbank2[ti % 2]
                        for i in range(4):
                            do = 4 * gg + i
                            mm = t.matmul(
                                bk[:, 128 * i:128 * i + 128],
                                s_foT[:, 512 * do + 128 * tt:
                                      512 * do + 128 * tt + 128],
                                s_ident[:, :], start=True, stop=True)
                        m_c.inc(mm)
                        assert m_c.n == trm2[(tt, gg)]

            @blk.scalar
            def _(a, tc=tc):
                for fc in range(22):
                    for ft in range(2):
                        a.wait_ge(m_c.h, abm[(fc, ft, 0)])
                        if fc >= 1 or ft >= 1:
                            pv = (fc, ft - 1) if ft > 0 else (fc - 1, 1)
                            a.wait_ge(v_c.h, hm[pv])
                        a_c.inc(a.activation(s_silu[:, :],
                                             p_s[2 * ft + 0][:, :], AF.Silu))
                        assert a_c.n == silm[(fc, ft)]
                for do in range(16):
                    a.wait_ge(m_c.h, outm[do])
                    a_c.inc(a.copy(s_foT[:, 512 * do:512 * do + 512],
                                   p_pj[do % 2][:, :]))
                    assert a_c.n == oute[do]
                for tt in range(4):
                    if tt >= 1:
                        a.wait_ge(v_c.h, ln2m[tt - 1])
                        a.wait_ge(d_g.h, outd[tt - 1])
                    for gg in range(4):
                        a.wait_ge(m_c.h, trm2[(tt, gg)])
                        bk = tbank2[tg2_of[(tt, gg)] % 2]
                        a_c.inc(a.copy(s_natf[:, 512 * gg:512 * gg + 512],
                                       bk[:, :]))
                        assert a_c.n == tre2[(tt, gg)]
                    a.wait_ge(v_c.h, lnv2[tt])
                    a_c.inc(a.activation(s_sd[:, :], s_v1[:, :], AF.Sqrt,
                                         bias=s_lneps[:, :],
                                         scale=1.0 / EMB))
                    assert a_c.n == lns2[tt]

            @blk.vector
            def _(v, tc=tc):
                for fc in range(22):
                    for ft in range(2):
                        fi = 2 * fc + ft
                        v.wait_ge(a_c.h, silm[(fc, ft)])
                        v.wait_ge(m_c.h, abm[(fc, ft, 1)])
                        v_c.inc(v.tensor_tensor(
                            s_hT[:, 512 * fi:512 * fi + 512],
                            s_silu[:, :], p_s[2 * ft + 1][:, :], ALU.mult))
                        assert v_c.n == hm[(fc, ft)]
                for tt in range(4):
                    for gg in range(4):
                        v.wait_ge(a_c.h, tre2[(tt, gg)])
                    v.wait_ge(d_res.h, resm[tt])
                    v.tensor_add(s_natf[:, :], s_natf[:, :], s_stgf[:, :])
                    vf(v, v.tensor_reduce(s_m1[:, :], s_natf[:, :], AX.X,
                                          ALU.add))
                    vf(v, v.tensor_scalar(s_m1[:, :], s_m1[:, :], 1.0 / EMB,
                                          None, ALU.mult))
                    v.tensor_scalar(s_natf[:, :], s_natf[:, :], s_m1[:, :],
                                    None, ALU.subtract)
                    v.tensor_tensor(s_sqf[:, :], s_natf[:, :], s_natf[:, :],
                                    ALU.mult)
                    v_c.inc(v.tensor_reduce(s_v1[:, :], s_sqf[:, :],
                                            AX.X, ALU.add))
                    assert v_c.n == lnv2[tt]
                    v.wait_ge(a_c.h, lns2[tt])
                    vf(v, v.reciprocal(s_r1[:, :], s_sd[:, :]))
                    v_c.inc(v.tensor_scalar(s_natf[:, :], s_natf[:, :],
                                            s_r1[:, :], None, ALU.mult))
                    assert v_c.n == ln2m[tt]

            @blk.gpsimd
            def _(g_, tc=tc):
                for tt in range(4):
                    g_.wait_ge(v_c.h, ln2m[tt])
                    row = t0 + 128 * tt
                    g_.dma_start(out=out[row:row + 128, :],
                                 in_=s_natf[:, :]).then_inc(d_g.h, 16)
                    d_g.n += 16
                    assert d_g.n == outd[tt]

    if DEBUG_TAPS:
        taps = [("ag1_out", ag_out[0], [16384, 704], BF16),
                ("ag3_out", ag_out[2], [45056, 256], BF16),
                ("ago_in", ago_in, [1024, N], BF16),
                ("y1_scr", y1_scr, [NT, EMB], F32),
                ("scl_flat", scl_flat, [3, 22528], F32),
                ("rep_flat", rep_flat, [3, 22528], F32),
                ("ar_out", ar_out, [1, 4], F32)]
        dbg_outs = {nm: nc.dram_tensor("dbg_" + nm, shp, dtp,
                                       kind="ExternalOutput")
                    for nm, _, shp, dtp in taps}
        with nc.Block() as blk:
            @blk.gpsimd
            def _(g_):
                g_.wait_ge(d_g.h, d_g.n)
                for nm, src_t, shp, dtp in taps:
                    g_.dma_start(out=dbg_outs[nm][:, :],
                                 in_=src_t[:, :]).then_inc(d_g.h, 16)
                    d_g.n += 16
                g_.wait_ge(d_g.h, d_g.n)
    with nc.Block() as blk:
        @blk.gpsimd
        def _(g_):
            g_.wait_ge(d_g.h, d_g.n)

    fes.close()
    wes.close()
    ges.close()
    return nc


def kernel(**inputs):
    x = np.asarray(inputs["L0_attn_x"])
    wq = np.asarray(inputs["L0_attn_wq"])
    bq = np.asarray(inputs["L0_attn_bq"])
    wk = np.asarray(inputs["L0_attn_wk"])
    bk = np.asarray(inputs["L0_attn_bk"])
    wv = np.asarray(inputs["L0_attn_wv"])
    bv = np.asarray(inputs["L0_attn_bv"])
    wo = np.asarray(inputs["L0_attn_wo"])
    bo = np.asarray(inputs["L0_attn_bo"])
    w1 = np.asarray(inputs["L0_ffn_weight1"])
    w2 = np.asarray(inputs["L0_ffn_weight2"])
    w3 = np.asarray(inputs["L0_ffn_weight3"])

    ident = np.eye(128, dtype=bf16)
    cmask = np.triu(np.full((128, 128), -1e30, dtype=np.float32), k=1)

    in_maps = []
    for c in range(8):
        b, hh = c // 2, c % 2
        hs = slice(1024 * hh, 1024 * hh + 1024)
        in_maps.append({
            "xT": np.ascontiguousarray(x[b].T),
            "x_res": np.ascontiguousarray(x[b, 2048 * hh:2048 * hh + 2048, :]),
            "wqT": np.ascontiguousarray(wq[hs, :].T),
            "wkT": np.ascontiguousarray(wk[hs, :].T),
            "wvT": np.ascontiguousarray(wv[hs, :].T),
            "woT": np.ascontiguousarray(wo.T),
            "bqkv": np.ascontiguousarray(
                np.stack([bq[hs], bk[hs], bv[hs]], axis=1)),
            "bvfull": np.ascontiguousarray(bv.reshape(16, 128)),
            "bo_in": np.ascontiguousarray(bo.reshape(1, 2048)),
            "w1s": np.ascontiguousarray(w1[704 * c:704 * c + 704, :]),
            "w2s": np.ascontiguousarray(w2[704 * c:704 * c + 704, :]),
            "w3s": np.ascontiguousarray(w3[256 * c:256 * c + 256, :]),
            "identw": ident,
            "cmaskw": cmask,
            "offsw": np.array([[2048 * hh + 512 * qt for qt in range(4)]
                               + [0, 0, 0, 0]], dtype=np.int32),
        })

    nc = build_program()
    res = run_bass_kernel_spmd(nc, in_maps, core_ids=list(range(8)))
    outp = np.empty((4, 4096, 2048), dtype=np.float32)
    for c in range(8):
        b, hh = c // 2, c % 2
        outp[b, 2048 * hh:2048 * hh + 2048, :] = \
            np.asarray(res.results[c]["out"], dtype=np.float32)
    return outp


if __name__ == "__main__":
    p = build_program()
    n = sum(1 for _ in p.all_instructions())
    print("built, instructions:", n)
